# version 02
# baseline (speedup 1.0000x reference)
import numpy as np
import ml_dtypes
import contextlib
import concourse.bass as bass
import concourse.mybir as mybir
from concourse.bass_utils import run_bass_kernel_spmd
from concourse.alu_op_type import AluOpType as OP

F32 = mybir.dt.float32
BF16 = mybir.dt.bfloat16
AF = mybir.ActivationFunctionType
AX = mybir.AxisListType.X
bfloat16 = ml_dtypes.bfloat16

B, T, DIM = 4, 4096, 1024
NCHUNK = DIM // 128
HALO, OWN = 1280, 2048
WIN = HALO + OWN
NEG = -30000.0
HEADS_O, HD = 16, 64
NSUM = 64
HEADS_I = 8
WLOC = 512
HID = 2730
H2 = 2 * HID
QK_SCALE = 8.0
R_OUT = (0, WIN)
R_KV1 = (128, WIN)
R_ATT1 = (640, WIN)
R_KV2 = (640, WIN)
R_MID2 = (1152, WIN)
R_FFN2 = (1280, WIN)
GS = 512


def groups(r):
    out, t = [], r[0]
    while t < r[1]:
        w = min(GS, r[1] - t)
        out.append((t, w))
        t += w
    return out


def qtiles(r):
    return list(range(r[0], r[1], 128))


N_QT_OUT = len(qtiles(R_OUT))
N_QT_A1 = len(qtiles(R_ATT1))
N_QT_A2 = len(qtiles(R_MID2))

W1_TILES = []
_j = 0
while _j < HID:
    W1_TILES.append(min(128, HID - _j))
    _j += 128
N_JT = len(W1_TILES)



def _pair_perm():
    perm = []
    for p in range(HEADS_I // 2):
        a, b = 2 * p, 2 * p + 1
        perm += list(range(64 * a, 64 * a + 32))
        perm += list(range(64 * b, 64 * b + 32))
        perm += list(range(64 * a + 32, 64 * a + 64))
        perm += list(range(64 * b + 32, 64 * b + 64))
    return np.array(perm)


def _qs_col(qs):
    d = np.concatenate([qs[0:32], qs[0:32], qs[32:64], qs[32:64]])
    return np.ascontiguousarray(d.reshape(128, 1).astype(np.float32))


def _rotary_tables(t_abs):
    half = np.arange(0, HD, 2, dtype=np.float64)
    inv_freq = 1.0 / 10000.0 ** (half / HD)
    tt = t_abs.astype(np.float64)
    fr = tt[:, None] * inv_freq[None, :]
    base = (half + 0.4 * HD) / (1.4 * HD)
    power = (tt - T // 2) / float(WLOC // 2)
    sc = base[None, :] ** power[:, None]
    cos, sin = np.cos(fr), np.sin(fr)

    def dup(a):
        return np.concatenate([a, a], axis=1)
    cos64, sin64, sc64 = dup(cos), dup(sin), dup(sc)
    sign = np.where(np.arange(HD) < 32, -1.0, 1.0)[None, :]
    tabs = [cos64 * sc64, sin64 * sc64 * sign, cos64 / sc64, sin64 / sc64 * sign]
    rows = (np.arange(128) % 32) + 32 * (np.arange(128) // 64)
    out = [np.ascontiguousarray(a.T[rows].astype(bfloat16)) for a in tabs]
    for i in (1, 3):
        out[i] = np.ascontiguousarray(
            np.concatenate([out[i][64:], out[i][:64]], axis=0))
    return out


def _inner_masks(first_abs):
    r = np.arange(128)[:, None]
    c = np.arange(640)[None, :]
    out = []
    for t0 in first_abs:
        bad = (c < r) | (c > r + 512) | ((t0 - 512 + c) < 0)
        out.append(np.where(bad, NEG, 0.0))
    return np.stack(out).astype(bfloat16)


def _outer_masks(first_abs):
    r = np.arange(128)[:, None]
    c = np.arange(NSUM)[None, :]
    out = []
    for t0 in first_abs:
        bad = (64 * c) > (t0 + r)
        out.append(np.where(bad, NEG, 0.0))
    return np.stack(out).astype(bfloat16)


def _prep_core(inp, b, s):
    S = s * OWN
    x = np.asarray(inp["L0_attn_x"][b])
    xT = np.ascontiguousarray(x.T)
    lo = S - HALO
    xw = np.zeros((DIM, WIN), np.float32)
    src_lo = max(lo, 0)
    xw[:, src_lo - lo:] = xT[:, src_lo:S + OWN]
    d = {
        "xf": xT,
        "xw": xw,
        "Wq": np.ascontiguousarray(np.asarray(inp["L0_attn_Wq"])),
        "Wk": np.ascontiguousarray(np.asarray(inp["L0_attn_Wk"])),
        "Wv": np.ascontiguousarray(np.asarray(inp["L0_attn_Wv"])),
        "Wo": np.ascontiguousarray(np.asarray(inp["L0_attn_Wo"])),
        "bq_c": np.ascontiguousarray(np.asarray(inp["L0_attn_bq"]).reshape(NCHUNK, 128).T),
        "bk_c": np.ascontiguousarray(np.asarray(inp["L0_attn_bk"]).reshape(NCHUNK, 128).T),
        "bo_c": np.ascontiguousarray(np.asarray(inp["L0_attn_bo"]).reshape(NCHUNK, 128).T),
        "bv_r": np.ascontiguousarray(np.asarray(inp["L0_attn_bv"]).reshape(1, DIM)),
    }
    perm = _pair_perm()
    for l in range(2):
        wqkv = np.asarray(inp["L0_ffn_wqkv"][l])
        d[f"wqkv{l}"] = np.ascontiguousarray(np.concatenate(
            [wqkv[:, 0:512][:, perm], wqkv[:, 512:1024][:, perm],
             wqkv[:, 1024:1536]], axis=1))
        d[f"qs{l}"] = _qs_col(np.asarray(inp["L0_ffn_q_scale"][l]))
        d[f"ks{l}"] = _qs_col(np.asarray(inp["L0_ffn_k_scale"][l]))
        d[f"wo{l}"] = np.ascontiguousarray(np.asarray(inp["L0_ffn_wo"][l]))
        d[f"lg_c{l}"] = np.ascontiguousarray(
            np.asarray(inp["L0_ffn_ln_g"][l]).reshape(NCHUNK, 128).T)
        d[f"lg_r{l}"] = np.ascontiguousarray(
            np.asarray(inp["L0_ffn_ln_g"][l]).reshape(1, DIM))
        d[f"lb_r{l}"] = np.ascontiguousarray(
            np.asarray(inp["L0_ffn_ln_b"][l]).reshape(1, DIM))
        w1 = np.asarray(inp["L0_ffn_w1"][l])
        b1 = np.asarray(inp["L0_ffn_b1"][l])
        pieces, bcols = [], []
        j = 0
        while j < HID:
            wdt = min(128, HID - j)
            pieces += [w1[:, j:j + wdt], w1[:, HID + j:HID + j + wdt]]
            bc = np.zeros((128, 2), np.float32)
            bc[:wdt, 0] = b1[j:j + wdt]
            bc[:wdt, 1] = b1[HID + j:HID + j + wdt]
            bcols.append(bc)
            j += wdt
        d[f"w1i{l}"] = np.ascontiguousarray(np.concatenate(pieces, axis=1))
        d[f"b1i{l}"] = np.ascontiguousarray(np.concatenate(bcols, axis=1))
        d[f"w2_{l}"] = np.ascontiguousarray(np.asarray(inp["L0_ffn_w2"][l]))
        d[f"b2_c{l}"] = np.ascontiguousarray(
            np.asarray(inp["L0_ffn_b2"][l]).reshape(NCHUNK, 128).T)
    d["ident"] = np.eye(128, dtype=bfloat16)
    ev = np.zeros((128,), np.float32)
    ev[0:32] = 1.0
    ev[64:96] = 1.0
    d["hsel2"] = np.ascontiguousarray(
        np.stack([ev, 1.0 - ev], axis=1).astype(bfloat16))
    d["psel2"] = np.ascontiguousarray(
        np.stack([ev, 1.0 - ev], axis=0).astype(bfloat16))
    d["selperm"] = np.ascontiguousarray(np.stack([ev, 1.0 - ev], axis=1))
    d["bcol"] = np.full((128, 1), 0.0 if s == 0 else 1.0, np.float32)
    d["mask_o"] = _outer_masks([lo + 128 * i for i in range(N_QT_OUT)])
    d["mask_i1"] = _inner_masks([lo + R_ATT1[0] + 128 * i for i in range(N_QT_A1)])
    d["mask_i2"] = _inner_masks([lo + R_MID2[0] + 128 * i for i in range(N_QT_A2)])
    tabs = _rotary_tables(np.arange(lo, S + OWN))
    for nm, a in zip(("csq", "sq", "csk", "sk"), tabs):
        d[f"rot_{nm}"] = a
    return d



class Buf:

    def __init__(self, ap):
        self.ap = ap
        self.w = None
        self.rs = []

    def __getitem__(self, idx):
        return self.ap[idx]


class Bld:

    def __init__(self, nc, es, block):
        self.nc = nc
        self.es = es
        self.block = block
        self.sems = {}
        self.counts = {}

    def _sem(self, name):
        if name not in self.sems:
            self.sems[name] = self.es.enter_context(self.nc.semaphore(name))
            self.counts[name] = 0
        return self.sems[name]

    def inc(self, name, instr, k=1):
        s = self._sem(name)
        instr.then_inc(s, k)
        self.counts[name] += k
        return (name, self.counts[name])

    def wait(self, eng, tok):
        if tok is None:
            return
        name, v = tok
        if v > 0:
            eng.wait_ge(self._sem(name), v)

    def wait_r(self, eng, buf):
        for t in buf.rs:
            self.wait(eng, t)

    def wrote(self, buf, tok):
        buf.w = tok
        buf.rs = []

    def read(self, eng, buf):
        self.wait(eng, buf.w)

    def did_read(self, buf, tok):
        buf.rs.append(tok)

    def SY(self, f):
        self.block.sync(f)

    def TE(self, f):
        self.block.tensor(f)

    def VE(self, f):
        self.block.vector(f)

    def SC(self, f):
        self.block.scalar(f)

    def GP(self, f):
        self.block.gpsimd(f)



def build_program(dev=None):
    nc = bass.Bass("TRN2", target_bir_lowering=False)
    es = contextlib.ExitStack()

    def din(name, shape, dt=F32):
        return nc.dram_tensor(name, list(shape), dt, kind="ExternalInput").ap()

    xf_d = din("xf", [DIM, T])
    xw_d = din("xw", [DIM, WIN])
    Wq_d = din("Wq", [DIM, DIM]); Wk_d = din("Wk", [DIM, DIM])
    Wv_d = din("Wv", [DIM, DIM]); Wo_d = din("Wo", [DIM, DIM])
    bq_d = din("bq_c", [128, NCHUNK]); bk_d = din("bk_c", [128, NCHUNK])
    bo_d = din("bo_c", [128, NCHUNK]); bv_d = din("bv_r", [1, DIM])
    ident_d = din("ident", [128, 128], BF16)
    mask_o_d = din("mask_o", [N_QT_OUT, 128, NSUM], BF16)
    mask_i_d = [din("mask_i1", [N_QT_A1, 128, 640], BF16),
                din("mask_i2", [N_QT_A2, 128, 640], BF16)]
    rot_d = {nm: din(f"rot_{nm}", [128, WIN], BF16)
             for nm in ("csq", "sq", "csk", "sk")}
    wqkv_d = [din(f"wqkv{l}", [DIM, 1536]) for l in range(2)]
    qs_d = [din(f"qs{l}", [128, 1]) for l in range(2)]
    ks_d = [din(f"ks{l}", [128, 1]) for l in range(2)]
    wo2_d = [din(f"wo{l}", [512, DIM]) for l in range(2)]
    lgc_d = [din(f"lg_c{l}", [128, NCHUNK]) for l in range(2)]
    lgr_d = [din(f"lg_r{l}", [1, DIM]) for l in range(2)]
    lbr_d = [din(f"lb_r{l}", [1, DIM]) for l in range(2)]
    w1i_d = [din(f"w1i{l}", [DIM, H2]) for l in range(2)]
    b1i_d = [din(f"b1i{l}", [128, 44]) for l in range(2)]
    w2_d = [din(f"w2_{l}", [HID, DIM]) for l in range(2)]
    b2_d = [din(f"b2_c{l}", [128, NCHUNK]) for l in range(2)]

    hsel2_d = din("hsel2", [128, 2], BF16)
    psel2_d = din("psel2", [2, 128], BF16)
    selperm_d = din("selperm", [128, 2])
    bcol_d = din("bcol", [128, 1])

    out_d = nc.dram_tensor("out", [DIM, OWN], F32, kind="ExternalOutput").ap()

    def dsp(name, tokens, dt=F32):
        return nc.dram_tensor(name, [DIM, tokens], dt, kind="Internal").ap()

    x2_sp = dsp("x2_sp", WIN)
    xffn_sp = dsp("xffn_sp", WIN)
    mid_sp = dsp("mid_sp", WIN - R_ATT1[0])
    l1o_sp = dsp("l1o_sp", WIN - R_ATT1[0])
    l1ob_sp = dsp("l1ob_sp", WIN - R_ATT1[0], BF16)
    xnmid_sp = dsp("xnmid_sp", WIN - R_ATT1[0], BF16)
    dbg_d = None
    if dev in ("p2q", "p2s", "p2t", "p2v", "p2sa", "p2sb"):
        dbg_d = nc.dram_tensor("dbg", [DIM, 4096], F32, kind="ExternalOutput").ap()
    elif dev is not None and dev.startswith("x2g"):
        dbg_d = nc.dram_tensor("dbg", [DIM, WIN], F32, kind="ExternalOutput").ap()
    elif dev == "x2":
        dbg_d = nc.dram_tensor("dbg", [DIM, WIN], F32, kind="ExternalOutput").ap()
    elif dev == "sum":
        dbg_d = nc.dram_tensor("dbg", [DIM, NSUM], F32, kind="ExternalOutput").ap()
    elif dev == "kv":
        dbg_d = nc.dram_tensor("dbg", [3 * 128, DIM], F32, kind="ExternalOutput").ap()
    elif dev == "xffn":
        dbg_d = nc.dram_tensor("dbg", [DIM, WIN], F32, kind="ExternalOutput").ap()
    elif dev == "l1":
        dbg_d = nc.dram_tensor("dbg", [DIM, WIN - R_ATT1[0]], F32,
                               kind="ExternalOutput").ap()

    def sbt(name, shape, dt):
        return es.enter_context(nc.sbuf_tensor(name, list(shape), dt))[:]

    def sbuf(name, shape, dt):
        return Buf(sbt(name, shape, dt))

    with es:
        ident = sbt("identb", [128, 128], BF16)
        bqc = sbt("bqc", [128, NCHUNK], F32)
        bkc = sbt("bkc", [128, NCHUNK], F32)
        boc = sbt("boc", [128, NCHUNK], F32)
        b2c = [sbt(f"b2c{l}", [128, NCHUNK], F32) for l in range(2)]
        lgcol = [sbt(f"lgcol{l}", [128, NCHUNK], F32) for l in range(2)]
        b1col = [sbt(f"b1col{l}", [128, 44], F32) for l in range(2)]
        qscol = [sbt(f"qscol{l}", [128, 1], F32) for l in range(2)]
        kscol = [sbt(f"kscol{l}", [128, 1], F32) for l in range(2)]
        rows = sbt("rows", [128, 3072], F32)
        RO_BV, RO_MU, RO_M2, RO_LG = 0, 1024, 1536, 2048
        rows_bf = sbt("rows_bf", [128, 1024], BF16)
        crow = sbt("crow", [128, 5696], BF16)
        CR_BV, CR_ONE, CR_M1 = 0, 5120, 5184
        CR_LG = [1024, 3072]
        CR_LB = [2048, 4096]
        e0ones = sbt("e0ones", [128, 128], BF16)
        onescol = sbt("onescol", [128, 1], BF16)
        epscol = sbt("epscol", [128, 1], F32)

        selcol = sbt("selcol", [128, 2], F32)
        wA = [sbuf(f"wA{c}", [128, DIM], BF16) for c in range(NCHUNK)]
        wB = [sbuf(f"wB{c}", [128, DIM], BF16) for c in range(NCHUNK)]

        xfr = [sbuf(f"xfr{c}", [128, GS], F32) for c in range(NCHUNK)]
        xbr = [sbuf("xbr0", [128, GS], BF16)]
        xsq = [sbuf(f"xsq{i}", [128, GS], BF16) for i in range(2)]
        t1r = [sbuf("t1r0", [128, GS], F32)]

        so_s = [sbuf(f"so{i}", [128, 640], F32) for i in range(2)]
        stat = sbuf("stat_s", [128, 64], F32)
        pcm = sbuf("pcm", [128, 1024], BF16)
        maskt = [sbuf("maskt0", [128, 640], BF16)]

        khat = [sbuf(f"khat{p}", [128, 3200], BF16) for p in range(4)]
        qhat = [sbuf(f"qhat{p}", [128, 2688], BF16) for p in range(4)]
        vI = [sbuf(f"vI{t}", [128, 512], BF16) for t in range(25)]
        kband = [sbuf(f"kband{i}", [128, 640], BF16) for i in range(2)]
        pn_s = kband
        xgb = [sbuf(f"xgb{c}", [128, 2 * 513], BF16) for c in range(NCHUNK)]
        fsp = [sbuf(f"fsp{c}", [128, 513], F32) for c in range(NCHUNK)]
        qn_s = [sbuf(f"qn{i}", [128, 512], BF16) for i in range(2)]
        t2_s = [sbuf(f"t2{i}", [128, 512], BF16) for i in range(2)]
        t3_s = [sbuf(f"t3{i}", [128, 512], BF16) for i in range(2)]
        hsel2 = sbt("hsel2b", [128, 2], BF16)
        psel2 = sbt("psel2b", [2, 128], BF16)
        selperm = sbt("selpermb", [128, 2], F32)
        bcolb = sbt("bcolb", [128, 1], F32)
        zcol = sbuf("zcol", [128, 8], F32)
        sum_fm = [vI[2 + c // 4].ap[:, 128 * (c % 4):128 * (c % 4) + NSUM]
                  for c in range(NCHUNK)]
        ksumh = [vI[4 + h // 8].ap[:, 64 * (h % 8):64 * (h % 8) + NSUM]
                 for h in range(16)]
        oI = [Buf(qn_s[0].ap), Buf(qn_s[1].ap), Buf(t2_s[0].ap), Buf(t2_s[1].ap)]
        rsb = sbt("rsb", [2, 512], BF16)
        eps18 = sbt("eps18", [128, 1], F32)
        qo = [Buf(khat[p % 4].ap[:, 512 * (p // 4):512 * (p // 4) + 512])
              for p in range(8)]
        ofm = [Buf(khat[p % 4].ap[:, 1024 + 512 * (p // 4):
                                  1024 + 512 * (p // 4) + 512])
               for p in range(8)]
        vlo_b = Buf(khat[0].ap[:, 2048:3072])
        vhi_b = Buf(khat[1].ap[:, 2048:3072])
        vlo, vhi = vlo_b.ap, vhi_b.ap
        se_s = [Buf(qhat[i].ap.bitcast(F32)[:, 0:512]) for i in range(2)]
        x2r = [Buf(fsp[c].ap[:, 1:513]) for c in range(NCHUNK)]
        pbank = [Buf(es.enter_context(
            nc.psum_tensor(f"pb{i}", [128, 512], F32))[:]) for i in range(6)]
        pbf = [Buf(es.enter_context(
            nc.psum_tensor(f"pQ{i}", [128, 1024], BF16))[:]) for i in range(2)]
        pbfF = [Buf(pbf[i].ap.bitcast(F32)) for i in range(2)]

        block = es.enter_context(nc.Block())
        bd = Bld(nc, es, block)

        def p0_sync(sy):
            for dst, src in [(ident, ident_d), (bqc, bq_d), (bkc, bk_d),
                             (boc, bo_d)]:
                bd.inc("cld", sy.dma_start(out=dst, in_=src), 16)
            for l in range(2):
                for dst, src in [(b2c[l], b2_d[l]), (lgcol[l], lgc_d[l]),
                                 (b1col[l], b1i_d[l]), (qscol[l], qs_d[l]),
                                 (kscol[l], ks_d[l])]:
                    bd.inc("cld", sy.dma_start(out=dst, in_=src), 16)
            bd.inc("cld", sy.dma_start(out=rows[0:1, RO_BV:RO_BV + DIM],
                                       in_=bv_d), 16)
            bd.inc("cld", sy.dma_start(out=hsel2, in_=hsel2_d), 16)
            bd.inc("cld", sy.dma_start(out=psel2, in_=psel2_d), 16)
            bd.inc("cld", sy.dma_start(out=selperm, in_=selperm_d), 16)
            bd.inc("cld", sy.dma_start(out=bcolb, in_=bcol_d), 16)
        bd.SY(p0_sync)
        tok_cld = ("cld", bd.counts["cld"])

        def p0_init(v):
            bd.wait(v, tok_cld)
            v.memset(e0ones, 0.0)
            v.drain()
            v.memset(e0ones[0:1, 0:128], 1.0)
            v.memset(onescol, 1.0)
            v.memset(crow[0:1, CR_ONE:CR_ONE + 64], 1.0)
            v.memset(crow[0:1, CR_M1:CR_M1 + 512], -1.0)
            v.memset(epscol, 1e-5)
            v.memset(eps18, 1e-18)
            v.memset(vlo, 0.0)
            v.memset(vhi, 0.0)
            v.memset(rows_bf, 0.0)
            v.memset(selcol[0:64, 0:1], 1.0)
            v.memset(selcol[64:128, 0:1], 0.0)
            v.memset(selcol[0:64, 1:2], 0.0)
            v.memset(selcol[64:128, 1:2], 1.0)
            bd.inc("cinit", v.tensor_copy(out=crow[0:1, CR_BV:CR_BV + DIM],
                                          in_=rows[0:1, RO_BV:RO_BV + DIM]))
        bd.VE(p0_init)
        tok_cinit = ("cinit", bd.counts["cinit"])

        for l in range(2):
            for which, src in (("lg", lgr_d[l]), ("lb", lbr_d[l])):
                def r1(sy, src=src):
                    bd.wait(sy, ("rowcast", bd.counts.get("rowcast", 0)))
                    if bd.counts.get("rowld", 0) == 0:
                        bd.wait(sy, tok_cinit)
                    bd.inc("rowld", sy.dma_start(
                        out=rows[0:1, RO_LG:RO_LG + DIM], in_=src), 16)
                bd.SY(r1)

                def r2(v, l=l, which=which):
                    bd.wait(v, ("rowld", bd.counts["rowld"]))
                    off = CR_LG[l] if which == "lg" else CR_LB[l]
                    bd.inc("rowcast", v.tensor_copy(
                        out=crow[0:1, off:off + DIM],
                        in_=rows[0:1, RO_LG:RO_LG + DIM]))
                bd.VE(r2)

        def emit_ln(w, xf32_bufs, xn_slices, affine_l=None, extra_bf=None,
                    out_f32=None):
            bsx, bsq, brb, bmb = pbank[0], pbank[1], pbank[2], pbank[3]
            for c in range(NCHUNK):
                def cs_f(g, c=c):
                    bd.read(g, xf32_bufs[c][0])
                    bd.wait_r(g, xbr[0])
                    tk = bd.inc("ln_cast", g.tensor_copy(
                        out=xbr[0][:, 0:w], in_=xf32_bufs[c][1]))
                    bd.wrote(xbr[0], tk)
                    bd.did_read(xf32_bufs[c][0], tk)
                bd.GP(cs_f)

                def sq_f(sc, c=c):
                    bd.wait(sc, xf32_bufs[c][0].w)
                    bd.wait_r(sc, xsq[c % 2])
                    tk = bd.inc("ln_sq", sc.activation(
                        out=xsq[c % 2][:, 0:w], in_=xf32_bufs[c][1],
                        func=AF.Square))
                    bd.wrote(xsq[c % 2], tk)
                    bd.did_read(xf32_bufs[c][0], tk)
                bd.SC(sq_f)

                def sums_f(t, c=c):
                    if c == 0:
                        bd.wait_r(t, bsx)
                        bd.wait_r(t, bsq)
                    bd.read(t, xbr[0])
                    i1 = t.matmul(bsx[0:1, 0:w], onescol, xbr[0][:, 0:w],
                                  start=(c == 0), stop=(c == NCHUNK - 1))
                    tk = bd.inc("ln_sx", i1)
                    bd.did_read(xbr[0], tk)
                    bd.read(t, xsq[c % 2])
                    i2 = t.matmul(bsq[0:1, 0:w], onescol, xsq[c % 2][:, 0:w],
                                  start=(c == 0), stop=(c == NCHUNK - 1))
                    tk = bd.inc("ln_sq2", i2)
                    bd.did_read(xsq[c % 2], tk)
                    if c == NCHUNK - 1:
                        bd.wrote(bsx, ("ln_sx", bd.counts["ln_sx"]))
                        bd.wrote(bsq, ("ln_sq2", bd.counts["ln_sq2"]))
                bd.TE(sums_f)

            def stats_f(v):
                bd.read(v, bsx)
                bd.wait(v, ("ln_apply", bd.counts.get("ln_apply", 0)))
                t1 = bd.inc("vch", v.tensor_scalar_mul(
                    rows[0:1, RO_MU:RO_MU + w], bsx[0:1, 0:w], 1.0 / DIM))
                bd.wait(v, t1)
                t2 = bd.inc("vch", v.tensor_tensor(
                    out=rows[0:1, RO_M2:RO_M2 + w],
                    in0=rows[0:1, RO_MU:RO_MU + w],
                    in1=rows[0:1, RO_MU:RO_MU + w], op=OP.mult))
                bd.read(v, bsq)
                bd.wait(v, t2)
                tk = bd.inc("ln_ve", v.scalar_tensor_tensor(
                    out=rows[0:1, RO_M2:RO_M2 + w], in0=bsq[0:1, 0:w],
                    scalar=1.0 / DIM, in1=rows[0:1, RO_M2:RO_M2 + w],
                    op0=OP.mult, op1=OP.subtract))
                bd.did_read(bsx, tk)
                bd.did_read(bsq, tk)
            bd.VE(stats_f)

            def rstd_f(sc):
                bd.wait(sc, ("ln_ve", bd.counts["ln_ve"]))
                bd.wait(sc, ("ln_rbc", bd.counts.get("ln_rbc", 0)))
                bd.wait(sc, tok_cinit)
                tl = bd.inc("sch", sc.activation(
                    out=rows[0:1, RO_M2:RO_M2 + w],
                    in_=rows[0:1, RO_M2:RO_M2 + w], func=AF.Ln,
                    bias=epscol[0:1, :]))
                bd.wait(sc, tl)
                bd.inc("ln_rstd", sc.activation(
                    out=rows_bf[0:1, 0:w], in_=rows[0:1, RO_M2:RO_M2 + w],
                    func=AF.Exp, scale=-0.5))
            bd.SC(rstd_f)

            def mr_f(v):
                bd.wait(v, ("ln_rstd", bd.counts["ln_rstd"]))
                bd.wait(v, ("ln_mbc", bd.counts.get("ln_mbc", 0)))
                bd.wait(v, ("ln_dp", bd.counts.get("ln_dp", 0)))
                bd.inc("ln_mr", v.tensor_tensor(
                    out=rows_bf[0:1, 512:512 + w],
                    in0=rows[0:1, RO_MU:RO_MU + w],
                    in1=rows_bf[0:1, 0:w], op=OP.mult))
            bd.VE(mr_f)

            def bc_f(t):
                bd.wait(t, tok_cinit)
                bd.wait_r(t, brb)
                bd.wait(t, ("ln_rstd", bd.counts["ln_rstd"]))
                tk = bd.inc("ln_rbc", t.matmul(
                    brb[:, 0:w], e0ones, rows_bf[:, 0:w], start=True, stop=True))
                bd.wrote(brb, tk)
                if affine_l is None:
                    bd.wait_r(t, bmb)
                    bd.wait(t, ("ln_mr", bd.counts["ln_mr"]))
                    tk = bd.inc("ln_mbc", t.matmul(
                        bmb[:, 0:w], e0ones, rows_bf[:, 512:512 + w],
                        start=True, stop=True))
                    bd.wrote(bmb, tk)
            bd.TE(bc_f)

            if affine_l is None:
                def apply_plain(v):
                    bd.read(v, pbank[2])
                    bd.read(v, pbank[3])
                    for c in range(NCHUNK):
                        bd.wait_r(v, t1r[0])
                        bd.wait_r(v, xn_slices[c][0])
                        v.drain()
                        v.scalar_tensor_tensor(
                            out=t1r[0][:, 0:w], in0=xf32_bufs[c][1],
                            scalar=1.0,
                            in1=pbank[2][:, 0:w], op0=OP.mult, op1=OP.mult)
                        v.drain()
                        last = v.tensor_tensor(
                            out=xn_slices[c][1], in0=t1r[0][:, 0:w],
                            in1=pbank[3][:, 0:w], op=OP.subtract)
                        if out_f32 is not None:
                            bd.wait_r(v, out_f32[c][0])
                            v.drain()
                            last = v.tensor_tensor(
                                out=out_f32[c][1], in0=t1r[0][:, 0:w],
                                in1=pbank[3][:, 0:w], op=OP.subtract)
                        if c == NCHUNK - 1:
                            tk = bd.inc("ln_apply", last)
                        else:
                            tk = bd.inc("ln_app_c", last)
                        bd.wrote(t1r[0], tk)
                        bd.wrote(xn_slices[c][0], tk)
                        if out_f32 is not None:
                            bd.wrote(out_f32[c][0], tk)
                        bd.did_read(xf32_bufs[c][0], tk)
                    bd.did_read(pbank[2], tk)
                    bd.did_read(pbank[3], tk)
                bd.VE(apply_plain)
            else:
                for c in range(NCHUNK):
                    def dp_f(t, c=c):
                        bd.wait_r(t, bmb)
                        if c == 0:
                            bd.wait(t, ("ln_mr", bd.counts["ln_mr"]))
                        lgs = crow[0:1, CR_LG[affine_l] + 128 * c:
                                    CR_LG[affine_l] + 128 * (c + 1)]
                        lbs = crow[0:1, CR_LB[affine_l] + 128 * c:
                                    CR_LB[affine_l] + 128 * (c + 1)]
                        t.matmul(bmb[:, 0:w], lgs, rows_bf[0:1, 512:512 + w],
                                 start=True, stop=False)
                        tk = bd.inc("ln_dp", t.matmul(
                            bmb[:, 0:w], lbs, crow[0:1, CR_M1:CR_M1 + w],
                            start=False, stop=True))
                        bd.wrote(bmb, tk)
                    bd.TE(dp_f)

                    def ap_f(v, c=c):
                        if c == 0:
                            bd.read(v, pbank[2])
                        bd.read(v, pbank[3])
                        bd.wait_r(v, t1r[0])
                        bd.wait_r(v, xn_slices[c][0])
                        v.drain()
                        v.scalar_tensor_tensor(
                            out=t1r[0][:, 0:w], in0=xf32_bufs[c][1],
                            scalar=lgcol[affine_l][:, c:c + 1],
                            in1=pbank[2][:, 0:w], op0=OP.mult, op1=OP.mult)
                        v.drain()
                        last = v.tensor_tensor(
                            out=xn_slices[c][1], in0=t1r[0][:, 0:w],
                            in1=pbank[3][:, 0:w], op=OP.subtract)
                        if c == NCHUNK - 1:
                            tk = bd.inc("ln_apply", last)
                        else:
                            tk = bd.inc("ln_dused", last)
                        bd.wrote(t1r[0], tk)
                        bd.wrote(xn_slices[c][0], tk)
                        bd.did_read(xf32_bufs[c][0], tk)
                        bd.did_read(bmb, tk)
                        if c == NCHUNK - 1:
                            bd.did_read(pbank[2], tk)
                    bd.VE(ap_f)
            return ("ln_apply", bd.counts["ln_apply"])

        for g in range(8):
            slot = g % 2

            def p1_ld(sy, g=g, slot=slot):
                for c in range(NCHUNK):
                    bd.wait_r(sy, xfr[c])
                for c in range(NCHUNK):
                    tk = bd.inc(f"xld{slot}", sy.dma_start(
                        out=xfr[c][:, 0:GS],
                        in_=xf_d[128 * c:128 * (c + 1), g * GS:(g + 1) * GS]), 16)
                for c in range(NCHUNK):
                    bd.wrote(xfr[c], tk)
            bd.SY(p1_ld)
            xf32 = [(xfr[c], xfr[c][:, 0:GS])
                    for c in range(NCHUNK)]
            xnout = [(xgb[c], xgb[c][:, slot * 513 + 1:slot * 513 + 1 + GS])
                     for c in range(NCHUNK)]
            ln_tok = emit_ln(GS, xf32, xnout)

            def p1_sum(v, g=g, slot=slot, ln_tok=ln_tok):
                bd.wait(v, ln_tok)
                bd.wait_r(v, stat)
                pslot = (g - 1) % 2
                for c in range(NCHUNK):
                    if c > 0:
                        v.drain()
                    src = xgb[c][:, slot * 513 + 2: slot * 513 + 2 + 448]
                    src3 = src.rearrange("p (j w) -> p j w", w=64)
                    v.tensor_reduce(out=stat[:, 1:8], in_=src3,
                                    axis=AX, op=OP.add)
                    if g >= 1:
                        v.tensor_reduce(
                            out=stat[:, 0:1],
                            in_=xgb[c][:, pslot * 513 + 450: pslot * 513 + 513],
                            axis=AX, op=OP.add)
                        v.drain()
                        v.tensor_tensor(out=stat[:, 0:1], in0=stat[:, 0:1],
                                        in1=xgb[c][:, slot * 513 + 1:slot * 513 + 2],
                                        op=OP.add)
                        v.drain()
                        last = v.tensor_scalar_mul(
                            sum_fm[c][:, 8 * g: 8 * g + 8],
                            stat[:, 0:8], 1.0 / 64)
                    else:
                        v.drain()
                        last = v.tensor_scalar_mul(
                            sum_fm[c][:, 1:8], stat[:, 1:8], 1.0 / 64)
                        last = v.tensor_copy(out=sum_fm[c][:, 0:1],
                                             in_=xgb[c][:, 1:2])
                tk = bd.inc("p1_sum", last)
                bd.wrote(stat, tk)
                for c in range(NCHUNK):
                    bd.did_read(xgb[c], tk)
            bd.VE(p1_sum)
        tok_sum = ("p1_sum", bd.counts["p1_sum"])

        def stream_w(dram, dest, tag, ncols=DIM, nchunk=NCHUNK):
            def ld(g):
                for c in range(nchunk):
                    bd.wait_r(g, dest[c])
                for c in range(nchunk):
                    tk = bd.inc(f"wc_{tag}", g.dma_start(
                        out=dest[c][:, 0:ncols],
                        in_=dram[128 * c:128 * (c + 1), 0:ncols]), 16)
                for c in range(nchunk):
                    bd.wrote(dest[c], tk)
            bd.GP(ld)
            return [( f"wc_{tag}", bd.counts[f"wc_{tag}"])]

        if dev == "sum":
            def dump_sum(v):
                bd.wait(v, tok_sum)
                for c in range(NCHUNK):
                    tk = bd.inc("dmp", v.tensor_copy(out=t1r[c][:, 0:NSUM],
                                                     in_=sum_fm[c]))
                    bd.wrote(t1r[c], tk)
            bd.VE(dump_sum)

            def dump_dma(g):
                for c in range(NCHUNK):
                    bd.read(g, t1r[c])
                for c in range(NCHUNK):
                    tk = bd.inc("dmpd", g.dma_start(
                        out=dbg_d[128 * c:128 * (c + 1), :],
                        in_=t1r[c][:, 0:NSUM]), 16)
                bd.wait(g, tk)
            bd.GP(dump_dma)
            return nc

        tok_wk = stream_w(Wk_d, wA, "wk")
        tok_wv = stream_w(Wv_d, wB, "wv")

        def p1b_ks(t):
            bd.wait(t, tok_sum)
            bd.wait(t, tok_wk[-1])
            for p in range(8):
                bank = pbank[4 + p // 4]
                if p % 4 == 0:
                    bd.wait_r(t, bank)
                off = 64 * (p % 4)
                for c in range(NCHUNK):
                    i = t.matmul(bank[:, off:off + NSUM],
                                 wA[c][:, 128 * p:128 * (p + 1)], sum_fm[c],
                                 start=(c == 0), stop=(c == NCHUNK - 1))
                tk = bd.inc("ks_mm", i)
                if p % 4 == 3:
                    bd.wrote(bank, tk)
            for c in range(NCHUNK):
                bd.did_read(wA[c], ("ks_mm", bd.counts["ks_mm"]))
        bd.TE(p1b_ks)

        def p1b_ksev(v):
            bd.wait(v, ("ks_mm", bd.counts["ks_mm"]))
            bd.wait(v, tok_cinit)
            for h in range(16):
                p = h // 2
                bank = pbank[4 + p // 4]
                off = 64 * (p % 4)
                last = v.tensor_scalar(
                    out=ksumh[h], in0=bank[:, off:off + NSUM],
                    scalar1=bkc[:, p:p + 1], scalar2=selcol[:, h % 2:h % 2 + 1],
                    op0=OP.add, op1=OP.mult)
            tk = bd.inc("ks_ev", last)
            bd.did_read(pbank[4], tk)
            bd.did_read(pbank[5], tk)
        bd.VE(p1b_ksev)
        tok_ks = ("ks_ev", bd.counts["ks_ev"])

        def p1b_vs(t):
            bd.wait(t, tok_sum)
            bd.wait(t, tok_wv[-1])
            bd.wait(t, tok_cinit)
            for h in range(2):
                bank = pbank[4 + h]
                bd.wait_r(t, bank)
                for c in range(NCHUNK):
                    t.matmul(bank[0:64, 0:512], sum_fm[c],
                             wB[c][:, 512 * h:512 * (h + 1)],
                             start=(c == 0), stop=False)
                tk = bd.inc("vs_mm", t.matmul(
                    bank[0:64, 0:512], crow[0:1, CR_ONE:CR_ONE + 64],
                    crow[0:1, CR_BV + 512 * h:CR_BV + 512 * (h + 1)],
                    start=False, stop=True))
                bd.wrote(bank, tk)
            for c in range(NCHUNK):
                bd.did_read(wB[c], ("vs_mm", bd.counts["vs_mm"]))
        bd.TE(p1b_vs)

        def p1b_vsev(v):
            bd.wait(v, ("vs_mm", bd.counts["vs_mm"]))
            bd.wait(v, tok_cinit)
            for h in range(2):
                bank = pbank[4 + h]
                v.tensor_copy(out=vlo[0:64, 512 * h:512 * (h + 1)],
                              in_=bank[0:64, 0:512])
                last = v.tensor_copy(out=vhi[64:128, 512 * h:512 * (h + 1)],
                                     in_=bank[0:64, 0:512])
                bd.did_read(bank, bd.inc("vs_ev", last))
        bd.VE(p1b_vsev)
        tok_vs = ("vs_ev", bd.counts["vs_ev"])

        if dev == "kv":
            def dump_kv(v):
                bd.wait(v, tok_ks)
                bd.wait(v, tok_vs)
                for p in range(8):
                    tk = bd.inc("dmp", v.tensor_copy(
                        out=t1r[0][:, 0:NSUM] if False else xfr[0][:, 0:NSUM],
                        in_=ksum[p]))
            bd.VE(dump_kv)

            def dump_kv2(v):
                for p in range(8):
                    tk = bd.inc("dmp2", v.tensor_copy(
                        out=xfr[p][:, 0:NSUM], in_=ksum[p]))
                    bd.wrote(xfr[p], tk)
                tk = bd.inc("dmp2", v.tensor_copy(out=x2r[0][:, 0:DIM // 2],
                                                  in_=vlo[:, 0:DIM // 2]))
                bd.wrote(x2r[0], tk)
                tk = bd.inc("dmp2", v.tensor_copy(out=x2r[1][:, 0:DIM // 2],
                                                  in_=vlo[:, DIM // 2:]))
                bd.wrote(x2r[1], tk)
                tk = bd.inc("dmp2", v.tensor_copy(out=x2r[2][:, 0:DIM // 2],
                                                  in_=vhi[:, 0:DIM // 2]))
                bd.wrote(x2r[2], tk)
                tk = bd.inc("dmp2", v.tensor_copy(out=x2r[3][:, 0:DIM // 2],
                                                  in_=vhi[:, DIM // 2:]))
                bd.wrote(x2r[3], tk)
            bd.VE(dump_kv2)

            def dump_kvd(g):
                for p in range(8):
                    bd.read(g, xfr[p])
                    tk = bd.inc("dmpd", g.dma_start(
                        out=dbg_d[0:128, NSUM * p:NSUM * (p + 1)],
                        in_=xfr[p][:, 0:NSUM]), 16)
                for i in range(4):
                    bd.read(g, x2r[i])
                    tk = bd.inc("dmpd", g.dma_start(
                        out=dbg_d[128 * (1 + i // 2):128 * (2 + i // 2),
                                  (DIM // 2) * (i % 2):(DIM // 2) * (i % 2 + 1)],
                        in_=x2r[i][:, 0:DIM // 2]), 16)
                bd.wait(g, tk)
            bd.GP(dump_kvd)
            return nc

        tok_wq = stream_w(Wq_d, wA, "wq")
        tok_wo = stream_w(Wo_d, wB, "wo")

        p2_groups = groups(R_OUT)
        if dev is not None and dev.startswith("x2g"):
            p2_groups = p2_groups[:int(dev[3:])]
        for gi, (g0, gw) in enumerate(p2_groups):
            slot = gi % 2

            def p2_ld(sy, g0=g0, gw=gw, slot=slot):
                for c in range(NCHUNK):
                    bd.wait_r(sy, xfr[c])
                for c in range(NCHUNK):
                    tk = bd.inc(f"xld{slot}", sy.dma_start(
                        out=xfr[c][:, 0:gw],
                        in_=xw_d[128 * c:128 * (c + 1), g0:g0 + gw]), 16)
                for c in range(NCHUNK):
                    bd.wrote(xfr[c], tk)
            bd.SY(p2_ld)
            xf32 = [(xfr[c], xfr[c][:, 0:gw])
                    for c in range(NCHUNK)]
            xnout = [(xgb[c], xgb[c][:, slot * 513 + 1:slot * 513 + 1 + gw])
                     for c in range(NCHUNK)]
            ln_tok = emit_ln(gw, xf32, xnout)

            for p in range(8):
                def p2_q(t, p=p, gi=gi, gw=gw, slot=slot, ln_tok=ln_tok):
                    if p == 0:
                        bd.wait(t, ln_tok)
                        if gi == 0:
                            bd.wait(t, tok_wq[-1])
                            bd.wait(t, tok_ks)
                    bank = pbank[4 + p % 2]
                    bd.wait_r(t, bank)
                    for c in range(NCHUNK):
                        i = t.matmul(bank[:, 0:gw],
                                     wA[c][:, 128 * p:128 * (p + 1)],
                                     xgb[c][:, slot * 513 + 1:slot * 513 + 1 + gw],
                                     start=(c == 0), stop=(c == NCHUNK - 1))
                    tk = bd.inc("p2_qmm", i)
                    bd.wrote(pbank[4 + p % 2], tk)
                    if p == 7:
                        for c in range(NCHUNK):
                            bd.did_read(xgb[c], tk)
                bd.TE(p2_q)

                def p2_qev(v, p=p, gw=gw):
                    bank = pbank[4 + p % 2]
                    bd.read(v, bank)
                    bd.wait_r(v, qo[p])
                    tk = bd.inc("p2_qev", v.tensor_scalar_add(
                        qo[p][:, 0:gw], bank[:, 0:gw], bqc[:, p:p + 1]))
                    bd.wrote(qo[p], tk)
                    bd.did_read(bank, tk)
                bd.VE(p2_qev)

            if dev == "p2q":
                def dump_q(v):
                    bd.wait(v, ("p2_qev", bd.counts["p2_qev"]))
                    for p in range(8):
                        tk = bd.inc("dmp", v.tensor_copy(out=x2r[p][:, 0:GS],
                                                         in_=qo[p][:, 0:GS]))
                        bd.wrote(x2r[p], tk)
                bd.VE(dump_q)

                def dump_qd(g):
                    for p in range(8):
                        bd.read(g, x2r[p])
                        tk = bd.inc("dmpd", g.dma_start(
                            out=dbg_d[128 * p:128 * (p + 1), 0:GS],
                            in_=x2r[p][:, 0:GS]), 16)
                    bd.wait(g, tk)
                bd.GP(dump_qd)
                return nc

            for qt in range(gw // 128):
                qti = g0 // 128 + qt
                qc = qt * 128
                mslot = maskt[0]

                def p2_mask(sy, qti=qti, mslot=mslot):
                    bd.wait_r(sy, mslot)
                    tk = bd.inc("p2_mld0", sy.dma_start(
                        out=mslot[:, 0:NSUM], in_=mask_o_d[qti, :, :]), 16)
                    bd.wrote(mslot, tk)
                bd.SY(p2_mask)

                def p2_s(t, qc=qc):
                    for pr in range(8):
                        bd.read(t, qo[pr])
                    for i in range(2):
                        bd.wait_r(t, pbank[4 + i])
                    for h in range(HEADS_O):
                        bank = pbank[4 + h // 8]
                        off = 64 * (h % 8)
                        pr = h // 2
                        i = t.matmul(bank[:, off:off + NSUM],
                                     qo[pr][:, qc:qc + 128],
                                     ksumh[h],
                                     start=True, stop=True)
                    tk = bd.inc("p2_smm", i)
                    for i in range(2):
                        bd.wrote(pbank[4 + i], tk)
                    for pr in range(8):
                        bd.did_read(qo[pr], tk)
                bd.TE(p2_s)

                if dev == "p2sa" and qt == 0:
                    def dump_sa(v):
                        bd.wait(v, ("p2_smm", bd.counts["p2_smm"]))
                        for i in range(2):
                            tk = bd.inc("dmp", v.tensor_copy(
                                out=x2r[i][:, 0:GS], in_=pbank[4 + i][:, 0:512]))
                            bd.wrote(x2r[i], tk)
                    bd.VE(dump_sa)

                    def dump_sad(g):
                        for i in range(2):
                            bd.read(g, x2r[i])
                            tk = bd.inc("dmpd", g.dma_start(
                                out=dbg_d[128 * i:128 * (i + 1), 0:GS],
                                in_=x2r[i][:, 0:GS]), 16)
                        bd.wait(g, tk)
                    bd.GP(dump_sad)
                    return nc

                def p2_soev(v, mslot=mslot):
                    bd.wait(v, ("p2_smm", bd.counts["p2_smm"]))
                    bd.read(v, mslot)
                    for i in range(2):
                        bd.wait_r(v, so_s[i])
                        m3 = mslot[:, 0:NSUM].unsqueeze(1) \
                            .broadcast_to([128, 8, NSUM])
                        tk = bd.inc("p2_soev", v.scalar_tensor_tensor(
                            out=so_s[i][:, 0:512].rearrange(
                                "p (h c) -> p h c", c=NSUM),
                            in0=pbank[4 + i][:, 0:512].rearrange(
                                "p (h c) -> p h c", c=NSUM),
                            scalar=0.125, in1=m3, op0=OP.mult, op1=OP.add))
                        bd.wrote(so_s[i], tk)
                        bd.did_read(pbank[4 + i], tk)
                        bd.did_read(mslot, tk)
                bd.VE(p2_soev)

                if dev == "p2sb" and qt == 0:
                    def dump_sb(v):
                        bd.wait(v, ("p2_soev", bd.counts["p2_soev"]))
                        for i in range(2):
                            tk = bd.inc("dmp", v.tensor_copy(
                                out=x2r[i][:, 0:GS], in_=so_s[i][:, 0:GS]))
                            bd.wrote(x2r[i], tk)
                    bd.VE(dump_sb)

                    def dump_sbd(g):
                        for i in range(2):
                            bd.read(g, x2r[i])
                            tk = bd.inc("dmpd", g.dma_start(
                                out=dbg_d[128 * i:128 * (i + 1), 0:GS],
                                in_=x2r[i][:, 0:GS]), 16)
                        bd.wait(g, tk)
                    bd.GP(dump_sbd)
                    return nc

                def p2_soft(v):
                    bd.wait_r(v, stat)
                    v.drain()
                    for i in range(2):
                        v.tensor_reduce(
                            out=stat[:, 16 * i:16 * i + 8],
                            in_=so_s[i][:, 0:512].rearrange(
                                "p (h c) -> p h c", c=NSUM),
                            axis=AX, op=OP.max, negate=True)
                        nb = stat[:, 16 * i:16 * i + 8].unsqueeze(2) \
                            .broadcast_to([128, 8, NSUM])
                        bd.wait_r(v, se_s[i])
                        v.drain()
                        tk = bd.inc("p2_se", v.tensor_tensor(
                            out=se_s[i][:, 0:512].rearrange(
                                "p (h c) -> p h c", c=NSUM),
                            in0=so_s[i][:, 0:512].rearrange(
                                "p (h c) -> p h c", c=NSUM),
                            in1=nb, op=OP.add))
                        bd.wrote(se_s[i], tk)
                        bd.did_read(so_s[i], tk)
                    stat.w = ("p2_se", bd.counts["p2_se"])
                bd.VE(p2_soft)

                def p2_exp(sc):
                    bd.wait(sc, ("p2_se", bd.counts["p2_se"]))
                    for i in range(2):
                        bd.wait_r(sc, pn_s[i])
                        tk = bd.inc("p2_exp", sc.activation(
                            out=pn_s[i][:, 0:512], in_=se_s[i][:, 0:512],
                            func=AF.Exp))
                        bd.wrote(pn_s[i], tk)
                        bd.did_read(se_s[i], tk)
                bd.SC(p2_exp)

                def p2_norm(v):
                    bd.wait(v, ("p2_exp", bd.counts["p2_exp"]))
                    for i in range(2):
                        v.tensor_reduce(
                            out=stat[:, 32 + 8 * i:40 + 8 * i],
                            in_=pn_s[i][:, 0:512].rearrange(
                                "p (h c) -> p h c", c=NSUM),
                            axis=AX, op=OP.add)
                        rz = stat[:, 48 + 8 * i:56 + 8 * i]
                        v.drain()
                        v.reciprocal(out=rz, in_=stat[:, 32 + 8 * i:40 + 8 * i])
                        rzb = rz.unsqueeze(2).broadcast_to([128, 8, NSUM])
                        v.drain()
                        tk = bd.inc("p2_pn", v.tensor_tensor(
                            out=pn_s[i][:, 0:512].rearrange(
                                "p (h c) -> p h c", c=NSUM),
                            in0=pn_s[i][:, 0:512].rearrange(
                                "p (h c) -> p h c", c=NSUM),
                            in1=rzb, op=OP.mult))
                        pn_s[i].w = tk
                bd.VE(p2_norm)

                if dev == "p2s" and qt == 0:
                    def dump_s(v):
                        bd.wait(v, ("p2_pn", bd.counts["p2_pn"]))
                        for i in range(2):
                            tk = bd.inc("dmp", v.tensor_copy(
                                out=x2r[i][:, 0:GS], in_=pn_s[i][:, 0:GS]))
                            bd.wrote(x2r[i], tk)
                        for i in range(2):
                            tk = bd.inc("dmp", v.tensor_copy(
                                out=x2r[2 + i][:, 0:GS], in_=so_s[i][:, 0:GS]))
                            bd.wrote(x2r[2 + i], tk)
                    bd.VE(dump_s)

                    def dump_sd(g):
                        for i in range(4):
                            bd.read(g, x2r[i])
                            tk = bd.inc("dmpd", g.dma_start(
                                out=dbg_d[128 * i:128 * (i + 1), 0:GS],
                                in_=x2r[i][:, 0:GS]), 16)
                        bd.wait(g, tk)
                    bd.GP(dump_sd)
                    return nc

                def p2_tr(t):
                    for i in range(2):
                        bd.read(t, pn_s[i])
                    bd.wait_r(t, pbf[0])
                    for p in range(8):
                        i = t.transpose(
                            pbf[0][:, 128 * p:128 * (p + 1)],
                            pn_s[p // 4][:, 128 * (p % 4):128 * (p % 4) + 128],
                            ident)
                    tk = bd.inc("p2_tr", i)
                    bd.wrote(pbf[0], tk)
                    for i in range(2):
                        bd.did_read(pn_s[i], tk)
                bd.TE(p2_tr)

                def p2_pcm(sc):
                    bd.read(sc, pbf[0])
                    bd.wait_r(sc, pcm)
                    tk = bd.inc("p2_pcm", sc.copy(out=pcm.ap, in_=pbf[0].ap))
                    bd.wrote(pcm, tk)
                    bd.did_read(pbf[0], tk)
                bd.SC(p2_pcm)

                if dev == "p2t" and qt == 0:
                    def dump_t(v):
                        bd.wait(v, ("p2_pcm", bd.counts["p2_pcm"]))
                        tk = bd.inc("dmp", v.tensor_copy(
                            out=x2r[0][:, 0:GS], in_=pcm[:, 0:512]))
                        bd.wrote(x2r[0], tk)
                        tk = bd.inc("dmp", v.tensor_copy(
                            out=x2r[1][:, 0:GS], in_=pcm[:, 512:1024]))
                        bd.wrote(x2r[1], tk)
                    bd.VE(dump_t)

                    def dump_td(g):
                        for i in range(2):
                            bd.read(g, x2r[i])
                            tk = bd.inc("dmpd", g.dma_start(
                                out=dbg_d[128 * i:128 * (i + 1), 0:GS],
                                in_=x2r[i][:, 0:GS]), 16)
                        bd.wait(g, tk)
                    bd.GP(dump_td)
                    return nc

                def p2_av(t, qti=qti):
                    bd.read(t, pcm)
                    if qti == 0:
                        bd.wait(t, tok_vs)
                    for bk in range(2, 6):
                        bd.wait_r(t, pbank[bk])
                    for h in range(HEADS_O):
                        vt = vlo if h % 2 == 0 else vhi
                        pr = h // 2
                        bank = pbank[2 + pr // 2]
                        base = 64 * (h % 2)
                        i = t.matmul(
                            bank[base:base + 64,
                                 128 * (pr % 2):128 * (pr % 2) + 128],
                            vt[:, 64 * h:64 * h + 64],
                            pcm[:, 128 * pr:128 * pr + 128],
                            start=True, stop=True,
                            tile_position=(0, base))
                    tk = bd.inc("p2_avmm", i)
                    for bk in range(2, 6):
                        bd.wrote(pbank[bk], tk)
                    bd.did_read(pcm, tk)
                bd.TE(p2_av)

                def p2_avev(v, qc=qc):
                    bd.wait(v, ("p2_avmm", bd.counts["p2_avmm"]))
                    for pr in range(8):
                        bank = pbank[2 + pr // 2]
                        tk = bd.inc("p2_avev", v.tensor_copy(
                            out=ofm[pr][:, qc:qc + 128],
                            in_=bank[:, 128 * (pr % 2):128 * (pr % 2) + 128]))
                        bd.did_read(bank, tk)
                        ofm[pr].w = tk
                bd.VE(p2_avev)


            if dev == "p2v":
                def dump_v(v):
                    bd.wait(v, ("p2_avev", bd.counts["p2_avev"]))
                    for p in range(8):
                        tk = bd.inc("dmp", v.tensor_copy(
                            out=x2r[p][:, 0:GS], in_=ofm[p][:, 0:GS]))
                        bd.wrote(x2r[p], tk)
                bd.VE(dump_v)

                def dump_vd(g):
                    for p in range(8):
                        bd.read(g, x2r[p])
                        tk = bd.inc("dmpd", g.dma_start(
                            out=dbg_d[128 * p:128 * (p + 1), 0:GS],
                            in_=x2r[p][:, 0:GS]), 16)
                    bd.wait(g, tk)
                bd.GP(dump_vd)
                return nc

            for dm in range(8):
                def p2_wo(t, dm=dm, gi=gi, gw=gw):
                    if dm == 0:
                        for pr in range(8):
                            bd.read(t, ofm[pr])
                        if gi == 0:
                            bd.wait(t, tok_wo[-1])
                    bank = pbank[4 + dm % 2]
                    bd.wait_r(t, bank)
                    for c in range(NCHUNK):
                        i = t.matmul(bank[:, 0:gw],
                                     wB[c][:, 128 * dm:128 * (dm + 1)],
                                     ofm[c][:, 0:gw],
                                     start=(c == 0), stop=(c == NCHUNK - 1))
                    tk = bd.inc("p2_womm", i)
                    bd.wrote(bank, tk)
                    if dm == 7:
                        for pr in range(8):
                            bd.did_read(ofm[pr], tk)
                bd.TE(p2_wo)

                def p2_x2(v, dm=dm, gw=gw, slot=slot):
                    bank = pbank[4 + dm % 2]
                    bd.read(v, bank)
                    bd.wait_r(v, x2r[dm])
                    tk = bd.inc("p2_x2", v.scalar_tensor_tensor(
                        out=x2r[dm][:, 0:gw], in0=bank[:, 0:gw],
                        scalar=boc[:, dm:dm + 1],
                        in1=xfr[dm][:, 0:gw],
                        op0=OP.add, op1=OP.add))
                    bd.wrote(x2r[dm], tk)
                    bd.did_read(bank, tk)
                    bd.did_read(xfr[dm], tk)
                bd.VE(p2_x2)

            def p2_sp(g, g0=g0, gw=gw):
                for c in range(NCHUNK):
                    bd.read(g, x2r[c])
                for c in range(NCHUNK):
                    tk = bd.inc("p2_sp", g.dma_start(
                        out=x2_sp[128 * c:128 * (c + 1), g0:g0 + gw],
                        in_=x2r[c][:, 0:gw]), 16)
                    if dev == "x2" or (dev is not None and dev.startswith("x2g")):
                        tk = bd.inc("p2_sp", g.dma_start(
                            out=dbg_d[128 * c:128 * (c + 1), g0:g0 + gw],
                            in_=x2r[c][:, 0:gw]), 16)
                for c in range(NCHUNK):
                    bd.did_read(x2r[c], tk)
            bd.GP(p2_sp)

        if dev == "x2" or (dev is not None and dev.startswith("x2g")):
            def fin(g):
                bd.wait(g, ("p2_sp", bd.counts["p2_sp"]))
            bd.GP(fin)
            return nc

        LAYERS = [
            dict(l=0, kv0=R_KV1[0], att0=R_ATT1[0], ffn0=R_ATT1[0],
                 mask_d=mask_i_d[0]),
            dict(l=1, kv0=R_KV2[0], att0=R_MID2[0], ffn0=R_FFN2[0],
                 mask_d=mask_i_d[1]),
        ]

        def init_tails(v):
            for c in range(NCHUNK):
                v.memset(xgb[c][:, 0:1], 0.0)
                v.memset(xgb[c][:, 513:514], 0.0)
                tk = bd.inc("tails", v.memset(fsp[c][:, 0:1], 0.0))
                xgb[c].rs.append(tk)
                fsp[c].rs.append(tk)
        bd.VE(init_tails)
        tok_tails = ("tails", bd.counts["tails"])


        for L in LAYERS:
            l = L["l"]
            kv0, att0, ffn0 = L["kv0"], L["att0"], L["ffn0"]
            tag = f"L{l}"
            kvg = groups((kv0 // GS * GS, WIN))
            attg = groups((att0, WIN))
            ffng = groups((ffn0, WIN))

            def wqkv_ld(g, l=l, tag=tag):
                for c in range(NCHUNK):
                    bd.wait_r(g, wA[c])
                    tk = bd.inc(f"wqk{tag}", g.dma_start(
                        out=wA[c][:, 0:1024],
                        in_=wqkv_d[l][128 * c:128 * (c + 1), 0:1024]), 16)
                    bd.wrote(wA[c], tk)
                    bd.wait_r(g, wB[c])
                    tk = bd.inc(f"wqk{tag}", g.dma_start(
                        out=wB[c][:, 0:512],
                        in_=wqkv_d[l][128 * c:128 * (c + 1), 1024:1536]), 16)
                    bd.wrote(wB[c], tk)
            bd.GP(wqkv_ld)
            tok_wqkv = (f"wqk{tag}", bd.counts[f"wqk{tag}"])

            for gi, (g0, gw) in enumerate(kvg):
                slot = gi % 2
                r0 = max(g0, kv0)
                a0, aw = r0 - g0, g0 + gw - r0
                qr0 = max(g0, att0)
                qa0, qaw = qr0 - g0, g0 + gw - qr0
                base = slot * 513 + 1

                if l == 0:
                    def p3_ld(sy, g0=g0, gw=gw, slot=slot, gi=gi):
                        if gi == 0:
                            bd.wait(sy, ("p2_sp", bd.counts.get("p2_sp", 0)))
                        for c in range(NCHUNK):
                            bd.wait_r(sy, xfr[c])
                        for c in range(NCHUNK):
                            tk = bd.inc(f"xld{slot}", sy.dma_start(
                                out=xfr[c][:, 0:gw],
                                in_=x2_sp[128 * c:128 * (c + 1), g0:g0 + gw]), 16)
                        for c in range(NCHUNK):
                            bd.wrote(xfr[c], tk)
                    bd.SY(p3_ld)
                    xf32 = [(xfr[c], xfr[c][:, 0:gw])
                            for c in range(NCHUNK)]
                    xnout = [(xgb[c], xgb[c][:, base:base + gw])
                             for c in range(NCHUNK)]
                    xf32out = [(xfr[c], xfr[c][:, 0:gw])
                               for c in range(NCHUNK)]
                    src_tok = emit_ln(gw, xf32, xnout, out_f32=xf32out)

                    def p3_sp(g, g0=g0, gw=gw, slot=slot, src_tok=src_tok):
                        bd.wait(g, src_tok)
                        for c in range(NCHUNK):
                            tk = bd.inc(f"xfsp{slot}", g.dma_start(
                                out=xffn_sp[128 * c:128 * (c + 1), g0:g0 + gw],
                                in_=xfr[c][:, 0:gw]), 16)
                        for c in range(NCHUNK):
                            xfr[c].rs.append(tk)
                    bd.GP(p3_sp)

                    def tails_c(v, slot=slot, gw=gw, src_tok=src_tok):
                        bd.wait(v, src_tok)
                        if bd.counts.get("tails", 0):
                            bd.wait(v, tok_tails)
                        nslot = 1 - slot
                        for c in range(NCHUNK):
                            bd.wait_r(v, xgb[c])
                            last = v.tensor_copy(
                                out=xgb[c][:, nslot * 513:nslot * 513 + 1],
                                in_=xgb[c][:, slot * 513 + gw:slot * 513 + gw + 1])
                        tk = bd.inc(f"tl{slot}", last)
                        for c in range(NCHUNK):
                            xgb[c].w = tk
                    bd.VE(tails_c)

                    if g0 <= HALO - 1 < g0 + gw:
                        def bz(v, slot=slot, g0=g0, src_tok=src_tok):
                            bd.wait(v, src_tok)
                            col = slot * 513 + 1 + (HALO - 1 - g0)
                            for c in range(NCHUNK):
                                last = v.tensor_scalar_mul(
                                    xgb[c][:, col:col + 1],
                                    xgb[c][:, col:col + 1], bcolb[:, 0:1])
                            tk = bd.inc(f"bz{slot}", last)
                            for c in range(NCHUNK):
                                xgb[c].w = tk
                        bd.VE(bz)
                    src_tok = (f"tl{slot}", bd.counts[f"tl{slot}"])
                else:
                    def p5_ld(sy, gi=gi, g0=g0, gw=gw, slot=slot):
                        if gi == 0:
                            bd.wait(sy, ("l1sp", bd.counts.get("l1sp", 0)))
                        lo1 = max(g0 - 1, R_ATT1[0])
                        ofs = slot * 513 + 1 - (g0 - lo1)
                        for c in range(NCHUNK):
                            bd.wait_r(sy, xgb[c])
                        for c in range(NCHUNK):
                            tk = bd.inc(f"gld{slot}", sy.dma_start(
                                out=xgb[c][:, ofs:slot * 513 + 1 + gw],
                                in_=l1ob_sp[128 * c:128 * (c + 1),
                                            lo1 - R_ATT1[0]:g0 + gw - R_ATT1[0]]), 16)
                        for c in range(NCHUNK):
                            bd.wrote(xgb[c], tk)
                    bd.SY(p5_ld)
                    if gi == 0:
                        def p5_ms(v, slot=slot):
                            bd.wait(v, (f"gld{slot}",
                                        bd.counts.get(f"gld{slot}", 0)))
                            for c in range(NCHUNK):
                                last = v.memset(
                                    xgb[c][:, slot * 513:slot * 513 + 129], 0.0)
                            tk = bd.inc(f"gld{slot}ms", last)
                            for c in range(NCHUNK):
                                xgb[c].w = tk
                        bd.VE(p5_ms)
                        src_tok = (f"gld{slot}ms", bd.counts[f"gld{slot}ms"])
                    else:
                        src_tok = (f"gld{slot}", bd.counts[f"gld{slot}"])

                def rhs_ap(c, a, b, slot=slot):
                    bb = slot * 513 + 1
                    if c < 4:
                        return xgb[c][:, bb + a:bb + b]
                    return xgb[c][:, bb + a - 1:bb + b - 1]

                def rot_ld(sy, g0=g0, gw=gw, slot=slot, l=l):
                    tabs = ["csq", "sq", "csk", "sk"]
                    for ti, nm in enumerate(tabs):
                        hb = wB[4 * slot + ti]
                        bd.wait_r(sy, hb)
                        tk = bd.inc(f"rt{slot}", sy.dma_start(
                            out=hb[:, 512:512 + gw],
                            in_=rot_d[nm][:, g0:g0 + gw]), 16)
                        hb.w = tk
                bd.SY(rot_ld)
                tok_rot = (f"rt{slot}", bd.counts[f"rt{slot}"])

                jobs = [("k", p) for p in range(4)]
                if qaw > 0:
                    jobs += [("q", p) for p in range(4)]
                nt = (aw + 127) // 128
                jobs += [("v", tt) for tt in range(nt)]
                for ji, (kind, p) in enumerate(jobs):
                    bank = pbank[[4, 5, 0, 1][ji % 4]]

                    def mm_f(t, kind=kind, p=p, bank=bank, gi=gi,
                             src_tok=src_tok, a0=a0, aw=aw, qa0=qa0, qaw=qaw,
                             rhs_ap=rhs_ap, ji=ji):
                        if ji == 0:
                            if gi == 0:
                                bd.wait(t, tok_wqkv)
                            bd.wait(t, src_tok)
                            if l == 0:
                                bd.wait(t, (f"bz{slot}",
                                            bd.counts.get(f"bz{slot}", 0)))
                        bd.wait_r(t, bank)
                        if kind == "k":
                            for c in range(NCHUNK):
                                i = t.matmul(bank[:, 0:aw],
                                             wA[c][:, 512 + 128 * p:640 + 128 * p],
                                             rhs_ap(c, a0, a0 + aw),
                                             start=(c == 0), stop=(c == NCHUNK - 1))
                        elif kind == "q":
                            for c in range(NCHUNK):
                                i = t.matmul(bank[:, 0:qaw],
                                             wA[c][:, 128 * p:128 * (p + 1)],
                                             rhs_ap(c, qa0, qa0 + qaw),
                                             start=(c == 0), stop=(c == NCHUNK - 1))
                        else:
                            tt = p
                            tw = min(128, aw - tt * 128)
                            for c in range(NCHUNK):
                                i = t.matmul(bank[0:tw, 0:512],
                                             rhs_ap(c, a0 + tt * 128,
                                                    a0 + tt * 128 + tw),
                                             wB[c][:, 0:512],
                                             start=(c == 0), stop=(c == NCHUNK - 1))
                        tk = bd.inc(f"pj{tag}", i)
                        bd.wrote(bank, tk)
                        for c in range(NCHUNK):
                            bd.did_read(xgb[c], tk)
                        if kind == "v":
                            for c in range(NCHUNK):
                                bd.did_read(wB[c], tk)
                        else:
                            for c in range(NCHUNK):
                                bd.did_read(wA[c], tk)
                    bd.TE(mm_f)

                    if kind == "v":
                        def vev_f(v, p=p, bank=bank, a0=a0, aw=aw, r0=r0,
                                  kv0=kv0):
                            tt = p
                            tw = min(128, aw - tt * 128)
                            vi = (r0 + tt * 128 - kv0) // 128
                            bd.read(v, bank)
                            bd.wait_r(v, vI[vi])
                            tk = bd.inc("vev", v.tensor_copy(
                                out=vI[vi][0:tw if tw < 128 else 128, :],
                                in_=bank[0:tw, 0:512]))
                            bd.wrote(vI[vi], tk)
                            bd.did_read(bank, tk)
                        bd.VE(vev_f)
                        continue

                    dst_is_q = (kind == "q")
                    w_ = qaw if dst_is_q else aw
                    col0 = (qr0 - att0) if dst_is_q else (r0 - kv0)
                    dest = qhat[p] if dst_is_q else khat[p]
                    tb_cs = wB[4 * slot + (0 if dst_is_q else 2)]
                    tb_s = wB[4 * slot + (1 if dst_is_q else 3)]
                    t_of = 512 + (qa0 if dst_is_q else a0)
                    tabs_cs = tb_cs[:, t_of:t_of + w_]
                    tabs_s = tb_s[:, t_of:t_of + w_]
                    scol = qscol[l] if dst_is_q else kscol[l]

                    def ev_f(sc, bank=bank, w_=w_, ji=ji):
                        bd.read(sc, bank)
                        rb = qn_s[ji % 2]
                        bd.wait_r(sc, rb)
                        tk = bd.inc("rawev", sc.copy(out=rb[:, 0:w_],
                                                     in_=bank[:, 0:w_]))
                        bd.wrote(rb, tk)
                        bd.did_read(bank, tk)
                    bd.SC(ev_f)

                    def sq_f(sc, w_=w_, ji=ji):
                        rb = qn_s[ji % 2]
                        sq = t2_s[ji % 2]
                        bd.read(sc, rb)
                        bd.wait_r(sc, sq)
                        tk = bd.inc("rsq", sc.activation(
                            out=sq[:, 0:w_], in_=rb[:, 0:w_], func=AF.Square))
                        bd.wrote(sq, tk)
                        rb.rs.append(tk)
                    bd.SC(sq_f)

                    def n2_f(t, w_=w_, ji=ji):
                        sq = t2_s[ji % 2]
                        bd.read(t, sq)
                        bd.wait_r(t, pbank[2])
                        tk = bd.inc("n2", t.matmul(
                            pbank[2][0:2, 0:w_], hsel2, sq[:, 0:w_],
                            start=True, stop=True))
                        bd.wrote(pbank[2], tk)
                        bd.did_read(sq, tk)
                    bd.TE(n2_f)

                    def rstd_f2(sc, w_=w_):
                        bd.read(sc, pbank[2])
                        bd.wait(sc, ("rbc2", bd.counts.get("rbc2", 0)))
                        tk = bd.inc("rs1", sc.activation(
                            out=rows[0:2, RO_LG:RO_LG + w_],
                            in_=pbank[2][0:2, 0:w_],
                            func=AF.Ln, bias=eps18[0:2, :]))
                        bd.wait(sc, tk)
                        tk2 = bd.inc("rs2", sc.activation(
                            out=rsb[0:2, 0:w_],
                            in_=rows[0:2, RO_LG:RO_LG + w_],
                            func=AF.Exp, scale=-0.5))
                        bd.did_read(pbank[2], tk2)
                    bd.SC(rstd_f2)

                    def rbc_f(t, w_=w_):
                        bd.wait(t, ("rs2", bd.counts["rs2"]))
                        bd.wait_r(t, pbank[3])
                        tk = bd.inc("rbc2", t.matmul(
                            pbank[3][:, 0:w_], psel2, rsb[0:2, 0:w_],
                            start=True, stop=True))
                        bd.wrote(pbank[3], tk)
                    bd.TE(rbc_f)

                    def rot_f(v, w_=w_, ji=ji, scol=scol, tabs_cs=tabs_cs,
                              tabs_s=tabs_s, dest=dest, col0=col0,
                              tok_rot=tok_rot, tb_cs=tb_cs, tb_s=tb_s):
                        rb = qn_s[ji % 2]
                        qn = t2_s[ji % 2]
                        t2 = t3_s[ji % 2]
                        bd.read(v, rb)
                        bd.read(v, pbank[3])
                        bd.wait_r(v, qn)
                        bd.wait(v, tok_rot)
                        v.scalar_tensor_tensor(
                            out=qn[:, 0:w_], in0=rb[:, 0:w_], scalar=scol[:, 0:1],
                            in1=pbank[3][:, 0:w_], op0=OP.mult, op1=OP.mult)
                        v.drain()
                        bd.wait_r(v, t2)
                        v.tensor_tensor(out=t2[0:64, 0:w_], in0=qn[64:128, 0:w_],
                                        in1=tabs_s[64:128, :], op=OP.mult)
                        v.tensor_tensor(out=t2[64:128, 0:w_], in0=qn[0:64, 0:w_],
                                        in1=tabs_s[0:64, :], op=OP.mult)
                        v.drain()
                        v.tensor_tensor(out=qn[:, 0:w_], in0=qn[:, 0:w_],
                                        in1=tabs_cs, op=OP.mult)
                        v.drain()
                        bd.wait_r(v, dest)
                        tk = bd.inc("rotd", v.tensor_tensor(
                            out=dest[:, col0:col0 + w_], in0=qn[:, 0:w_],
                            in1=t2[0:128, 0:w_], op=OP.add))
                        dest.w = tk
                        bd.wrote(qn, tk)
                        bd.wrote(t2, tk)
                        rb.rs.append(tk)
                        bd.did_read(pbank[3], tk)
                        bd.did_read(tb_cs, tk)
                        bd.did_read(tb_s, tk)
                    bd.VE(rot_f)

            if l == 0 and dev == "xffn":
                def dmp_xf(g):
                    bd.wait(g, ("xfsp0", bd.counts.get("xfsp0", 0)))
                    bd.wait(g, ("xfsp1", bd.counts.get("xfsp1", 0)))
                    for c in range(NCHUNK):
                        tk = bd.inc("dmpd", g.dma_start(
                            out=dbg_d[128 * c:128 * (c + 1), :],
                            in_=xffn_sp[128 * c:128 * (c + 1), :]), 16)
                    bd.wait(g, tk)
                bd.GP(dmp_xf)
                return nc

            def wo_ld(g, l=l, tag=tag):
                for c in range(4):
                    bd.wait_r(g, wB[c])
                    tk = bd.inc(f"wol{tag}", g.dma_start(
                        out=wB[c][:, 0:1024],
                        in_=wo2_d[l][128 * c:128 * (c + 1), 0:1024]), 16)
                    bd.wrote(wB[c], tk)
            bd.GP(wo_ld)
            tok_wo2 = (f"wol{tag}", bd.counts[f"wol{tag}"])

            for gi, (g0, gw) in enumerate(attg):
                slot = gi % 2
                def res_ld(sy, g0=g0, gw=gw, slot=slot, l=l, gi=gi):
                    if gi == 0:
                        if l == 0:
                            bd.wait(sy, ("xfsp0", bd.counts.get("xfsp0", 0)))
                            bd.wait(sy, ("xfsp1", bd.counts.get("xfsp1", 0)))
                        else:
                            bd.wait(sy, ("l1sp", bd.counts.get("l1sp", 0)))
                    for c in range(NCHUNK):
                        bd.wait_r(sy, xfr[c])
                    src = xffn_sp if l == 0 else l1o_sp
                    o = 0 if l == 0 else R_ATT1[0]
                    for c in range(NCHUNK):
                        tk = bd.inc(f"xld{slot}", sy.dma_start(
                            out=xfr[c][:, 0:gw],
                            in_=src[128 * c:128 * (c + 1), g0 - o:g0 - o + gw]), 16)
                    for c in range(NCHUNK):
                        bd.wrote(xfr[c], tk)
                bd.SY(res_ld)

                for qt in range(gw // 128):
                    w0 = g0 + 128 * qt
                    qti = (w0 - att0) // 128
                    qc = w0 - att0
                    mslot = maskt[0]

                    def i_mask(sy, qti=qti, mslot=mslot, L=L):
                        bd.wait_r(sy, mslot)
                        tk = bd.inc("p2_mld0", sy.dma_start(
                            out=mslot[:, 0:640], in_=L["mask_d"][qti, :, :]), 16)
                        bd.wrote(mslot, tk)
                    bd.SY(i_mask)

                    for h in range(HEADS_I):
                        pr = h // 2
                        par = h % 2
                        b0 = w0 - 512 - kv0
                        kb = kband[h % 2]
                        sb0, sb1 = (pbank[0], pbank[1]) if h % 2 == 0 else \
                                   (pbank[2], pbank[3])

                        def kb_f(v, pr=pr, par=par, b0=b0, kb=kb):
                            bd.read(v, khat[pr])
                            bd.wait_r(v, kb)
                            tk = bd.inc("kbs", v.tensor_scalar_mul(
                                kb[:, 0:640], khat[pr][:, b0:b0 + 640],
                                selperm[:, par:par + 1]))
                            bd.wrote(kb, tk)
                            khat[pr].rs.append(tk)
                        bd.VE(kb_f)

                        def s_f(t, pr=pr, qc=qc, kb=kb, sb0=sb0, sb1=sb1,
                                gi=gi, h=h):
                            if gi == 0 and qt == 0 and h == 0:
                                pass
                            bd.read(t, qhat[pr])
                            bd.read(t, kb)
                            bd.wait_r(t, sb0)
                            bd.wait_r(t, sb1)
                            t.matmul(sb0[:, 0:512], qhat[pr][:, qc:qc + 128],
                                     kb[:, 0:512], start=True, stop=True)
                            i = t.matmul(sb1[:, 0:128], qhat[pr][:, qc:qc + 128],
                                         kb[:, 512:640], start=True, stop=True)
                            tk = bd.inc("smm", i)
                            bd.wrote(sb0, tk)
                            bd.wrote(sb1, tk)
                            kb.rs.append(tk)
                            qhat[pr].rs.append(tk)
                        bd.TE(s_f)

                        def soev_f(v, mslot=mslot, h=h, sb0=sb0, sb1=sb1):
                            so = so_s[h % 2]
                            bd.read(v, sb0)
                            bd.read(v, sb1)
                            bd.read(v, mslot)
                            bd.wait_r(v, so)
                            v.scalar_tensor_tensor(
                                out=so[:, 0:512], in0=sb0[:, 0:512],
                                scalar=QK_SCALE, in1=mslot[:, 0:512],
                                op0=OP.mult, op1=OP.add)
                            tk = bd.inc("soe", v.scalar_tensor_tensor(
                                out=so[:, 512:640], in0=sb1[:, 0:128],
                                scalar=QK_SCALE, in1=mslot[:, 512:640],
                                op0=OP.mult, op1=OP.add))
                            bd.wrote(so, tk)
                            bd.did_read(sb0, tk)
                            bd.did_read(sb1, tk)
                            bd.did_read(mslot, tk)
                            v.drain()
                            tk = bd.inc("soe", v.tensor_reduce(
                                out=stat[:, h:h + 1], in_=so[:, 0:640],
                                axis=AX, op=OP.max, negate=True))
                            stat.w = tk
                        bd.VE(soev_f)

                        def exp_f(sc, h=h):
                            so = so_s[h % 2]
                            pn = pn_s[h % 2]
                            bd.read(sc, so)
                            bd.wait(sc, ("soe", bd.counts["soe"]))
                            bd.wait_r(sc, pn)
                            tk = bd.inc("expf", sc.activation(
                                out=pn[:, 0:640], in_=so[:, 0:640], func=AF.Exp,
                                bias=stat[:, h:h + 1],
                                accum_out=zcol[:, h:h + 1]))
                            bd.wrote(pn, tk)
                            zcol.w = tk
                            so.rs.append(tk)
                        bd.SC(exp_f)

                        def norm_f(v, h=h):
                            pn = pn_s[h % 2]
                            bd.read(v, pn)
                            bd.read(v, zcol)
                            v.reciprocal(out=stat[:, 32 + h:33 + h],
                                         in_=zcol[:, h:h + 1])
                            v.drain()
                            tk = bd.inc("pnn", v.tensor_scalar_mul(
                                pn[:, 0:640], pn[:, 0:640],
                                stat[:, 32 + h:33 + h]))
                            pn.w = tk
                        bd.VE(norm_f)

                        def tr_f(t, h=h):
                            pn = pn_s[h % 2]
                            bd.read(t, pn)
                            bd.wait_r(t, pbf[0])
                            for j in range(5):
                                i = t.transpose(
                                    pbf[0][:, 128 * j:128 * (j + 1)],
                                    pn[:, 128 * j:128 * (j + 1)], ident)
                            tk = bd.inc("trf", i)
                            bd.wrote(pbf[0], tk)
                            pn.rs.append(tk)
                        bd.TE(tr_f)

                        def pcm_f(sc):
                            bd.read(sc, pbf[0])
                            bd.wait_r(sc, pcm)
                            tk = bd.inc("pcmf", sc.copy(out=pcm[:, 0:640],
                                                        in_=pbf[0][:, 0:640]))
                            bd.wrote(pcm, tk)
                            bd.did_read(pbf[0], tk)
                        bd.SC(pcm_f)

                        def av_f(t, h=h, w0=w0, kv0=kv0):
                            bd.read(t, pcm)
                            bank = pbank[4 + h // 4]
                            if h % 4 == 0:
                                bd.wait_r(t, bank)
                            for j in range(5):
                                vi = (w0 - 512 + 128 * j - kv0) // 128
                                bd.read(t, vI[vi])
                                i = t.matmul(
                                    bank[0:64, 128 * (h % 4):128 * (h % 4) + 128],
                                    vI[vi][:, 64 * h:64 * (h + 1)],
                                    pcm[:, 128 * j:128 * (j + 1)],
                                    start=(j == 0), stop=(j == 4))
                            tk = bd.inc("avf", i)
                            bd.wrote(bank, tk) if h % 4 == 3 else bank.rs.append(tk)
                            bd.did_read(pcm, tk)
                            for j in range(5):
                                vi = (w0 - 512 + 128 * j - kv0) // 128
                                vI[vi].rs.append(tk)
                        bd.TE(av_f)

                        if h % 4 == 3:
                            def avev_f(v, h=h, qt=qt):
                                bank = pbank[4 + h // 4]
                                bd.read(v, bank)
                                for hh in range(h - 3, h + 1):
                                    prr = hh // 2
                                    ro = 64 * (hh % 2)
                                    bd.wait_r(v, oI[prr])
                                    tk = bd.inc("avev", v.tensor_copy(
                                        out=oI[prr][ro:ro + 64,
                                                    128 * qt:128 * qt + 128],
                                        in_=bank[0:64,
                                                 128 * (hh % 4):128 * (hh % 4) + 128]))
                                    oI[prr].w = tk
                                bd.did_read(bank, tk)
                            bd.VE(avev_f)

                for dm in range(8):
                    def wo_f(t, dm=dm, gi=gi, gw=gw, tok_wo2=tok_wo2):
                        if dm == 0:
                            if gi == 0:
                                bd.wait(t, tok_wo2)
                            for pr in range(4):
                                bd.read(t, oI[pr])
                        bank = pbank[4 + dm % 2]
                        bd.wait_r(t, bank)
                        for c in range(4):
                            i = t.matmul(bank[:, 0:gw],
                                         wB[c][:, 128 * dm:128 * (dm + 1)],
                                         oI[c][:, 0:gw],
                                         start=(c == 0), stop=(c == 3))
                        tk = bd.inc("womm2", i)
                        bd.wrote(bank, tk)
                        if dm == 7:
                            for pr in range(4):
                                oI[pr].rs.append(tk)
                    bd.TE(wo_f)

                    def mid_f(v, dm=dm, gw=gw, slot=slot):
                        bank = pbank[4 + dm % 2]
                        bd.read(v, bank)
                        bd.read(v, xfr[dm])
                        bd.wait_r(v, fsp[dm])
                        tk = bd.inc("midf", v.scalar_tensor_tensor(
                            out=fsp[dm][:, 1:1 + gw], in0=bank[:, 0:gw],
                            scalar=1.0,
                            in1=xfr[dm][:, 0:gw],
                            op0=OP.mult, op1=OP.add))
                        fsp[dm].w = tk
                        bd.did_read(bank, tk)
                        bd.did_read(xfr[dm], tk)
                    bd.VE(mid_f)

                if g0 <= HALO - 1 < g0 + gw:
                    def bzm(v, g0=g0):
                        col = 1 + (HALO - 1 - g0)
                        for c in range(NCHUNK):
                            bd.read(v, fsp[c])
                            last = v.tensor_scalar_mul(
                                fsp[c][:, col:col + 1], fsp[c][:, col:col + 1],
                                bcolb[:, 0:1])
                        tk = bd.inc("bzm", last)
                        for c in range(NCHUNK):
                            fsp[c].w = tk
                    bd.VE(bzm)

                def mid_sp_f(g, g0=g0, gw=gw, att0=att0):
                    for c in range(NCHUNK):
                        bd.read(g, fsp[c])
                    for c in range(NCHUNK):
                        tk = bd.inc("midsp", g.dma_start(
                            out=mid_sp[128 * c:128 * (c + 1),
                                       g0 - att0:g0 - att0 + gw],
                            in_=fsp[c][:, 1:1 + gw]), 16)
                    for c in range(NCHUNK):
                        fsp[c].rs.append(tk)
                bd.GP(mid_sp_f)

                xf32 = [(fsp[c], fsp[c][:, 1:1 + gw] if c < 4
                         else fsp[c][:, 0:gw]) for c in range(NCHUNK)]
                xnout = [(xgb[c], xgb[c][:, slot * 513 + 1:slot * 513 + 1 + gw])
                         for c in range(NCHUNK)]
                mid_tok = emit_ln(gw, xf32, xnout, affine_l=l)

                def xn_sp_f(g, g0=g0, gw=gw, att0=att0, slot=slot,
                            mid_tok=mid_tok):
                    bd.wait(g, mid_tok)
                    for c in range(NCHUNK):
                        tk = bd.inc("xnsp", g.dma_start(
                            out=xnmid_sp[128 * c:128 * (c + 1),
                                         g0 - att0:g0 - att0 + gw],
                            in_=xgb[c][:, slot * 513 + 1:slot * 513 + 1 + gw]), 16)
                    for c in range(NCHUNK):
                        xgb[c].rs.append(tk)
                bd.GP(xn_sp_f)

                def tail_f(v, gw=gw, mid_tok=mid_tok):
                    bd.wait(v, mid_tok)
                    for c in range(NCHUNK):
                        last = v.tensor_copy(out=fsp[c][:, 0:1],
                                             in_=fsp[c][:, gw:gw + 1])
                    tk = bd.inc("ftl", last)
                    for c in range(NCHUNK):
                        fsp[c].w = tk
                bd.VE(tail_f)

            def w2home(j):
                if j < 8:
                    return qhat[j // 2][:, 1024 * (j % 2):1024 * (j % 2) + 1024], \
                        qhat[j // 2]
                if j < 20:
                    jj = j - 8
                    return khat[jj // 3][:, 1024 * (jj % 3):1024 * (jj % 3) + 1024], \
                        khat[jj // 3]
                jj = j - 20
                ap = so_s[jj].ap.bitcast(BF16)
                return ap[:, 0:1024], so_s[jj]

            def w2_ld(g, l=l, tag=tag):
                homes = []
                for j in range(N_JT):
                    ap, home = w2home(j)
                    if home not in homes:
                        bd.wait_r(g, home)
                        homes.append(home)
                for j in range(N_JT):
                    wdt = W1_TILES[j]
                    ap, home = w2home(j)
                    tk = bd.inc(f"w2l{tag}", g.dma_start(
                        out=ap[0:wdt, :],
                        in_=w2_d[l][128 * j:128 * j + wdt, :]), 16)
                for home in homes:
                    bd.wrote(home, tk)
            bd.GP(w2_ld)
            tok_w2 = (f"w2l{tag}", bd.counts[f"w2l{tag}"])

            for gi, (g0, gw) in enumerate(ffng):
                slot = gi % 2

                def ffn_ld(sy, g0=g0, gw=gw, slot=slot, ffn0=ffn0, att0=att0,
                           l=l, gi=gi):
                    if gi == 0:
                        bd.wait(sy, ("xnsp", bd.counts.get("xnsp", 0)))
                        bd.wait(sy, ("midsp", bd.counts.get("midsp", 0)))
                    for c in range(NCHUNK):
                        bd.wait_r(sy, xgb[c])
                    for c in range(NCHUNK):
                        tk = bd.inc(f"gld{slot}", sy.dma_start(
                            out=xgb[c][:, slot * 513:slot * 513 + gw],
                            in_=xnmid_sp[128 * c:128 * (c + 1),
                                         g0 - att0:g0 - att0 + gw]), 16)
                    for c in range(NCHUNK):
                        bd.wrote(xgb[c], tk)
                    for c in range(NCHUNK):
                        bd.wait_r(sy, xfr[c])
                    for c in range(NCHUNK):
                        tk = bd.inc(f"xld0", sy.dma_start(
                            out=xfr[c][:, 0:gw],
                            in_=mid_sp[128 * c:128 * (c + 1),
                                       g0 - att0:g0 - att0 + gw]), 16)
                    for c in range(NCHUNK):
                        bd.wrote(xfr[c], tk)
                bd.SY(ffn_ld)
                tok_ffnld = (f"gld{slot}", bd.counts[f"gld{slot}"])
                tok_ffnx = ("xld0", bd.counts["xld0"])

                for j in range(N_JT):
                    wdt = W1_TILES[j]
                    sa = wA[2 * (j % 2)]
                    sg = wA[2 * (j % 2) + 1]

                    def w1_ld2(g, j=j, wdt=wdt, sa=sa, sg=sg, l=l):
                        off = sum(2 * W1_TILES[jj] for jj in range(j))
                        for ag, ws in ((0, sa), (1, sg)):
                            bd.wait_r(g, ws)
                            for c in range(NCHUNK):
                                tk = bd.inc(f"w1l{j % 2}{ag}", g.dma_start(
                                    out=ws[:, wdt * c:wdt * (c + 1)],
                                    in_=w1i_d[l][128 * c:128 * (c + 1),
                                                 off + wdt * ag:
                                                 off + wdt * (ag + 1)]), 16)
                            bd.wrote(ws, tk)
                    bd.GP(w1_ld2)

                    def w1_mm(t, j=j, wdt=wdt, sa=sa, sg=sg, gw=gw, slot=slot,
                              gi=gi, tok_ffnld=tok_ffnld, tok_w2=tok_w2):
                        if j == 0 and gi == 0:
                            bd.wait(t, tok_w2)
                        if j == 0:
                            bd.wait(t, tok_ffnld)
                        for ag, ws in ((0, sa), (1, sg)):
                            bd.read(t, ws)
                            bank = pbank[(4 + ag) if j % 2 == 0 else (2 + ag)]
                            bd.wait_r(t, bank)
                            for c in range(NCHUNK):
                                i = t.matmul(
                                    bank[0:wdt, 0:gw],
                                    ws[:, wdt * c:wdt * (c + 1)],
                                    xgb[c][:, slot * 513:slot * 513 + gw],
                                    start=(c == 0), stop=(c == NCHUNK - 1))
                            tk = bd.inc("w1mm", i)
                            bd.wrote(bank, tk)
                            ws.rs.append(tk)
                        if j == N_JT - 1:
                            for c in range(NCHUNK):
                                xgb[c].rs.append(tk)
                    bd.TE(w1_mm)

                    def gelu_f(sc, j=j, wdt=wdt, gw=gw):
                        bank = pbank[5 if j % 2 == 0 else 3]
                        bd.read(sc, bank)
                        gl = xsq[j % 2]
                        bd.wait_r(sc, gl)
                        tk = bd.inc("gel", sc.activation(
                            out=gl[0:wdt, 0:gw], in_=bank[0:wdt, 0:gw],
                            func=AF.Gelu, bias=b1col[l][0:wdt, 2 * j + 1:2 * j + 2]))
                        bd.wrote(gl, tk)
                        bd.did_read(bank, tk)
                    bd.SC(gelu_f)

                    def m_f(v, j=j, wdt=wdt, gw=gw, l=l):
                        bank = pbank[4 if j % 2 == 0 else 2]
                        gl = xsq[j % 2]
                        bd.read(v, bank)
                        bd.read(v, gl)
                        bd.wait_r(v, vI[j])
                        tk = bd.inc("mf", v.scalar_tensor_tensor(
                            out=vI[j][0:wdt, 0:gw], in0=bank[0:wdt, 0:gw],
                            scalar=b1col[l][0:wdt, 2 * j:2 * j + 1],
                            in1=gl[0:wdt, 0:gw], op0=OP.add, op1=OP.mult))
                        bd.wrote(vI[j], tk)
                        bd.did_read(bank, tk)
                        gl.rs.append(tk)
                    bd.VE(m_f)

                accs = [pbank[i] for i in range(6)] + [pbfF[0], pbfF[1]]

                def w2_mm(t, gw=gw, l=l):
                    for i in range(2):
                        bd.read(t, pbf[i])
                        bd.wait_r(t, pbf[i])
                    for dm in range(8):
                        bd.wait_r(t, accs[dm])
                    for j in range(N_JT):
                        wdt = W1_TILES[j]
                        ap, home = w2home(j)
                        bd.read(t, home)
                        bd.read(t, vI[j])
                        for dm in range(8):
                            i = t.matmul(
                                accs[dm][:, 0:gw] if dm < 6 else accs[dm][:, 0:gw],
                                ap[0:wdt, 128 * dm:128 * (dm + 1)],
                                vI[j][0:wdt, 0:gw],
                                start=(j == 0), stop=(j == N_JT - 1))
                        tk = bd.inc("w2mm", i)
                        vI[j].rs.append(tk)
                        home.rs.append(tk)
                    for dm in range(8):
                        bd.wrote(accs[dm], ("w2mm", bd.counts["w2mm"]))
                bd.TE(w2_mm)

                for dm in range(8):
                    def out_f(v, dm=dm, g0=g0, gw=gw, l=l,
                              tok_ffnx=tok_ffnx):
                        acc = accs[dm]
                        bd.read(v, acc)
                        bd.wait(v, tok_ffnx)
                        bd.wait_r(v, fsp[dm])
                        tk = bd.inc("outf", v.scalar_tensor_tensor(
                            out=fsp[dm][:, 1:1 + gw], in0=acc[:, 0:gw],
                            scalar=b2c[l][:, dm:dm + 1],
                            in1=xfr[dm][:, 0:gw], op0=OP.add, op1=OP.add))
                        fsp[dm].w = tk
                        bd.did_read(acc, tk)
                        if dm >= 6:
                            bd.did_read(pbf[dm - 6], tk)
                        bd.did_read(xfr[dm], tk)
                    bd.VE(out_f)

                if l == 0 and g0 <= HALO - 1 < g0 + gw:
                    def bzo(v, g0=g0):
                        col = 1 + (HALO - 1 - g0)
                        for c in range(NCHUNK):
                            bd.read(v, fsp[c])
                            last = v.tensor_scalar_mul(
                                fsp[c][:, col:col + 1], fsp[c][:, col:col + 1],
                                bcolb[:, 0:1])
                        tk = bd.inc("bzo", last)
                        for c in range(NCHUNK):
                            fsp[c].w = tk
                    bd.VE(bzo)

                def out_sp(g, g0=g0, gw=gw, l=l, ffn0=ffn0):
                    for c in range(NCHUNK):
                        bd.read(g, fsp[c])
                    for c in range(NCHUNK):
                        if l == 0:
                            tk = bd.inc("l1sp", g.dma_start(
                                out=l1o_sp[128 * c:128 * (c + 1),
                                           g0 - ffn0:g0 - ffn0 + gw],
                                in_=fsp[c][:, 1:1 + gw]), 16)
                            tk = bd.inc("l1sp", g.dma_start(
                                out=l1ob_sp[128 * c:128 * (c + 1),
                                            g0 - ffn0:g0 - ffn0 + gw],
                                in_=fsp[c][:, 1:1 + gw]), 16)
                        else:
                            tk = bd.inc("osp", g.dma_start(
                                out=out_d[128 * c:128 * (c + 1),
                                          g0 - R_FFN2[0]:g0 - R_FFN2[0] + gw],
                                in_=fsp[c][:, 1:1 + gw]), 16)
                            bd.wait(g, tk)
                            tk = bd.inc("osp", g.dma_start(
                                out=out_d[128 * c:128 * (c + 1),
                                          g0 - R_FFN2[0]:g0 - R_FFN2[0] + gw],
                                in_=x2_sp[128 * c:128 * (c + 1), g0:g0 + gw],
                                accum_op=OP.add), 16)
                    for c in range(NCHUNK):
                        fsp[c].rs.append(tk)
                bd.GP(out_sp)

            if l == 0 and dev == "l1":
                def dmp_l1(g):
                    bd.wait(g, ("l1sp", bd.counts["l1sp"]))
                    for c in range(NCHUNK):
                        tk = bd.inc("dmpd", g.dma_start(
                            out=dbg_d[128 * c:128 * (c + 1), :],
                            in_=l1o_sp[128 * c:128 * (c + 1), :]), 16)
                    bd.wait(g, tk)
                bd.GP(dmp_l1)
                return nc

        def fin_f(g):
            bd.wait(g, ("osp", bd.counts["osp"]))
        bd.GP(fin_f)
    return nc




def kernel(**inputs):
    nc = build_program(dev=None)
    in_maps = [_prep_core(inputs, c // 2, c % 2) for c in range(8)]
    res = run_bass_kernel_spmd(nc, in_maps, core_ids=list(range(8)))
    out = np.zeros((B, T, DIM), np.float32)
    for c in range(8):
        b, s = c // 2, c % 2
        out[b, s * OWN:(s + 1) * OWN, :] = res.results[c]["out"].T
    return out


# version 03
# speedup vs baseline: 1.0056x; 1.0056x over previous
import numpy as np
import ml_dtypes
import contextlib
import concourse.bass as bass
import concourse.mybir as mybir
from concourse.bass_utils import run_bass_kernel_spmd
from concourse.alu_op_type import AluOpType as OP

F32 = mybir.dt.float32
BF16 = mybir.dt.bfloat16
AF = mybir.ActivationFunctionType
AX = mybir.AxisListType.X
bfloat16 = ml_dtypes.bfloat16

B, T, DIM = 4, 4096, 1024
NCHUNK = DIM // 128
HALO, OWN = 1280, 2048
WIN = HALO + OWN
NEG = -30000.0
HEADS_O, HD = 16, 64
NSUM = 64
HEADS_I = 8
WLOC = 512
HID = 2730
H2 = 2 * HID
QK_SCALE = 8.0
R_OUT = (0, WIN)
R_KV1 = (128, WIN)
R_ATT1 = (640, WIN)
R_KV2 = (640, WIN)
R_MID2 = (1152, WIN)
R_FFN2 = (1280, WIN)
GS = 512


def groups(r):
    out, t = [], r[0]
    while t < r[1]:
        w = min(GS, r[1] - t)
        out.append((t, w))
        t += w
    return out


def qtiles(r):
    return list(range(r[0], r[1], 128))


N_QT_OUT = len(qtiles(R_OUT))
N_QT_A1 = len(qtiles(R_ATT1))
N_QT_A2 = len(qtiles(R_MID2))

W1_TILES = []
_j = 0
while _j < HID:
    W1_TILES.append(min(128, HID - _j))
    _j += 128
N_JT = len(W1_TILES)



def _pair_perm():
    perm = []
    for p in range(HEADS_I // 2):
        a, b = 2 * p, 2 * p + 1
        perm += list(range(64 * a, 64 * a + 32))
        perm += list(range(64 * b, 64 * b + 32))
        perm += list(range(64 * a + 32, 64 * a + 64))
        perm += list(range(64 * b + 32, 64 * b + 64))
    return np.array(perm)


def _qs_col(qs):
    d = np.concatenate([qs[0:32], qs[0:32], qs[32:64], qs[32:64]])
    return np.ascontiguousarray(d.reshape(128, 1).astype(np.float32))


def _rotary_tables(t_abs):
    half = np.arange(0, HD, 2, dtype=np.float64)
    inv_freq = 1.0 / 10000.0 ** (half / HD)
    tt = t_abs.astype(np.float64)
    fr = tt[:, None] * inv_freq[None, :]
    base = (half + 0.4 * HD) / (1.4 * HD)
    power = (tt - T // 2) / float(WLOC // 2)
    sc = base[None, :] ** power[:, None]
    cos, sin = np.cos(fr), np.sin(fr)

    def dup(a):
        return np.concatenate([a, a], axis=1)
    cos64, sin64, sc64 = dup(cos), dup(sin), dup(sc)
    sign = np.where(np.arange(HD) < 32, -1.0, 1.0)[None, :]
    tabs = [cos64 * sc64, sin64 * sc64 * sign, cos64 / sc64, sin64 / sc64 * sign]
    rows = (np.arange(128) % 32) + 32 * (np.arange(128) // 64)
    out = [np.ascontiguousarray(a.T[rows].astype(bfloat16)) for a in tabs]
    for i in (1, 3):
        out[i] = np.ascontiguousarray(
            np.concatenate([out[i][64:], out[i][:64]], axis=0))
    return out


def _inner_masks(first_abs):
    r = np.arange(128)[:, None]
    c = np.arange(640)[None, :]
    out = []
    for t0 in first_abs:
        bad = (c < r) | (c > r + 512) | ((t0 - 512 + c) < 0)
        out.append(np.where(bad, NEG, 0.0))
    return np.stack(out).astype(bfloat16)


def _outer_masks(first_abs):
    r = np.arange(128)[:, None]
    c = np.arange(NSUM)[None, :]
    out = []
    for t0 in first_abs:
        bad = (64 * c) > (t0 + r)
        out.append(np.where(bad, NEG, 0.0))
    return np.stack(out).astype(bfloat16)


def _prep_core(inp, b, s):
    S = s * OWN
    x = np.asarray(inp["L0_attn_x"][b])
    xT = np.ascontiguousarray(x.T)
    lo = S - HALO
    xw = np.zeros((DIM, WIN), np.float32)
    src_lo = max(lo, 0)
    xw[:, src_lo - lo:] = xT[:, src_lo:S + OWN]
    d = {
        "xf": xT,
        "xw": xw,
        "Wq": np.ascontiguousarray(np.asarray(inp["L0_attn_Wq"])),
        "Wk": np.ascontiguousarray(np.asarray(inp["L0_attn_Wk"])),
        "Wv": np.ascontiguousarray(np.asarray(inp["L0_attn_Wv"])),
        "Wo": np.ascontiguousarray(np.asarray(inp["L0_attn_Wo"])),
        "bq_c": np.ascontiguousarray(np.asarray(inp["L0_attn_bq"]).reshape(NCHUNK, 128).T),
        "bk_c": np.ascontiguousarray(np.asarray(inp["L0_attn_bk"]).reshape(NCHUNK, 128).T),
        "bo_c": np.ascontiguousarray(np.asarray(inp["L0_attn_bo"]).reshape(NCHUNK, 128).T),
        "bv_r": np.ascontiguousarray(np.asarray(inp["L0_attn_bv"]).reshape(1, DIM)),
    }
    perm = _pair_perm()
    for l in range(2):
        wqkv = np.asarray(inp["L0_ffn_wqkv"][l])
        d[f"wqkv{l}"] = np.ascontiguousarray(np.concatenate(
            [wqkv[:, 0:512][:, perm], wqkv[:, 512:1024][:, perm],
             wqkv[:, 1024:1536]], axis=1))
        d[f"qs{l}"] = _qs_col(np.asarray(inp["L0_ffn_q_scale"][l]))
        d[f"ks{l}"] = _qs_col(np.asarray(inp["L0_ffn_k_scale"][l]))
        d[f"wo{l}"] = np.ascontiguousarray(np.asarray(inp["L0_ffn_wo"][l]))
        d[f"lg_c{l}"] = np.ascontiguousarray(
            np.asarray(inp["L0_ffn_ln_g"][l]).reshape(NCHUNK, 128).T)
        d[f"lg_r{l}"] = np.ascontiguousarray(
            np.asarray(inp["L0_ffn_ln_g"][l]).reshape(1, DIM))
        d[f"lb_r{l}"] = np.ascontiguousarray(
            np.asarray(inp["L0_ffn_ln_b"][l]).reshape(1, DIM))
        w1 = np.asarray(inp["L0_ffn_w1"][l])
        b1 = np.asarray(inp["L0_ffn_b1"][l])
        pieces, bcols = [], []
        j = 0
        while j < HID:
            wdt = min(128, HID - j)
            pieces += [w1[:, j:j + wdt], w1[:, HID + j:HID + j + wdt]]
            bc = np.zeros((128, 2), np.float32)
            bc[:wdt, 0] = b1[j:j + wdt]
            bc[:wdt, 1] = b1[HID + j:HID + j + wdt]
            bcols.append(bc)
            j += wdt
        d[f"w1i{l}"] = np.ascontiguousarray(np.concatenate(pieces, axis=1))
        d[f"b1i{l}"] = np.ascontiguousarray(np.concatenate(bcols, axis=1))
        d[f"w2_{l}"] = np.ascontiguousarray(np.asarray(inp["L0_ffn_w2"][l]))
        d[f"b2_c{l}"] = np.ascontiguousarray(
            np.asarray(inp["L0_ffn_b2"][l]).reshape(NCHUNK, 128).T)
    d["ident"] = np.eye(128, dtype=bfloat16)
    ev = np.zeros((128,), np.float32)
    ev[0:32] = 1.0
    ev[64:96] = 1.0
    d["hsel2"] = np.ascontiguousarray(
        np.stack([ev, 1.0 - ev], axis=1).astype(bfloat16))
    d["psel2"] = np.ascontiguousarray(
        np.stack([ev, 1.0 - ev], axis=0).astype(bfloat16))
    d["selperm"] = np.ascontiguousarray(np.stack([ev, 1.0 - ev], axis=1))
    d["bcol"] = np.full((128, 1), 0.0 if s == 0 else 1.0, np.float32)
    d["mask_o"] = _outer_masks([lo + 128 * i for i in range(N_QT_OUT)])
    d["mask_i1"] = _inner_masks([lo + R_ATT1[0] + 128 * i for i in range(N_QT_A1)])
    d["mask_i2"] = _inner_masks([lo + R_MID2[0] + 128 * i for i in range(N_QT_A2)])
    tabs = _rotary_tables(np.arange(lo, S + OWN))
    for nm, a in zip(("csq", "sq", "csk", "sk"), tabs):
        d[f"rot_{nm}"] = a
    return d



class Buf:

    def __init__(self, ap):
        self.ap = ap
        self.w = None
        self.rs = []

    def __getitem__(self, idx):
        return self.ap[idx]


class Bld:

    def __init__(self, nc, es, block):
        self.nc = nc
        self.es = es
        self.block = block
        self.sems = {}
        self.counts = {}

    def _sem(self, name):
        if name not in self.sems:
            self.sems[name] = self.es.enter_context(self.nc.semaphore(name))
            self.counts[name] = 0
        return self.sems[name]

    def inc(self, name, instr, k=1):
        s = self._sem(name)
        instr.then_inc(s, k)
        self.counts[name] += k
        return (name, self.counts[name])

    def wait(self, eng, tok):
        if tok is None:
            return
        name, v = tok
        if v > 0:
            eng.wait_ge(self._sem(name), v)

    def wait_r(self, eng, buf):
        for t in buf.rs:
            self.wait(eng, t)

    def wrote(self, buf, tok):
        buf.w = tok
        buf.rs = []

    def read(self, eng, buf):
        self.wait(eng, buf.w)

    def did_read(self, buf, tok):
        buf.rs.append(tok)

    def SY(self, f):
        self.block.sync(f)

    def TE(self, f):
        self.block.tensor(f)

    def VE(self, f):
        self.block.vector(f)

    def SC(self, f):
        self.block.scalar(f)

    def GP(self, f):
        self.block.gpsimd(f)



def build_program(dev=None):
    nc = bass.Bass("TRN2", target_bir_lowering=False)
    es = contextlib.ExitStack()

    def din(name, shape, dt=F32):
        return nc.dram_tensor(name, list(shape), dt, kind="ExternalInput").ap()

    xf_d = din("xf", [DIM, T])
    xw_d = din("xw", [DIM, WIN])
    Wq_d = din("Wq", [DIM, DIM]); Wk_d = din("Wk", [DIM, DIM])
    Wv_d = din("Wv", [DIM, DIM]); Wo_d = din("Wo", [DIM, DIM])
    bq_d = din("bq_c", [128, NCHUNK]); bk_d = din("bk_c", [128, NCHUNK])
    bo_d = din("bo_c", [128, NCHUNK]); bv_d = din("bv_r", [1, DIM])
    ident_d = din("ident", [128, 128], BF16)
    mask_o_d = din("mask_o", [N_QT_OUT, 128, NSUM], BF16)
    mask_i_d = [din("mask_i1", [N_QT_A1, 128, 640], BF16),
                din("mask_i2", [N_QT_A2, 128, 640], BF16)]
    rot_d = {nm: din(f"rot_{nm}", [128, WIN], BF16)
             for nm in ("csq", "sq", "csk", "sk")}
    wqkv_d = [din(f"wqkv{l}", [DIM, 1536]) for l in range(2)]
    qs_d = [din(f"qs{l}", [128, 1]) for l in range(2)]
    ks_d = [din(f"ks{l}", [128, 1]) for l in range(2)]
    wo2_d = [din(f"wo{l}", [512, DIM]) for l in range(2)]
    lgc_d = [din(f"lg_c{l}", [128, NCHUNK]) for l in range(2)]
    lgr_d = [din(f"lg_r{l}", [1, DIM]) for l in range(2)]
    lbr_d = [din(f"lb_r{l}", [1, DIM]) for l in range(2)]
    w1i_d = [din(f"w1i{l}", [DIM, H2]) for l in range(2)]
    b1i_d = [din(f"b1i{l}", [128, 44]) for l in range(2)]
    w2_d = [din(f"w2_{l}", [HID, DIM]) for l in range(2)]
    b2_d = [din(f"b2_c{l}", [128, NCHUNK]) for l in range(2)]

    hsel2_d = din("hsel2", [128, 2], BF16)
    psel2_d = din("psel2", [2, 128], BF16)
    selperm_d = din("selperm", [128, 2])
    bcol_d = din("bcol", [128, 1])

    out_d = nc.dram_tensor("out", [DIM, OWN], F32, kind="ExternalOutput").ap()

    def dsp(name, tokens, dt=F32):
        return nc.dram_tensor(name, [DIM, tokens], dt, kind="Internal").ap()

    x2_sp = dsp("x2_sp", WIN)
    xffn_sp = dsp("xffn_sp", WIN)
    mid_sp = dsp("mid_sp", WIN - R_ATT1[0])
    l1o_sp = dsp("l1o_sp", WIN - R_ATT1[0])
    l1ob_sp = dsp("l1ob_sp", WIN - R_ATT1[0], BF16)
    xnmid_sp = dsp("xnmid_sp", WIN - R_ATT1[0], BF16)
    dbg_d = None
    if dev in ("p2q", "p2s", "p2t", "p2v", "p2sa", "p2sb"):
        dbg_d = nc.dram_tensor("dbg", [DIM, 4096], F32, kind="ExternalOutput").ap()
    elif dev is not None and dev.startswith("x2g"):
        dbg_d = nc.dram_tensor("dbg", [DIM, WIN], F32, kind="ExternalOutput").ap()
    elif dev == "x2":
        dbg_d = nc.dram_tensor("dbg", [DIM, WIN], F32, kind="ExternalOutput").ap()
    elif dev == "sum":
        dbg_d = nc.dram_tensor("dbg", [DIM, NSUM], F32, kind="ExternalOutput").ap()
    elif dev == "kv":
        dbg_d = nc.dram_tensor("dbg", [3 * 128, DIM], F32, kind="ExternalOutput").ap()
    elif dev == "xffn":
        dbg_d = nc.dram_tensor("dbg", [DIM, WIN], F32, kind="ExternalOutput").ap()
    elif dev == "l1":
        dbg_d = nc.dram_tensor("dbg", [DIM, WIN - R_ATT1[0]], F32,
                               kind="ExternalOutput").ap()

    def sbt(name, shape, dt):
        return es.enter_context(nc.sbuf_tensor(name, list(shape), dt))[:]

    def sbuf(name, shape, dt):
        return Buf(sbt(name, shape, dt))

    with es:
        ident = sbt("identb", [128, 128], BF16)
        bqc = sbt("bqc", [128, NCHUNK], F32)
        bkc = sbt("bkc", [128, NCHUNK], F32)
        boc = sbt("boc", [128, NCHUNK], F32)
        b2c = [sbt(f"b2c{l}", [128, NCHUNK], F32) for l in range(2)]
        lgcol = [sbt(f"lgcol{l}", [128, NCHUNK], F32) for l in range(2)]
        b1col = [sbt(f"b1col{l}", [128, 44], F32) for l in range(2)]
        qscol = [sbt(f"qscol{l}", [128, 1], F32) for l in range(2)]
        kscol = [sbt(f"kscol{l}", [128, 1], F32) for l in range(2)]
        rows = sbt("rows", [128, 3072], F32)
        RO_BV, RO_MU, RO_M2, RO_LG = 0, 1024, 1536, 2048
        rows_bf = sbt("rows_bf", [128, 1024], BF16)
        crow = sbt("crow", [128, 5696], BF16)
        CR_BV, CR_ONE, CR_M1 = 0, 5120, 5184
        CR_LG = [1024, 3072]
        CR_LB = [2048, 4096]
        e0ones = sbt("e0ones", [128, 128], BF16)
        onescol = sbt("onescol", [128, 1], BF16)
        epscol = sbt("epscol", [128, 1], F32)

        selcol = sbt("selcol", [128, 2], F32)
        wA = [sbuf(f"wA{c}", [128, DIM], BF16) for c in range(NCHUNK)]
        wB = [sbuf(f"wB{c}", [128, DIM], BF16) for c in range(NCHUNK)]

        xfr = [sbuf(f"xfr{c}", [128, GS], F32) for c in range(NCHUNK)]
        xbr = [sbuf("xbr0", [128, GS], BF16)]
        xsq = [sbuf(f"xsq{i}", [128, GS], BF16) for i in range(2)]
        t1r = [sbuf("t1r0", [128, GS], F32)]

        so_s = [sbuf(f"so{i}", [128, 640], F32) for i in range(2)]
        stat = sbuf("stat_s", [128, 64], F32)
        pcm = sbuf("pcm", [128, 1024], BF16)
        maskt = [sbuf(f"maskt{i}", [128, 640], BF16) for i in range(2)]

        khat = [sbuf(f"khat{p}", [128, 3200], BF16) for p in range(4)]
        qhat = [sbuf(f"qhat{p}", [128, 2688], BF16) for p in range(4)]
        vI = [sbuf(f"vI{t}", [128, 512], BF16) for t in range(25)]
        kband = [sbuf(f"kband{i}", [128, 640], BF16) for i in range(2)]
        pn_s = kband
        xgb = [sbuf(f"xgb{c}", [128, 2 * 513], BF16) for c in range(NCHUNK)]
        fsp = [sbuf(f"fsp{c}", [128, 513], F32) for c in range(NCHUNK)]
        qn_s = [sbuf(f"qn{i}", [128, 512], BF16) for i in range(2)]
        t2_s = [sbuf(f"t2{i}", [128, 512], BF16) for i in range(2)]
        t3_s = [sbuf(f"t3{i}", [128, 512], BF16) for i in range(2)]
        hsel2 = sbt("hsel2b", [128, 2], BF16)
        psel2 = sbt("psel2b", [2, 128], BF16)
        selperm = sbt("selpermb", [128, 2], F32)
        bcolb = sbt("bcolb", [128, 1], F32)
        zcol = sbuf("zcol", [128, 8], F32)
        sum_fm = [vI[2 + c // 4].ap[:, 128 * (c % 4):128 * (c % 4) + NSUM]
                  for c in range(NCHUNK)]
        ksumh = [vI[4 + h // 8].ap[:, 64 * (h % 8):64 * (h % 8) + NSUM]
                 for h in range(16)]
        oI = [Buf(qn_s[0].ap), Buf(qn_s[1].ap), Buf(t2_s[0].ap), Buf(t2_s[1].ap)]
        rsb = sbt("rsb", [2, 512], BF16)
        eps18 = sbt("eps18", [128, 1], F32)
        qo = [Buf(khat[p % 4].ap[:, 512 * (p // 4):512 * (p // 4) + 512])
              for p in range(8)]
        ofm = [Buf(khat[p % 4].ap[:, 1024 + 512 * (p // 4):
                                  1024 + 512 * (p // 4) + 512])
               for p in range(8)]
        vlo_b = Buf(khat[0].ap[:, 2048:3072])
        vhi_b = Buf(khat[1].ap[:, 2048:3072])
        vlo, vhi = vlo_b.ap, vhi_b.ap
        se_s = [Buf(qhat[i].ap.bitcast(F32)[:, 0:512]) for i in range(2)]
        x2r = [Buf(fsp[c].ap[:, 1:513]) for c in range(NCHUNK)]
        pbank = [Buf(es.enter_context(
            nc.psum_tensor(f"pb{i}", [128, 512], F32))[:]) for i in range(6)]
        pbf = [Buf(es.enter_context(
            nc.psum_tensor(f"pQ{i}", [128, 1024], BF16))[:]) for i in range(2)]
        pbfF = [Buf(pbf[i].ap.bitcast(F32)) for i in range(2)]

        block = es.enter_context(nc.Block())
        bd = Bld(nc, es, block)

        def p0_sync(sy):
            for dst, src in [(ident, ident_d), (bqc, bq_d), (bkc, bk_d),
                             (boc, bo_d)]:
                bd.inc("cld", sy.dma_start(out=dst, in_=src), 16)
            for l in range(2):
                for dst, src in [(b2c[l], b2_d[l]), (lgcol[l], lgc_d[l]),
                                 (b1col[l], b1i_d[l]), (qscol[l], qs_d[l]),
                                 (kscol[l], ks_d[l])]:
                    bd.inc("cld", sy.dma_start(out=dst, in_=src), 16)
            bd.inc("cld", sy.dma_start(out=rows[0:1, RO_BV:RO_BV + DIM],
                                       in_=bv_d), 16)
            bd.inc("cld", sy.dma_start(out=hsel2, in_=hsel2_d), 16)
            bd.inc("cld", sy.dma_start(out=psel2, in_=psel2_d), 16)
            bd.inc("cld", sy.dma_start(out=selperm, in_=selperm_d), 16)
            bd.inc("cld", sy.dma_start(out=bcolb, in_=bcol_d), 16)
        bd.SY(p0_sync)
        tok_cld = ("cld", bd.counts["cld"])

        def p0_init(v):
            bd.wait(v, tok_cld)
            v.memset(e0ones, 0.0)
            v.drain()
            v.memset(e0ones[0:1, 0:128], 1.0)
            v.memset(onescol, 1.0)
            v.memset(crow[0:1, CR_ONE:CR_ONE + 64], 1.0)
            v.memset(crow[0:1, CR_M1:CR_M1 + 512], -1.0)
            v.memset(epscol, 1e-5)
            v.memset(eps18, 1e-18)
            v.memset(vlo, 0.0)
            v.memset(vhi, 0.0)
            v.memset(rows_bf, 0.0)
            v.memset(selcol[0:64, 0:1], 1.0)
            v.memset(selcol[64:128, 0:1], 0.0)
            v.memset(selcol[0:64, 1:2], 0.0)
            v.memset(selcol[64:128, 1:2], 1.0)
            bd.inc("cinit", v.tensor_copy(out=crow[0:1, CR_BV:CR_BV + DIM],
                                          in_=rows[0:1, RO_BV:RO_BV + DIM]))
        bd.VE(p0_init)
        tok_cinit = ("cinit", bd.counts["cinit"])

        for l in range(2):
            for which, src in (("lg", lgr_d[l]), ("lb", lbr_d[l])):
                def r1(sy, src=src):
                    bd.wait(sy, ("rowcast", bd.counts.get("rowcast", 0)))
                    if bd.counts.get("rowld", 0) == 0:
                        bd.wait(sy, tok_cinit)
                    bd.inc("rowld", sy.dma_start(
                        out=rows[0:1, RO_LG:RO_LG + DIM], in_=src), 16)
                bd.SY(r1)

                def r2(v, l=l, which=which):
                    bd.wait(v, ("rowld", bd.counts["rowld"]))
                    off = CR_LG[l] if which == "lg" else CR_LB[l]
                    bd.inc("rowcast", v.tensor_copy(
                        out=crow[0:1, off:off + DIM],
                        in_=rows[0:1, RO_LG:RO_LG + DIM]))
                bd.VE(r2)

        def emit_ln(w, xf32_bufs, xn_slices, affine_l=None, extra_bf=None,
                    out_f32=None):
            bsx, bsq, brb, bmb = pbank[0], pbank[1], pbank[2], pbank[3]
            for c in range(NCHUNK):
                def cs_f(g, c=c):
                    bd.read(g, xf32_bufs[c][0])
                    bd.wait_r(g, xbr[0])
                    tk = bd.inc("ln_cast", g.tensor_copy(
                        out=xbr[0][:, 0:w], in_=xf32_bufs[c][1]))
                    bd.wrote(xbr[0], tk)
                    bd.did_read(xf32_bufs[c][0], tk)
                bd.GP(cs_f)

                def sq_f(sc, c=c):
                    bd.wait(sc, xf32_bufs[c][0].w)
                    bd.wait_r(sc, xsq[c % 2])
                    tk = bd.inc("ln_sq", sc.activation(
                        out=xsq[c % 2][:, 0:w], in_=xf32_bufs[c][1],
                        func=AF.Square))
                    bd.wrote(xsq[c % 2], tk)
                    bd.did_read(xf32_bufs[c][0], tk)
                bd.SC(sq_f)

                def sums_f(t, c=c):
                    if c == 0:
                        bd.wait_r(t, bsx)
                        bd.wait_r(t, bsq)
                    bd.read(t, xbr[0])
                    i1 = t.matmul(bsx[0:1, 0:w], onescol, xbr[0][:, 0:w],
                                  start=(c == 0), stop=(c == NCHUNK - 1))
                    tk = bd.inc("ln_sx", i1)
                    bd.did_read(xbr[0], tk)
                    bd.read(t, xsq[c % 2])
                    i2 = t.matmul(bsq[0:1, 0:w], onescol, xsq[c % 2][:, 0:w],
                                  start=(c == 0), stop=(c == NCHUNK - 1))
                    tk = bd.inc("ln_sq2", i2)
                    bd.did_read(xsq[c % 2], tk)
                    if c == NCHUNK - 1:
                        bd.wrote(bsx, ("ln_sx", bd.counts["ln_sx"]))
                        bd.wrote(bsq, ("ln_sq2", bd.counts["ln_sq2"]))
                bd.TE(sums_f)

            def stats_f(v):
                bd.read(v, bsx)
                bd.wait(v, ("ln_apply", bd.counts.get("ln_apply", 0)))
                t1 = bd.inc("vch", v.tensor_scalar_mul(
                    rows[0:1, RO_MU:RO_MU + w], bsx[0:1, 0:w], 1.0 / DIM))
                bd.wait(v, t1)
                t2 = bd.inc("vch", v.tensor_tensor(
                    out=rows[0:1, RO_M2:RO_M2 + w],
                    in0=rows[0:1, RO_MU:RO_MU + w],
                    in1=rows[0:1, RO_MU:RO_MU + w], op=OP.mult))
                bd.read(v, bsq)
                bd.wait(v, t2)
                tk = bd.inc("ln_ve", v.scalar_tensor_tensor(
                    out=rows[0:1, RO_M2:RO_M2 + w], in0=bsq[0:1, 0:w],
                    scalar=1.0 / DIM, in1=rows[0:1, RO_M2:RO_M2 + w],
                    op0=OP.mult, op1=OP.subtract))
                bd.did_read(bsx, tk)
                bd.did_read(bsq, tk)
            bd.VE(stats_f)

            def rstd_f(sc):
                bd.wait(sc, ("ln_ve", bd.counts["ln_ve"]))
                bd.wait(sc, ("ln_rbc", bd.counts.get("ln_rbc", 0)))
                bd.wait(sc, tok_cinit)
                tl = bd.inc("sch", sc.activation(
                    out=rows[0:1, RO_M2:RO_M2 + w],
                    in_=rows[0:1, RO_M2:RO_M2 + w], func=AF.Ln,
                    bias=epscol[0:1, :]))
                bd.wait(sc, tl)
                bd.inc("ln_rstd", sc.activation(
                    out=rows_bf[0:1, 0:w], in_=rows[0:1, RO_M2:RO_M2 + w],
                    func=AF.Exp, scale=-0.5))
            bd.SC(rstd_f)

            def mr_f(v):
                bd.wait(v, ("ln_rstd", bd.counts["ln_rstd"]))
                bd.wait(v, ("ln_mbc", bd.counts.get("ln_mbc", 0)))
                bd.wait(v, ("ln_dp", bd.counts.get("ln_dp", 0)))
                bd.inc("ln_mr", v.tensor_tensor(
                    out=rows_bf[0:1, 512:512 + w],
                    in0=rows[0:1, RO_MU:RO_MU + w],
                    in1=rows_bf[0:1, 0:w], op=OP.mult))
            bd.VE(mr_f)

            def bc_f(t):
                bd.wait(t, tok_cinit)
                bd.wait_r(t, brb)
                bd.wait(t, ("ln_rstd", bd.counts["ln_rstd"]))
                tk = bd.inc("ln_rbc", t.matmul(
                    brb[:, 0:w], e0ones, rows_bf[:, 0:w], start=True, stop=True))
                bd.wrote(brb, tk)
                if affine_l is None:
                    bd.wait_r(t, bmb)
                    bd.wait(t, ("ln_mr", bd.counts["ln_mr"]))
                    tk = bd.inc("ln_mbc", t.matmul(
                        bmb[:, 0:w], e0ones, rows_bf[:, 512:512 + w],
                        start=True, stop=True))
                    bd.wrote(bmb, tk)
            bd.TE(bc_f)

            if affine_l is None:
                def apply_plain(v):
                    bd.read(v, pbank[2])
                    bd.read(v, pbank[3])
                    for c in range(NCHUNK):
                        bd.wait_r(v, t1r[0])
                        bd.wait_r(v, xn_slices[c][0])
                        v.drain()
                        v.scalar_tensor_tensor(
                            out=t1r[0][:, 0:w], in0=xf32_bufs[c][1],
                            scalar=1.0,
                            in1=pbank[2][:, 0:w], op0=OP.mult, op1=OP.mult)
                        v.drain()
                        last = v.tensor_tensor(
                            out=xn_slices[c][1], in0=t1r[0][:, 0:w],
                            in1=pbank[3][:, 0:w], op=OP.subtract)
                        if out_f32 is not None:
                            bd.wait_r(v, out_f32[c][0])
                            v.drain()
                            last = v.tensor_tensor(
                                out=out_f32[c][1], in0=t1r[0][:, 0:w],
                                in1=pbank[3][:, 0:w], op=OP.subtract)
                        if c == NCHUNK - 1:
                            tk = bd.inc("ln_apply", last)
                        else:
                            tk = bd.inc("ln_app_c", last)
                        bd.wrote(t1r[0], tk)
                        bd.wrote(xn_slices[c][0], tk)
                        if out_f32 is not None:
                            bd.wrote(out_f32[c][0], tk)
                        bd.did_read(xf32_bufs[c][0], tk)
                    bd.did_read(pbank[2], tk)
                    bd.did_read(pbank[3], tk)
                bd.VE(apply_plain)
            else:
                for c in range(NCHUNK):
                    def dp_f(t, c=c):
                        bd.wait_r(t, bmb)
                        if c == 0:
                            bd.wait(t, ("ln_mr", bd.counts["ln_mr"]))
                        lgs = crow[0:1, CR_LG[affine_l] + 128 * c:
                                    CR_LG[affine_l] + 128 * (c + 1)]
                        lbs = crow[0:1, CR_LB[affine_l] + 128 * c:
                                    CR_LB[affine_l] + 128 * (c + 1)]
                        t.matmul(bmb[:, 0:w], lgs, rows_bf[0:1, 512:512 + w],
                                 start=True, stop=False)
                        tk = bd.inc("ln_dp", t.matmul(
                            bmb[:, 0:w], lbs, crow[0:1, CR_M1:CR_M1 + w],
                            start=False, stop=True))
                        bd.wrote(bmb, tk)
                    bd.TE(dp_f)

                    def ap_f(v, c=c):
                        if c == 0:
                            bd.read(v, pbank[2])
                        bd.read(v, pbank[3])
                        bd.wait_r(v, t1r[0])
                        bd.wait_r(v, xn_slices[c][0])
                        v.drain()
                        v.scalar_tensor_tensor(
                            out=t1r[0][:, 0:w], in0=xf32_bufs[c][1],
                            scalar=lgcol[affine_l][:, c:c + 1],
                            in1=pbank[2][:, 0:w], op0=OP.mult, op1=OP.mult)
                        v.drain()
                        last = v.tensor_tensor(
                            out=xn_slices[c][1], in0=t1r[0][:, 0:w],
                            in1=pbank[3][:, 0:w], op=OP.subtract)
                        if c == NCHUNK - 1:
                            tk = bd.inc("ln_apply", last)
                        else:
                            tk = bd.inc("ln_dused", last)
                        bd.wrote(t1r[0], tk)
                        bd.wrote(xn_slices[c][0], tk)
                        bd.did_read(xf32_bufs[c][0], tk)
                        bd.did_read(bmb, tk)
                        if c == NCHUNK - 1:
                            bd.did_read(pbank[2], tk)
                    bd.VE(ap_f)
            return ("ln_apply", bd.counts["ln_apply"])

        for g in range(8):
            slot = g % 2

            def p1_ld(sy, g=g, slot=slot):
                for c in range(NCHUNK):
                    bd.wait_r(sy, xfr[c])
                for c in range(NCHUNK):
                    tk = bd.inc(f"xld{slot}", sy.dma_start(
                        out=xfr[c][:, 0:GS],
                        in_=xf_d[128 * c:128 * (c + 1), g * GS:(g + 1) * GS]), 16)
                for c in range(NCHUNK):
                    bd.wrote(xfr[c], tk)
            bd.SY(p1_ld)
            xf32 = [(xfr[c], xfr[c][:, 0:GS])
                    for c in range(NCHUNK)]
            xnout = [(xgb[c], xgb[c][:, slot * 513 + 1:slot * 513 + 1 + GS])
                     for c in range(NCHUNK)]
            ln_tok = emit_ln(GS, xf32, xnout)

            def p1_sum(v, g=g, slot=slot, ln_tok=ln_tok):
                bd.wait(v, ln_tok)
                bd.wait_r(v, stat)
                pslot = (g - 1) % 2
                for c in range(NCHUNK):
                    if c > 0:
                        v.drain()
                    src = xgb[c][:, slot * 513 + 2: slot * 513 + 2 + 448]
                    src3 = src.rearrange("p (j w) -> p j w", w=64)
                    v.tensor_reduce(out=stat[:, 1:8], in_=src3,
                                    axis=AX, op=OP.add)
                    if g >= 1:
                        v.tensor_reduce(
                            out=stat[:, 0:1],
                            in_=xgb[c][:, pslot * 513 + 450: pslot * 513 + 513],
                            axis=AX, op=OP.add)
                        v.drain()
                        v.tensor_tensor(out=stat[:, 0:1], in0=stat[:, 0:1],
                                        in1=xgb[c][:, slot * 513 + 1:slot * 513 + 2],
                                        op=OP.add)
                        v.drain()
                        last = v.tensor_scalar_mul(
                            sum_fm[c][:, 8 * g: 8 * g + 8],
                            stat[:, 0:8], 1.0 / 64)
                    else:
                        v.drain()
                        last = v.tensor_scalar_mul(
                            sum_fm[c][:, 1:8], stat[:, 1:8], 1.0 / 64)
                        last = v.tensor_copy(out=sum_fm[c][:, 0:1],
                                             in_=xgb[c][:, 1:2])
                tk = bd.inc("p1_sum", last)
                bd.wrote(stat, tk)
                for c in range(NCHUNK):
                    bd.did_read(xgb[c], tk)
            bd.VE(p1_sum)
        tok_sum = ("p1_sum", bd.counts["p1_sum"])

        def stream_w(dram, dest, tag, ncols=DIM, nchunk=NCHUNK):
            def ld(g):
                for c in range(nchunk):
                    bd.wait_r(g, dest[c])
                for c in range(nchunk):
                    tk = bd.inc(f"wc_{tag}", g.dma_start(
                        out=dest[c][:, 0:ncols],
                        in_=dram[128 * c:128 * (c + 1), 0:ncols]), 16)
                for c in range(nchunk):
                    bd.wrote(dest[c], tk)
            bd.GP(ld)
            return [( f"wc_{tag}", bd.counts[f"wc_{tag}"])]

        if dev == "sum":
            def dump_sum(v):
                bd.wait(v, tok_sum)
                for c in range(NCHUNK):
                    tk = bd.inc("dmp", v.tensor_copy(out=t1r[c][:, 0:NSUM],
                                                     in_=sum_fm[c]))
                    bd.wrote(t1r[c], tk)
            bd.VE(dump_sum)

            def dump_dma(g):
                for c in range(NCHUNK):
                    bd.read(g, t1r[c])
                for c in range(NCHUNK):
                    tk = bd.inc("dmpd", g.dma_start(
                        out=dbg_d[128 * c:128 * (c + 1), :],
                        in_=t1r[c][:, 0:NSUM]), 16)
                bd.wait(g, tk)
            bd.GP(dump_dma)
            return nc

        tok_wk = stream_w(Wk_d, wA, "wk")
        tok_wv = stream_w(Wv_d, wB, "wv")

        def p1b_ks(t):
            bd.wait(t, tok_sum)
            bd.wait(t, tok_wk[-1])
            for p in range(8):
                bank = pbank[4 + p // 4]
                if p % 4 == 0:
                    bd.wait_r(t, bank)
                off = 64 * (p % 4)
                for c in range(NCHUNK):
                    i = t.matmul(bank[:, off:off + NSUM],
                                 wA[c][:, 128 * p:128 * (p + 1)], sum_fm[c],
                                 start=(c == 0), stop=(c == NCHUNK - 1))
                tk = bd.inc("ks_mm", i)
                if p % 4 == 3:
                    bd.wrote(bank, tk)
            for c in range(NCHUNK):
                bd.did_read(wA[c], ("ks_mm", bd.counts["ks_mm"]))
        bd.TE(p1b_ks)

        def p1b_ksev(v):
            bd.wait(v, ("ks_mm", bd.counts["ks_mm"]))
            bd.wait(v, tok_cinit)
            for h in range(16):
                p = h // 2
                bank = pbank[4 + p // 4]
                off = 64 * (p % 4)
                last = v.tensor_scalar(
                    out=ksumh[h], in0=bank[:, off:off + NSUM],
                    scalar1=bkc[:, p:p + 1], scalar2=selcol[:, h % 2:h % 2 + 1],
                    op0=OP.add, op1=OP.mult)
            tk = bd.inc("ks_ev", last)
            bd.did_read(pbank[4], tk)
            bd.did_read(pbank[5], tk)
        bd.VE(p1b_ksev)
        tok_ks = ("ks_ev", bd.counts["ks_ev"])

        def p1b_vs(t):
            bd.wait(t, tok_sum)
            bd.wait(t, tok_wv[-1])
            bd.wait(t, tok_cinit)
            for h in range(2):
                bank = pbank[4 + h]
                bd.wait_r(t, bank)
                for c in range(NCHUNK):
                    t.matmul(bank[0:64, 0:512], sum_fm[c],
                             wB[c][:, 512 * h:512 * (h + 1)],
                             start=(c == 0), stop=False)
                tk = bd.inc("vs_mm", t.matmul(
                    bank[0:64, 0:512], crow[0:1, CR_ONE:CR_ONE + 64],
                    crow[0:1, CR_BV + 512 * h:CR_BV + 512 * (h + 1)],
                    start=False, stop=True))
                bd.wrote(bank, tk)
            for c in range(NCHUNK):
                bd.did_read(wB[c], ("vs_mm", bd.counts["vs_mm"]))
        bd.TE(p1b_vs)

        def p1b_vsev(v):
            bd.wait(v, ("vs_mm", bd.counts["vs_mm"]))
            bd.wait(v, tok_cinit)
            for h in range(2):
                bank = pbank[4 + h]
                v.tensor_copy(out=vlo[0:64, 512 * h:512 * (h + 1)],
                              in_=bank[0:64, 0:512])
                last = v.tensor_copy(out=vhi[64:128, 512 * h:512 * (h + 1)],
                                     in_=bank[0:64, 0:512])
                bd.did_read(bank, bd.inc("vs_ev", last))
        bd.VE(p1b_vsev)
        tok_vs = ("vs_ev", bd.counts["vs_ev"])

        if dev == "kv":
            def dump_kv(v):
                bd.wait(v, tok_ks)
                bd.wait(v, tok_vs)
                for p in range(8):
                    tk = bd.inc("dmp", v.tensor_copy(
                        out=t1r[0][:, 0:NSUM] if False else xfr[0][:, 0:NSUM],
                        in_=ksum[p]))
            bd.VE(dump_kv)

            def dump_kv2(v):
                for p in range(8):
                    tk = bd.inc("dmp2", v.tensor_copy(
                        out=xfr[p][:, 0:NSUM], in_=ksum[p]))
                    bd.wrote(xfr[p], tk)
                tk = bd.inc("dmp2", v.tensor_copy(out=x2r[0][:, 0:DIM // 2],
                                                  in_=vlo[:, 0:DIM // 2]))
                bd.wrote(x2r[0], tk)
                tk = bd.inc("dmp2", v.tensor_copy(out=x2r[1][:, 0:DIM // 2],
                                                  in_=vlo[:, DIM // 2:]))
                bd.wrote(x2r[1], tk)
                tk = bd.inc("dmp2", v.tensor_copy(out=x2r[2][:, 0:DIM // 2],
                                                  in_=vhi[:, 0:DIM // 2]))
                bd.wrote(x2r[2], tk)
                tk = bd.inc("dmp2", v.tensor_copy(out=x2r[3][:, 0:DIM // 2],
                                                  in_=vhi[:, DIM // 2:]))
                bd.wrote(x2r[3], tk)
            bd.VE(dump_kv2)

            def dump_kvd(g):
                for p in range(8):
                    bd.read(g, xfr[p])
                    tk = bd.inc("dmpd", g.dma_start(
                        out=dbg_d[0:128, NSUM * p:NSUM * (p + 1)],
                        in_=xfr[p][:, 0:NSUM]), 16)
                for i in range(4):
                    bd.read(g, x2r[i])
                    tk = bd.inc("dmpd", g.dma_start(
                        out=dbg_d[128 * (1 + i // 2):128 * (2 + i // 2),
                                  (DIM // 2) * (i % 2):(DIM // 2) * (i % 2 + 1)],
                        in_=x2r[i][:, 0:DIM // 2]), 16)
                bd.wait(g, tk)
            bd.GP(dump_kvd)
            return nc

        tok_wq = stream_w(Wq_d, wA, "wq")
        tok_wo = stream_w(Wo_d, wB, "wo")

        p2_groups = groups(R_OUT)
        if dev is not None and dev.startswith("x2g"):
            p2_groups = p2_groups[:int(dev[3:])]
        for gi, (g0, gw) in enumerate(p2_groups):
            slot = gi % 2

            def p2_ld(sy, g0=g0, gw=gw, slot=slot):
                for c in range(NCHUNK):
                    bd.wait_r(sy, xfr[c])
                for c in range(NCHUNK):
                    tk = bd.inc(f"xld{slot}", sy.dma_start(
                        out=xfr[c][:, 0:gw],
                        in_=xw_d[128 * c:128 * (c + 1), g0:g0 + gw]), 16)
                for c in range(NCHUNK):
                    bd.wrote(xfr[c], tk)
            bd.SY(p2_ld)
            xf32 = [(xfr[c], xfr[c][:, 0:gw])
                    for c in range(NCHUNK)]
            xnout = [(xgb[c], xgb[c][:, slot * 513 + 1:slot * 513 + 1 + gw])
                     for c in range(NCHUNK)]
            ln_tok = emit_ln(gw, xf32, xnout)

            for p in range(8):
                def p2_q(t, p=p, gi=gi, gw=gw, slot=slot, ln_tok=ln_tok):
                    if p == 0:
                        bd.wait(t, ln_tok)
                        if gi == 0:
                            bd.wait(t, tok_wq[-1])
                            bd.wait(t, tok_ks)
                    bank = pbank[4 + p % 2]
                    bd.wait_r(t, bank)
                    for c in range(NCHUNK):
                        i = t.matmul(bank[:, 0:gw],
                                     wA[c][:, 128 * p:128 * (p + 1)],
                                     xgb[c][:, slot * 513 + 1:slot * 513 + 1 + gw],
                                     start=(c == 0), stop=(c == NCHUNK - 1))
                    tk = bd.inc("p2_qmm", i)
                    bd.wrote(pbank[4 + p % 2], tk)
                    if p == 7:
                        for c in range(NCHUNK):
                            bd.did_read(xgb[c], tk)
                bd.TE(p2_q)

                def p2_qev(v, p=p, gw=gw):
                    bank = pbank[4 + p % 2]
                    bd.read(v, bank)
                    bd.wait_r(v, qo[p])
                    tk = bd.inc("p2_qev", v.tensor_scalar_add(
                        qo[p][:, 0:gw], bank[:, 0:gw], bqc[:, p:p + 1]))
                    bd.wrote(qo[p], tk)
                    bd.did_read(bank, tk)
                bd.VE(p2_qev)

            if dev == "p2q":
                def dump_q(v):
                    bd.wait(v, ("p2_qev", bd.counts["p2_qev"]))
                    for p in range(8):
                        tk = bd.inc("dmp", v.tensor_copy(out=x2r[p][:, 0:GS],
                                                         in_=qo[p][:, 0:GS]))
                        bd.wrote(x2r[p], tk)
                bd.VE(dump_q)

                def dump_qd(g):
                    for p in range(8):
                        bd.read(g, x2r[p])
                        tk = bd.inc("dmpd", g.dma_start(
                            out=dbg_d[128 * p:128 * (p + 1), 0:GS],
                            in_=x2r[p][:, 0:GS]), 16)
                    bd.wait(g, tk)
                bd.GP(dump_qd)
                return nc

            for qt in range(gw // 128):
                qti = g0 // 128 + qt
                qc = qt * 128
                mslot = maskt[qti % 2]

                def p2_mask(sy, qti=qti, mslot=mslot):
                    bd.wait_r(sy, mslot)
                    tk = bd.inc(f"p2_mld{qti % 2}", sy.dma_start(
                        out=mslot[:, 0:NSUM], in_=mask_o_d[qti, :, :]), 16)
                    bd.wrote(mslot, tk)
                bd.SY(p2_mask)

                def p2_s(t, qc=qc):
                    for pr in range(8):
                        bd.read(t, qo[pr])
                    for i in range(2):
                        bd.wait_r(t, pbank[4 + i])
                    for h in range(HEADS_O):
                        bank = pbank[4 + h // 8]
                        off = 64 * (h % 8)
                        pr = h // 2
                        i = t.matmul(bank[:, off:off + NSUM],
                                     qo[pr][:, qc:qc + 128],
                                     ksumh[h],
                                     start=True, stop=True)
                    tk = bd.inc("p2_smm", i)
                    for i in range(2):
                        bd.wrote(pbank[4 + i], tk)
                    for pr in range(8):
                        bd.did_read(qo[pr], tk)
                bd.TE(p2_s)

                if dev == "p2sa" and qt == 0:
                    def dump_sa(v):
                        bd.wait(v, ("p2_smm", bd.counts["p2_smm"]))
                        for i in range(2):
                            tk = bd.inc("dmp", v.tensor_copy(
                                out=x2r[i][:, 0:GS], in_=pbank[4 + i][:, 0:512]))
                            bd.wrote(x2r[i], tk)
                    bd.VE(dump_sa)

                    def dump_sad(g):
                        for i in range(2):
                            bd.read(g, x2r[i])
                            tk = bd.inc("dmpd", g.dma_start(
                                out=dbg_d[128 * i:128 * (i + 1), 0:GS],
                                in_=x2r[i][:, 0:GS]), 16)
                        bd.wait(g, tk)
                    bd.GP(dump_sad)
                    return nc

                def p2_soev(v, mslot=mslot):
                    bd.wait(v, ("p2_smm", bd.counts["p2_smm"]))
                    bd.read(v, mslot)
                    for i in range(2):
                        bd.wait_r(v, so_s[i])
                        m3 = mslot[:, 0:NSUM].unsqueeze(1) \
                            .broadcast_to([128, 8, NSUM])
                        tk = bd.inc("p2_soev", v.scalar_tensor_tensor(
                            out=so_s[i][:, 0:512].rearrange(
                                "p (h c) -> p h c", c=NSUM),
                            in0=pbank[4 + i][:, 0:512].rearrange(
                                "p (h c) -> p h c", c=NSUM),
                            scalar=0.125, in1=m3, op0=OP.mult, op1=OP.add))
                        bd.wrote(so_s[i], tk)
                        bd.did_read(pbank[4 + i], tk)
                        bd.did_read(mslot, tk)
                bd.VE(p2_soev)

                if dev == "p2sb" and qt == 0:
                    def dump_sb(v):
                        bd.wait(v, ("p2_soev", bd.counts["p2_soev"]))
                        for i in range(2):
                            tk = bd.inc("dmp", v.tensor_copy(
                                out=x2r[i][:, 0:GS], in_=so_s[i][:, 0:GS]))
                            bd.wrote(x2r[i], tk)
                    bd.VE(dump_sb)

                    def dump_sbd(g):
                        for i in range(2):
                            bd.read(g, x2r[i])
                            tk = bd.inc("dmpd", g.dma_start(
                                out=dbg_d[128 * i:128 * (i + 1), 0:GS],
                                in_=x2r[i][:, 0:GS]), 16)
                        bd.wait(g, tk)
                    bd.GP(dump_sbd)
                    return nc

                def p2_soft(v):
                    bd.wait_r(v, stat)
                    v.drain()
                    for i in range(2):
                        v.tensor_reduce(
                            out=stat[:, 16 * i:16 * i + 8],
                            in_=so_s[i][:, 0:512].rearrange(
                                "p (h c) -> p h c", c=NSUM),
                            axis=AX, op=OP.max, negate=True)
                        nb = stat[:, 16 * i:16 * i + 8].unsqueeze(2) \
                            .broadcast_to([128, 8, NSUM])
                        bd.wait_r(v, se_s[i])
                        v.drain()
                        tk = bd.inc("p2_se", v.tensor_tensor(
                            out=se_s[i][:, 0:512].rearrange(
                                "p (h c) -> p h c", c=NSUM),
                            in0=so_s[i][:, 0:512].rearrange(
                                "p (h c) -> p h c", c=NSUM),
                            in1=nb, op=OP.add))
                        bd.wrote(se_s[i], tk)
                        bd.did_read(so_s[i], tk)
                    stat.w = ("p2_se", bd.counts["p2_se"])
                bd.VE(p2_soft)

                def p2_exp(sc):
                    bd.wait(sc, ("p2_se", bd.counts["p2_se"]))
                    for i in range(2):
                        bd.wait_r(sc, pn_s[i])
                        tk = bd.inc("p2_exp", sc.activation(
                            out=pn_s[i][:, 0:512], in_=se_s[i][:, 0:512],
                            func=AF.Exp))
                        bd.wrote(pn_s[i], tk)
                        bd.did_read(se_s[i], tk)
                bd.SC(p2_exp)

                def p2_norm(v):
                    bd.wait(v, ("p2_exp", bd.counts["p2_exp"]))
                    for i in range(2):
                        v.tensor_reduce(
                            out=stat[:, 32 + 8 * i:40 + 8 * i],
                            in_=pn_s[i][:, 0:512].rearrange(
                                "p (h c) -> p h c", c=NSUM),
                            axis=AX, op=OP.add)
                        rz = stat[:, 48 + 8 * i:56 + 8 * i]
                        v.drain()
                        v.reciprocal(out=rz, in_=stat[:, 32 + 8 * i:40 + 8 * i])
                        rzb = rz.unsqueeze(2).broadcast_to([128, 8, NSUM])
                        v.drain()
                        tk = bd.inc("p2_pn", v.tensor_tensor(
                            out=pn_s[i][:, 0:512].rearrange(
                                "p (h c) -> p h c", c=NSUM),
                            in0=pn_s[i][:, 0:512].rearrange(
                                "p (h c) -> p h c", c=NSUM),
                            in1=rzb, op=OP.mult))
                        pn_s[i].w = tk
                bd.VE(p2_norm)

                if dev == "p2s" and qt == 0:
                    def dump_s(v):
                        bd.wait(v, ("p2_pn", bd.counts["p2_pn"]))
                        for i in range(2):
                            tk = bd.inc("dmp", v.tensor_copy(
                                out=x2r[i][:, 0:GS], in_=pn_s[i][:, 0:GS]))
                            bd.wrote(x2r[i], tk)
                        for i in range(2):
                            tk = bd.inc("dmp", v.tensor_copy(
                                out=x2r[2 + i][:, 0:GS], in_=so_s[i][:, 0:GS]))
                            bd.wrote(x2r[2 + i], tk)
                    bd.VE(dump_s)

                    def dump_sd(g):
                        for i in range(4):
                            bd.read(g, x2r[i])
                            tk = bd.inc("dmpd", g.dma_start(
                                out=dbg_d[128 * i:128 * (i + 1), 0:GS],
                                in_=x2r[i][:, 0:GS]), 16)
                        bd.wait(g, tk)
                    bd.GP(dump_sd)
                    return nc

                def p2_tr(t):
                    for i in range(2):
                        bd.read(t, pn_s[i])
                    bd.wait_r(t, pbf[0])
                    for p in range(8):
                        i = t.transpose(
                            pbf[0][:, 128 * p:128 * (p + 1)],
                            pn_s[p // 4][:, 128 * (p % 4):128 * (p % 4) + 128],
                            ident)
                    tk = bd.inc("p2_tr", i)
                    bd.wrote(pbf[0], tk)
                    for i in range(2):
                        bd.did_read(pn_s[i], tk)
                bd.TE(p2_tr)

                def p2_pcm(sc):
                    bd.read(sc, pbf[0])
                    bd.wait_r(sc, pcm)
                    tk = bd.inc("p2_pcm", sc.copy(out=pcm.ap, in_=pbf[0].ap))
                    bd.wrote(pcm, tk)
                    bd.did_read(pbf[0], tk)
                bd.SC(p2_pcm)

                if dev == "p2t" and qt == 0:
                    def dump_t(v):
                        bd.wait(v, ("p2_pcm", bd.counts["p2_pcm"]))
                        tk = bd.inc("dmp", v.tensor_copy(
                            out=x2r[0][:, 0:GS], in_=pcm[:, 0:512]))
                        bd.wrote(x2r[0], tk)
                        tk = bd.inc("dmp", v.tensor_copy(
                            out=x2r[1][:, 0:GS], in_=pcm[:, 512:1024]))
                        bd.wrote(x2r[1], tk)
                    bd.VE(dump_t)

                    def dump_td(g):
                        for i in range(2):
                            bd.read(g, x2r[i])
                            tk = bd.inc("dmpd", g.dma_start(
                                out=dbg_d[128 * i:128 * (i + 1), 0:GS],
                                in_=x2r[i][:, 0:GS]), 16)
                        bd.wait(g, tk)
                    bd.GP(dump_td)
                    return nc

                def p2_av(t, qti=qti):
                    bd.read(t, pcm)
                    if qti == 0:
                        bd.wait(t, tok_vs)
                    for bk in range(2, 6):
                        bd.wait_r(t, pbank[bk])
                    for h in range(HEADS_O):
                        vt = vlo if h % 2 == 0 else vhi
                        pr = h // 2
                        bank = pbank[2 + pr // 2]
                        base = 64 * (h % 2)
                        i = t.matmul(
                            bank[base:base + 64,
                                 128 * (pr % 2):128 * (pr % 2) + 128],
                            vt[:, 64 * h:64 * h + 64],
                            pcm[:, 128 * pr:128 * pr + 128],
                            start=True, stop=True,
                            tile_position=(0, base))
                    tk = bd.inc("p2_avmm", i)
                    for bk in range(2, 6):
                        bd.wrote(pbank[bk], tk)
                    bd.did_read(pcm, tk)
                bd.TE(p2_av)

                def p2_avev(v, qc=qc):
                    bd.wait(v, ("p2_avmm", bd.counts["p2_avmm"]))
                    for pr in range(8):
                        bank = pbank[2 + pr // 2]
                        tk = bd.inc("p2_avev", v.tensor_copy(
                            out=ofm[pr][:, qc:qc + 128],
                            in_=bank[:, 128 * (pr % 2):128 * (pr % 2) + 128]))
                        bd.did_read(bank, tk)
                        ofm[pr].w = tk
                bd.VE(p2_avev)


            if dev == "p2v":
                def dump_v(v):
                    bd.wait(v, ("p2_avev", bd.counts["p2_avev"]))
                    for p in range(8):
                        tk = bd.inc("dmp", v.tensor_copy(
                            out=x2r[p][:, 0:GS], in_=ofm[p][:, 0:GS]))
                        bd.wrote(x2r[p], tk)
                bd.VE(dump_v)

                def dump_vd(g):
                    for p in range(8):
                        bd.read(g, x2r[p])
                        tk = bd.inc("dmpd", g.dma_start(
                            out=dbg_d[128 * p:128 * (p + 1), 0:GS],
                            in_=x2r[p][:, 0:GS]), 16)
                    bd.wait(g, tk)
                bd.GP(dump_vd)
                return nc

            for dm in range(8):
                def p2_wo(t, dm=dm, gi=gi, gw=gw):
                    if dm == 0:
                        for pr in range(8):
                            bd.read(t, ofm[pr])
                        if gi == 0:
                            bd.wait(t, tok_wo[-1])
                    bank = pbank[4 + dm % 2]
                    bd.wait_r(t, bank)
                    for c in range(NCHUNK):
                        i = t.matmul(bank[:, 0:gw],
                                     wB[c][:, 128 * dm:128 * (dm + 1)],
                                     ofm[c][:, 0:gw],
                                     start=(c == 0), stop=(c == NCHUNK - 1))
                    tk = bd.inc("p2_womm", i)
                    bd.wrote(bank, tk)
                    if dm == 7:
                        for pr in range(8):
                            bd.did_read(ofm[pr], tk)
                bd.TE(p2_wo)

                def p2_x2(v, dm=dm, gw=gw, slot=slot):
                    bank = pbank[4 + dm % 2]
                    bd.read(v, bank)
                    bd.wait_r(v, x2r[dm])
                    tk = bd.inc("p2_x2", v.scalar_tensor_tensor(
                        out=x2r[dm][:, 0:gw], in0=bank[:, 0:gw],
                        scalar=boc[:, dm:dm + 1],
                        in1=xfr[dm][:, 0:gw],
                        op0=OP.add, op1=OP.add))
                    bd.wrote(x2r[dm], tk)
                    bd.did_read(bank, tk)
                    bd.did_read(xfr[dm], tk)
                bd.VE(p2_x2)

            def p2_sp(g, g0=g0, gw=gw):
                for c in range(NCHUNK):
                    bd.read(g, x2r[c])
                for c in range(NCHUNK):
                    tk = bd.inc("p2_sp", g.dma_start(
                        out=x2_sp[128 * c:128 * (c + 1), g0:g0 + gw],
                        in_=x2r[c][:, 0:gw]), 16)
                    if dev == "x2" or (dev is not None and dev.startswith("x2g")):
                        tk = bd.inc("p2_sp", g.dma_start(
                            out=dbg_d[128 * c:128 * (c + 1), g0:g0 + gw],
                            in_=x2r[c][:, 0:gw]), 16)
                for c in range(NCHUNK):
                    bd.did_read(x2r[c], tk)
            bd.GP(p2_sp)

        if dev == "x2" or (dev is not None and dev.startswith("x2g")):
            def fin(g):
                bd.wait(g, ("p2_sp", bd.counts["p2_sp"]))
            bd.GP(fin)
            return nc

        LAYERS = [
            dict(l=0, kv0=R_KV1[0], att0=R_ATT1[0], ffn0=R_ATT1[0],
                 mask_d=mask_i_d[0]),
            dict(l=1, kv0=R_KV2[0], att0=R_MID2[0], ffn0=R_FFN2[0],
                 mask_d=mask_i_d[1]),
        ]

        def init_tails(v):
            for c in range(NCHUNK):
                v.memset(xgb[c][:, 0:1], 0.0)
                v.memset(xgb[c][:, 513:514], 0.0)
                tk = bd.inc("tails", v.memset(fsp[c][:, 0:1], 0.0))
                xgb[c].rs.append(tk)
                fsp[c].rs.append(tk)
        bd.VE(init_tails)
        tok_tails = ("tails", bd.counts["tails"])


        for L in LAYERS:
            l = L["l"]
            kv0, att0, ffn0 = L["kv0"], L["att0"], L["ffn0"]
            tag = f"L{l}"
            kvg = groups((kv0 // GS * GS, WIN))
            attg = groups((att0, WIN))
            ffng = groups((ffn0, WIN))

            def wqkv_ld(g, l=l, tag=tag):
                for c in range(NCHUNK):
                    bd.wait_r(g, wA[c])
                    tk = bd.inc(f"wqk{tag}", g.dma_start(
                        out=wA[c][:, 0:1024],
                        in_=wqkv_d[l][128 * c:128 * (c + 1), 0:1024]), 16)
                    bd.wrote(wA[c], tk)
                    bd.wait_r(g, wB[c])
                    tk = bd.inc(f"wqk{tag}", g.dma_start(
                        out=wB[c][:, 0:512],
                        in_=wqkv_d[l][128 * c:128 * (c + 1), 1024:1536]), 16)
                    bd.wrote(wB[c], tk)
            bd.GP(wqkv_ld)
            tok_wqkv = (f"wqk{tag}", bd.counts[f"wqk{tag}"])

            for gi, (g0, gw) in enumerate(kvg):
                slot = gi % 2
                r0 = max(g0, kv0)
                a0, aw = r0 - g0, g0 + gw - r0
                qr0 = max(g0, att0)
                qa0, qaw = qr0 - g0, g0 + gw - qr0
                base = slot * 513 + 1

                if l == 0:
                    def p3_ld(sy, g0=g0, gw=gw, slot=slot, gi=gi):
                        if gi == 0:
                            bd.wait(sy, ("p2_sp", bd.counts.get("p2_sp", 0)))
                        for c in range(NCHUNK):
                            bd.wait_r(sy, xfr[c])
                        for c in range(NCHUNK):
                            tk = bd.inc(f"xld{slot}", sy.dma_start(
                                out=xfr[c][:, 0:gw],
                                in_=x2_sp[128 * c:128 * (c + 1), g0:g0 + gw]), 16)
                        for c in range(NCHUNK):
                            bd.wrote(xfr[c], tk)
                    bd.SY(p3_ld)
                    xf32 = [(xfr[c], xfr[c][:, 0:gw])
                            for c in range(NCHUNK)]
                    xnout = [(xgb[c], xgb[c][:, base:base + gw])
                             for c in range(NCHUNK)]
                    xf32out = [(xfr[c], xfr[c][:, 0:gw])
                               for c in range(NCHUNK)]
                    src_tok = emit_ln(gw, xf32, xnout, out_f32=xf32out)

                    def p3_sp(g, g0=g0, gw=gw, slot=slot, src_tok=src_tok):
                        bd.wait(g, src_tok)
                        for c in range(NCHUNK):
                            tk = bd.inc(f"xfsp{slot}", g.dma_start(
                                out=xffn_sp[128 * c:128 * (c + 1), g0:g0 + gw],
                                in_=xfr[c][:, 0:gw]), 16)
                        for c in range(NCHUNK):
                            xfr[c].rs.append(tk)
                    bd.GP(p3_sp)

                    def tails_c(v, slot=slot, gw=gw, src_tok=src_tok):
                        bd.wait(v, src_tok)
                        if bd.counts.get("tails", 0):
                            bd.wait(v, tok_tails)
                        nslot = 1 - slot
                        for c in range(NCHUNK):
                            bd.wait_r(v, xgb[c])
                            last = v.tensor_copy(
                                out=xgb[c][:, nslot * 513:nslot * 513 + 1],
                                in_=xgb[c][:, slot * 513 + gw:slot * 513 + gw + 1])
                        tk = bd.inc(f"tl{slot}", last)
                        for c in range(NCHUNK):
                            xgb[c].w = tk
                    bd.VE(tails_c)

                    if g0 <= HALO - 1 < g0 + gw:
                        def bz(v, slot=slot, g0=g0, src_tok=src_tok):
                            bd.wait(v, src_tok)
                            col = slot * 513 + 1 + (HALO - 1 - g0)
                            for c in range(NCHUNK):
                                last = v.tensor_scalar_mul(
                                    xgb[c][:, col:col + 1],
                                    xgb[c][:, col:col + 1], bcolb[:, 0:1])
                            tk = bd.inc(f"bz{slot}", last)
                            for c in range(NCHUNK):
                                xgb[c].w = tk
                        bd.VE(bz)
                    src_tok = (f"tl{slot}", bd.counts[f"tl{slot}"])
                else:
                    def p5_ld(sy, gi=gi, g0=g0, gw=gw, slot=slot):
                        if gi == 0:
                            bd.wait(sy, ("l1sp", bd.counts.get("l1sp", 0)))
                        lo1 = max(g0 - 1, R_ATT1[0])
                        ofs = slot * 513 + 1 - (g0 - lo1)
                        for c in range(NCHUNK):
                            bd.wait_r(sy, xgb[c])
                        for c in range(NCHUNK):
                            tk = bd.inc(f"gld{slot}", sy.dma_start(
                                out=xgb[c][:, ofs:slot * 513 + 1 + gw],
                                in_=l1ob_sp[128 * c:128 * (c + 1),
                                            lo1 - R_ATT1[0]:g0 + gw - R_ATT1[0]]), 16)
                        for c in range(NCHUNK):
                            bd.wrote(xgb[c], tk)
                    bd.SY(p5_ld)
                    if gi == 0:
                        def p5_ms(v, slot=slot):
                            bd.wait(v, (f"gld{slot}",
                                        bd.counts.get(f"gld{slot}", 0)))
                            for c in range(NCHUNK):
                                last = v.memset(
                                    xgb[c][:, slot * 513:slot * 513 + 129], 0.0)
                            tk = bd.inc(f"gld{slot}ms", last)
                            for c in range(NCHUNK):
                                xgb[c].w = tk
                        bd.VE(p5_ms)
                        src_tok = (f"gld{slot}ms", bd.counts[f"gld{slot}ms"])
                    else:
                        src_tok = (f"gld{slot}", bd.counts[f"gld{slot}"])

                def rhs_ap(c, a, b, slot=slot):
                    bb = slot * 513 + 1
                    if c < 4:
                        return xgb[c][:, bb + a:bb + b]
                    return xgb[c][:, bb + a - 1:bb + b - 1]

                def rot_ld(sy, g0=g0, gw=gw, slot=slot, l=l):
                    tabs = ["csq", "sq", "csk", "sk"]
                    for ti, nm in enumerate(tabs):
                        hb = wB[4 * slot + ti]
                        bd.wait_r(sy, hb)
                        tk = bd.inc(f"rt{slot}", sy.dma_start(
                            out=hb[:, 512:512 + gw],
                            in_=rot_d[nm][:, g0:g0 + gw]), 16)
                        hb.w = tk
                bd.SY(rot_ld)
                tok_rot = (f"rt{slot}", bd.counts[f"rt{slot}"])

                jobs = [("k", p) for p in range(4)]
                if qaw > 0:
                    jobs += [("q", p) for p in range(4)]
                nt = (aw + 127) // 128
                jobs += [("v", tt) for tt in range(nt)]
                for ji, (kind, p) in enumerate(jobs):
                    bank = pbank[[4, 5, 0, 1][ji % 4]]

                    def mm_f(t, kind=kind, p=p, bank=bank, gi=gi,
                             src_tok=src_tok, a0=a0, aw=aw, qa0=qa0, qaw=qaw,
                             rhs_ap=rhs_ap, ji=ji):
                        if ji == 0:
                            if gi == 0:
                                bd.wait(t, tok_wqkv)
                            bd.wait(t, src_tok)
                            if l == 0:
                                bd.wait(t, (f"bz{slot}",
                                            bd.counts.get(f"bz{slot}", 0)))
                        bd.wait_r(t, bank)
                        if kind == "k":
                            for c in range(NCHUNK):
                                i = t.matmul(bank[:, 0:aw],
                                             wA[c][:, 512 + 128 * p:640 + 128 * p],
                                             rhs_ap(c, a0, a0 + aw),
                                             start=(c == 0), stop=(c == NCHUNK - 1))
                        elif kind == "q":
                            for c in range(NCHUNK):
                                i = t.matmul(bank[:, 0:qaw],
                                             wA[c][:, 128 * p:128 * (p + 1)],
                                             rhs_ap(c, qa0, qa0 + qaw),
                                             start=(c == 0), stop=(c == NCHUNK - 1))
                        else:
                            tt = p
                            tw = min(128, aw - tt * 128)
                            for c in range(NCHUNK):
                                i = t.matmul(bank[0:tw, 0:512],
                                             rhs_ap(c, a0 + tt * 128,
                                                    a0 + tt * 128 + tw),
                                             wB[c][:, 0:512],
                                             start=(c == 0), stop=(c == NCHUNK - 1))
                        tk = bd.inc(f"pj{tag}", i)
                        bd.wrote(bank, tk)
                        for c in range(NCHUNK):
                            bd.did_read(xgb[c], tk)
                        if kind == "v":
                            for c in range(NCHUNK):
                                bd.did_read(wB[c], tk)
                        else:
                            for c in range(NCHUNK):
                                bd.did_read(wA[c], tk)
                    bd.TE(mm_f)

                    if kind == "v":
                        def vev_f(v, p=p, bank=bank, a0=a0, aw=aw, r0=r0,
                                  kv0=kv0):
                            tt = p
                            tw = min(128, aw - tt * 128)
                            vi = (r0 + tt * 128 - kv0) // 128
                            bd.read(v, bank)
                            bd.wait_r(v, vI[vi])
                            tk = bd.inc("vev", v.tensor_copy(
                                out=vI[vi][0:tw if tw < 128 else 128, :],
                                in_=bank[0:tw, 0:512]))
                            bd.wrote(vI[vi], tk)
                            bd.did_read(bank, tk)
                        bd.VE(vev_f)
                        continue

                    dst_is_q = (kind == "q")
                    w_ = qaw if dst_is_q else aw
                    col0 = (qr0 - att0) if dst_is_q else (r0 - kv0)
                    dest = qhat[p] if dst_is_q else khat[p]
                    tb_cs = wB[4 * slot + (0 if dst_is_q else 2)]
                    tb_s = wB[4 * slot + (1 if dst_is_q else 3)]
                    t_of = 512 + (qa0 if dst_is_q else a0)
                    tabs_cs = tb_cs[:, t_of:t_of + w_]
                    tabs_s = tb_s[:, t_of:t_of + w_]
                    scol = qscol[l] if dst_is_q else kscol[l]

                    def ev_f(sc, bank=bank, w_=w_, ji=ji):
                        bd.read(sc, bank)
                        rb = qn_s[ji % 2]
                        bd.wait_r(sc, rb)
                        tk = bd.inc("rawev", sc.copy(out=rb[:, 0:w_],
                                                     in_=bank[:, 0:w_]))
                        bd.wrote(rb, tk)
                        bd.did_read(bank, tk)
                    bd.SC(ev_f)

                    def sq_f(sc, w_=w_, ji=ji):
                        rb = qn_s[ji % 2]
                        sq = t2_s[ji % 2]
                        bd.read(sc, rb)
                        bd.wait_r(sc, sq)
                        tk = bd.inc("rsq", sc.activation(
                            out=sq[:, 0:w_], in_=rb[:, 0:w_], func=AF.Square))
                        bd.wrote(sq, tk)
                        rb.rs.append(tk)
                    bd.SC(sq_f)

                    def n2_f(t, w_=w_, ji=ji):
                        sq = t2_s[ji % 2]
                        bd.read(t, sq)
                        bd.wait_r(t, pbank[2])
                        tk = bd.inc("n2", t.matmul(
                            pbank[2][0:2, 0:w_], hsel2, sq[:, 0:w_],
                            start=True, stop=True))
                        bd.wrote(pbank[2], tk)
                        bd.did_read(sq, tk)
                    bd.TE(n2_f)

                    def rstd_f2(sc, w_=w_):
                        bd.read(sc, pbank[2])
                        bd.wait(sc, ("rbc2", bd.counts.get("rbc2", 0)))
                        tk = bd.inc("rs1", sc.activation(
                            out=rows[0:2, RO_LG:RO_LG + w_],
                            in_=pbank[2][0:2, 0:w_],
                            func=AF.Ln, bias=eps18[0:2, :]))
                        bd.wait(sc, tk)
                        tk2 = bd.inc("rs2", sc.activation(
                            out=rsb[0:2, 0:w_],
                            in_=rows[0:2, RO_LG:RO_LG + w_],
                            func=AF.Exp, scale=-0.5))
                        bd.did_read(pbank[2], tk2)
                    bd.SC(rstd_f2)

                    def rbc_f(t, w_=w_):
                        bd.wait(t, ("rs2", bd.counts["rs2"]))
                        bd.wait_r(t, pbank[3])
                        tk = bd.inc("rbc2", t.matmul(
                            pbank[3][:, 0:w_], psel2, rsb[0:2, 0:w_],
                            start=True, stop=True))
                        bd.wrote(pbank[3], tk)
                    bd.TE(rbc_f)

                    def rot_f(v, w_=w_, ji=ji, scol=scol, tabs_cs=tabs_cs,
                              tabs_s=tabs_s, dest=dest, col0=col0,
                              tok_rot=tok_rot, tb_cs=tb_cs, tb_s=tb_s):
                        rb = qn_s[ji % 2]
                        qn = t2_s[ji % 2]
                        t2 = t3_s[ji % 2]
                        bd.read(v, rb)
                        bd.read(v, pbank[3])
                        bd.wait_r(v, qn)
                        bd.wait(v, tok_rot)
                        v.scalar_tensor_tensor(
                            out=qn[:, 0:w_], in0=rb[:, 0:w_], scalar=scol[:, 0:1],
                            in1=pbank[3][:, 0:w_], op0=OP.mult, op1=OP.mult)
                        v.drain()
                        bd.wait_r(v, t2)
                        v.tensor_tensor(out=t2[0:64, 0:w_], in0=qn[64:128, 0:w_],
                                        in1=tabs_s[64:128, :], op=OP.mult)
                        v.tensor_tensor(out=t2[64:128, 0:w_], in0=qn[0:64, 0:w_],
                                        in1=tabs_s[0:64, :], op=OP.mult)
                        v.drain()
                        v.tensor_tensor(out=qn[:, 0:w_], in0=qn[:, 0:w_],
                                        in1=tabs_cs, op=OP.mult)
                        v.drain()
                        bd.wait_r(v, dest)
                        tk = bd.inc("rotd", v.tensor_tensor(
                            out=dest[:, col0:col0 + w_], in0=qn[:, 0:w_],
                            in1=t2[0:128, 0:w_], op=OP.add))
                        dest.w = tk
                        bd.wrote(qn, tk)
                        bd.wrote(t2, tk)
                        rb.rs.append(tk)
                        bd.did_read(pbank[3], tk)
                        bd.did_read(tb_cs, tk)
                        bd.did_read(tb_s, tk)
                    bd.VE(rot_f)

            if l == 0 and dev == "xffn":
                def dmp_xf(g):
                    bd.wait(g, ("xfsp0", bd.counts.get("xfsp0", 0)))
                    bd.wait(g, ("xfsp1", bd.counts.get("xfsp1", 0)))
                    for c in range(NCHUNK):
                        tk = bd.inc("dmpd", g.dma_start(
                            out=dbg_d[128 * c:128 * (c + 1), :],
                            in_=xffn_sp[128 * c:128 * (c + 1), :]), 16)
                    bd.wait(g, tk)
                bd.GP(dmp_xf)
                return nc

            def wo_ld(g, l=l, tag=tag):
                for c in range(4):
                    bd.wait_r(g, wB[c])
                    tk = bd.inc(f"wol{tag}", g.dma_start(
                        out=wB[c][:, 0:1024],
                        in_=wo2_d[l][128 * c:128 * (c + 1), 0:1024]), 16)
                    bd.wrote(wB[c], tk)
            bd.GP(wo_ld)
            tok_wo2 = (f"wol{tag}", bd.counts[f"wol{tag}"])

            for gi, (g0, gw) in enumerate(attg):
                slot = gi % 2
                def res_ld(sy, g0=g0, gw=gw, slot=slot, l=l, gi=gi):
                    if gi == 0:
                        if l == 0:
                            bd.wait(sy, ("xfsp0", bd.counts.get("xfsp0", 0)))
                            bd.wait(sy, ("xfsp1", bd.counts.get("xfsp1", 0)))
                        else:
                            bd.wait(sy, ("l1sp", bd.counts.get("l1sp", 0)))
                    for c in range(NCHUNK):
                        bd.wait_r(sy, xfr[c])
                    src = xffn_sp if l == 0 else l1o_sp
                    o = 0 if l == 0 else R_ATT1[0]
                    for c in range(NCHUNK):
                        tk = bd.inc(f"xld{slot}", sy.dma_start(
                            out=xfr[c][:, 0:gw],
                            in_=src[128 * c:128 * (c + 1), g0 - o:g0 - o + gw]), 16)
                    for c in range(NCHUNK):
                        bd.wrote(xfr[c], tk)
                bd.SY(res_ld)

                for qt in range(gw // 128):
                    w0 = g0 + 128 * qt
                    qti = (w0 - att0) // 128
                    qc = w0 - att0
                    mslot = maskt[qti % 2]

                    def i_mask(sy, qti=qti, mslot=mslot, L=L):
                        bd.wait_r(sy, mslot)
                        tk = bd.inc(f"p2_mld{qti % 2}", sy.dma_start(
                            out=mslot[:, 0:640], in_=L["mask_d"][qti, :, :]), 16)
                        bd.wrote(mslot, tk)
                    bd.SY(i_mask)

                    for h in range(HEADS_I):
                        pr = h // 2
                        par = h % 2
                        b0 = w0 - 512 - kv0
                        kb = kband[h % 2]
                        sb0, sb1 = (pbank[0], pbank[1]) if h % 2 == 0 else \
                                   (pbank[2], pbank[3])

                        def kb_f(v, pr=pr, par=par, b0=b0, kb=kb):
                            bd.read(v, khat[pr])
                            bd.wait_r(v, kb)
                            tk = bd.inc("kbs", v.tensor_scalar_mul(
                                kb[:, 0:640], khat[pr][:, b0:b0 + 640],
                                selperm[:, par:par + 1]))
                            bd.wrote(kb, tk)
                            khat[pr].rs.append(tk)
                        bd.VE(kb_f)

                        def s_f(t, pr=pr, qc=qc, kb=kb, sb0=sb0, sb1=sb1,
                                gi=gi, h=h):
                            if gi == 0 and qt == 0 and h == 0:
                                pass
                            bd.read(t, qhat[pr])
                            bd.read(t, kb)
                            bd.wait_r(t, sb0)
                            bd.wait_r(t, sb1)
                            t.matmul(sb0[:, 0:512], qhat[pr][:, qc:qc + 128],
                                     kb[:, 0:512], start=True, stop=True)
                            i = t.matmul(sb1[:, 0:128], qhat[pr][:, qc:qc + 128],
                                         kb[:, 512:640], start=True, stop=True)
                            tk = bd.inc("smm", i)
                            bd.wrote(sb0, tk)
                            bd.wrote(sb1, tk)
                            kb.rs.append(tk)
                            qhat[pr].rs.append(tk)
                        bd.TE(s_f)

                        def soev_f(v, mslot=mslot, h=h, sb0=sb0, sb1=sb1):
                            so = so_s[h % 2]
                            bd.read(v, sb0)
                            bd.read(v, sb1)
                            bd.read(v, mslot)
                            bd.wait_r(v, so)
                            v.scalar_tensor_tensor(
                                out=so[:, 0:512], in0=sb0[:, 0:512],
                                scalar=QK_SCALE, in1=mslot[:, 0:512],
                                op0=OP.mult, op1=OP.add)
                            tk = bd.inc("soe", v.scalar_tensor_tensor(
                                out=so[:, 512:640], in0=sb1[:, 0:128],
                                scalar=QK_SCALE, in1=mslot[:, 512:640],
                                op0=OP.mult, op1=OP.add))
                            bd.wrote(so, tk)
                            bd.did_read(sb0, tk)
                            bd.did_read(sb1, tk)
                            bd.did_read(mslot, tk)
                            v.drain()
                            tk = bd.inc("soe", v.tensor_reduce(
                                out=stat[:, h:h + 1], in_=so[:, 0:640],
                                axis=AX, op=OP.max, negate=True))
                            stat.w = tk
                        bd.VE(soev_f)

                        def exp_f(sc, h=h):
                            so = so_s[h % 2]
                            pn = pn_s[h % 2]
                            bd.read(sc, so)
                            bd.wait(sc, ("soe", bd.counts["soe"]))
                            bd.wait_r(sc, pn)
                            tk = bd.inc("expf", sc.activation(
                                out=pn[:, 0:640], in_=so[:, 0:640], func=AF.Exp,
                                bias=stat[:, h:h + 1],
                                accum_out=zcol[:, h:h + 1]))
                            bd.wrote(pn, tk)
                            zcol.w = tk
                            so.rs.append(tk)
                        bd.SC(exp_f)

                        def norm_f(v, h=h):
                            pn = pn_s[h % 2]
                            bd.read(v, pn)
                            bd.read(v, zcol)
                            v.reciprocal(out=stat[:, 32 + h:33 + h],
                                         in_=zcol[:, h:h + 1])
                            v.drain()
                            tk = bd.inc("pnn", v.tensor_scalar_mul(
                                pn[:, 0:640], pn[:, 0:640],
                                stat[:, 32 + h:33 + h]))
                            pn.w = tk
                        bd.VE(norm_f)

                        def tr_f(t, h=h):
                            pn = pn_s[h % 2]
                            bd.read(t, pn)
                            bd.wait_r(t, pbf[0])
                            for j in range(5):
                                i = t.transpose(
                                    pbf[0][:, 128 * j:128 * (j + 1)],
                                    pn[:, 128 * j:128 * (j + 1)], ident)
                            tk = bd.inc("trf", i)
                            bd.wrote(pbf[0], tk)
                            pn.rs.append(tk)
                        bd.TE(tr_f)

                        def pcm_f(sc):
                            bd.read(sc, pbf[0])
                            bd.wait_r(sc, pcm)
                            tk = bd.inc("pcmf", sc.copy(out=pcm[:, 0:640],
                                                        in_=pbf[0][:, 0:640]))
                            bd.wrote(pcm, tk)
                            bd.did_read(pbf[0], tk)
                        bd.SC(pcm_f)

                        def av_f(t, h=h, w0=w0, kv0=kv0):
                            bd.read(t, pcm)
                            bank = pbank[4 + h // 4]
                            if h % 4 == 0:
                                bd.wait_r(t, bank)
                            for j in range(5):
                                vi = (w0 - 512 + 128 * j - kv0) // 128
                                bd.read(t, vI[vi])
                                i = t.matmul(
                                    bank[0:64, 128 * (h % 4):128 * (h % 4) + 128],
                                    vI[vi][:, 64 * h:64 * (h + 1)],
                                    pcm[:, 128 * j:128 * (j + 1)],
                                    start=(j == 0), stop=(j == 4))
                            tk = bd.inc("avf", i)
                            bd.wrote(bank, tk) if h % 4 == 3 else bank.rs.append(tk)
                            bd.did_read(pcm, tk)
                            for j in range(5):
                                vi = (w0 - 512 + 128 * j - kv0) // 128
                                vI[vi].rs.append(tk)
                        bd.TE(av_f)

                        if h % 4 == 3:
                            def avev_f(v, h=h, qt=qt):
                                bank = pbank[4 + h // 4]
                                bd.read(v, bank)
                                for hh in range(h - 3, h + 1):
                                    prr = hh // 2
                                    ro = 64 * (hh % 2)
                                    bd.wait_r(v, oI[prr])
                                    tk = bd.inc("avev", v.tensor_copy(
                                        out=oI[prr][ro:ro + 64,
                                                    128 * qt:128 * qt + 128],
                                        in_=bank[0:64,
                                                 128 * (hh % 4):128 * (hh % 4) + 128]))
                                    oI[prr].w = tk
                                bd.did_read(bank, tk)
                            bd.VE(avev_f)

                for dm in range(8):
                    def wo_f(t, dm=dm, gi=gi, gw=gw, tok_wo2=tok_wo2):
                        if dm == 0:
                            if gi == 0:
                                bd.wait(t, tok_wo2)
                            for pr in range(4):
                                bd.read(t, oI[pr])
                        bank = pbank[4 + dm % 2]
                        bd.wait_r(t, bank)
                        for c in range(4):
                            i = t.matmul(bank[:, 0:gw],
                                         wB[c][:, 128 * dm:128 * (dm + 1)],
                                         oI[c][:, 0:gw],
                                         start=(c == 0), stop=(c == 3))
                        tk = bd.inc("womm2", i)
                        bd.wrote(bank, tk)
                        if dm == 7:
                            for pr in range(4):
                                oI[pr].rs.append(tk)
                    bd.TE(wo_f)

                    def mid_f(v, dm=dm, gw=gw, slot=slot):
                        bank = pbank[4 + dm % 2]
                        bd.read(v, bank)
                        bd.read(v, xfr[dm])
                        bd.wait_r(v, fsp[dm])
                        tk = bd.inc("midf", v.scalar_tensor_tensor(
                            out=fsp[dm][:, 1:1 + gw], in0=bank[:, 0:gw],
                            scalar=1.0,
                            in1=xfr[dm][:, 0:gw],
                            op0=OP.mult, op1=OP.add))
                        fsp[dm].w = tk
                        bd.did_read(bank, tk)
                        bd.did_read(xfr[dm], tk)
                    bd.VE(mid_f)

                if g0 <= HALO - 1 < g0 + gw:
                    def bzm(v, g0=g0):
                        col = 1 + (HALO - 1 - g0)
                        for c in range(NCHUNK):
                            bd.read(v, fsp[c])
                            last = v.tensor_scalar_mul(
                                fsp[c][:, col:col + 1], fsp[c][:, col:col + 1],
                                bcolb[:, 0:1])
                        tk = bd.inc("bzm", last)
                        for c in range(NCHUNK):
                            fsp[c].w = tk
                    bd.VE(bzm)

                def mid_sp_f(g, g0=g0, gw=gw, att0=att0):
                    for c in range(NCHUNK):
                        bd.read(g, fsp[c])
                    for c in range(NCHUNK):
                        tk = bd.inc("midsp", g.dma_start(
                            out=mid_sp[128 * c:128 * (c + 1),
                                       g0 - att0:g0 - att0 + gw],
                            in_=fsp[c][:, 1:1 + gw]), 16)
                    for c in range(NCHUNK):
                        fsp[c].rs.append(tk)
                bd.GP(mid_sp_f)

                xf32 = [(fsp[c], fsp[c][:, 1:1 + gw] if c < 4
                         else fsp[c][:, 0:gw]) for c in range(NCHUNK)]
                xnout = [(xgb[c], xgb[c][:, slot * 513 + 1:slot * 513 + 1 + gw])
                         for c in range(NCHUNK)]
                mid_tok = emit_ln(gw, xf32, xnout, affine_l=l)

                def xn_sp_f(g, g0=g0, gw=gw, att0=att0, slot=slot,
                            mid_tok=mid_tok):
                    bd.wait(g, mid_tok)
                    for c in range(NCHUNK):
                        tk = bd.inc("xnsp", g.dma_start(
                            out=xnmid_sp[128 * c:128 * (c + 1),
                                         g0 - att0:g0 - att0 + gw],
                            in_=xgb[c][:, slot * 513 + 1:slot * 513 + 1 + gw]), 16)
                    for c in range(NCHUNK):
                        xgb[c].rs.append(tk)
                bd.GP(xn_sp_f)

                def tail_f(v, gw=gw, mid_tok=mid_tok):
                    bd.wait(v, mid_tok)
                    for c in range(NCHUNK):
                        last = v.tensor_copy(out=fsp[c][:, 0:1],
                                             in_=fsp[c][:, gw:gw + 1])
                    tk = bd.inc("ftl", last)
                    for c in range(NCHUNK):
                        fsp[c].w = tk
                bd.VE(tail_f)

            def w2home(j):
                if j < 8:
                    return qhat[j // 2][:, 1024 * (j % 2):1024 * (j % 2) + 1024], \
                        qhat[j // 2]
                if j < 20:
                    jj = j - 8
                    return khat[jj // 3][:, 1024 * (jj % 3):1024 * (jj % 3) + 1024], \
                        khat[jj // 3]
                jj = j - 20
                ap = so_s[jj].ap.bitcast(BF16)
                return ap[:, 0:1024], so_s[jj]

            def w2_ld(g, l=l, tag=tag):
                homes = []
                for j in range(N_JT):
                    ap, home = w2home(j)
                    if home not in homes:
                        bd.wait_r(g, home)
                        homes.append(home)
                for j in range(N_JT):
                    wdt = W1_TILES[j]
                    ap, home = w2home(j)
                    tk = bd.inc(f"w2l{tag}", g.dma_start(
                        out=ap[0:wdt, :],
                        in_=w2_d[l][128 * j:128 * j + wdt, :]), 16)
                for home in homes:
                    bd.wrote(home, tk)
            bd.GP(w2_ld)
            tok_w2 = (f"w2l{tag}", bd.counts[f"w2l{tag}"])

            for gi, (g0, gw) in enumerate(ffng):
                slot = gi % 2

                def ffn_ld(sy, g0=g0, gw=gw, slot=slot, ffn0=ffn0, att0=att0,
                           l=l, gi=gi):
                    if gi == 0:
                        bd.wait(sy, ("xnsp", bd.counts.get("xnsp", 0)))
                        bd.wait(sy, ("midsp", bd.counts.get("midsp", 0)))
                    for c in range(NCHUNK):
                        bd.wait_r(sy, xgb[c])
                    for c in range(NCHUNK):
                        tk = bd.inc(f"gld{slot}", sy.dma_start(
                            out=xgb[c][:, slot * 513:slot * 513 + gw],
                            in_=xnmid_sp[128 * c:128 * (c + 1),
                                         g0 - att0:g0 - att0 + gw]), 16)
                    for c in range(NCHUNK):
                        bd.wrote(xgb[c], tk)
                    for c in range(NCHUNK):
                        bd.wait_r(sy, xfr[c])
                    for c in range(NCHUNK):
                        tk = bd.inc(f"xld0", sy.dma_start(
                            out=xfr[c][:, 0:gw],
                            in_=mid_sp[128 * c:128 * (c + 1),
                                       g0 - att0:g0 - att0 + gw]), 16)
                    for c in range(NCHUNK):
                        bd.wrote(xfr[c], tk)
                bd.SY(ffn_ld)
                tok_ffnld = (f"gld{slot}", bd.counts[f"gld{slot}"])
                tok_ffnx = ("xld0", bd.counts["xld0"])

                for j in range(N_JT):
                    wdt = W1_TILES[j]
                    sa = wA[2 * (j % 2)]
                    sg = wA[2 * (j % 2) + 1]

                    def w1_ld2(g, j=j, wdt=wdt, sa=sa, sg=sg, l=l):
                        off = sum(2 * W1_TILES[jj] for jj in range(j))
                        for ag, ws in ((0, sa), (1, sg)):
                            bd.wait_r(g, ws)
                            for c in range(NCHUNK):
                                tk = bd.inc(f"w1l{j % 2}{ag}", g.dma_start(
                                    out=ws[:, wdt * c:wdt * (c + 1)],
                                    in_=w1i_d[l][128 * c:128 * (c + 1),
                                                 off + wdt * ag:
                                                 off + wdt * (ag + 1)]), 16)
                            bd.wrote(ws, tk)
                    bd.GP(w1_ld2)

                    def w1_mm(t, j=j, wdt=wdt, sa=sa, sg=sg, gw=gw, slot=slot,
                              gi=gi, tok_ffnld=tok_ffnld, tok_w2=tok_w2):
                        if j == 0 and gi == 0:
                            bd.wait(t, tok_w2)
                        if j == 0:
                            bd.wait(t, tok_ffnld)
                        for ag, ws in ((0, sa), (1, sg)):
                            bd.read(t, ws)
                            bank = pbank[(4 + ag) if j % 2 == 0 else (2 + ag)]
                            bd.wait_r(t, bank)
                            for c in range(NCHUNK):
                                i = t.matmul(
                                    bank[0:wdt, 0:gw],
                                    ws[:, wdt * c:wdt * (c + 1)],
                                    xgb[c][:, slot * 513:slot * 513 + gw],
                                    start=(c == 0), stop=(c == NCHUNK - 1))
                            tk = bd.inc("w1mm", i)
                            bd.wrote(bank, tk)
                            ws.rs.append(tk)
                        if j == N_JT - 1:
                            for c in range(NCHUNK):
                                xgb[c].rs.append(tk)
                    bd.TE(w1_mm)

                    def gelu_f(sc, j=j, wdt=wdt, gw=gw):
                        bank = pbank[5 if j % 2 == 0 else 3]
                        bd.read(sc, bank)
                        gl = xsq[j % 2]
                        bd.wait_r(sc, gl)
                        tk = bd.inc("gel", sc.activation(
                            out=gl[0:wdt, 0:gw], in_=bank[0:wdt, 0:gw],
                            func=AF.Gelu, bias=b1col[l][0:wdt, 2 * j + 1:2 * j + 2]))
                        bd.wrote(gl, tk)
                        bd.did_read(bank, tk)
                    bd.SC(gelu_f)

                    def m_f(v, j=j, wdt=wdt, gw=gw, l=l):
                        bank = pbank[4 if j % 2 == 0 else 2]
                        gl = xsq[j % 2]
                        bd.read(v, bank)
                        bd.read(v, gl)
                        bd.wait_r(v, vI[j])
                        tk = bd.inc("mf", v.scalar_tensor_tensor(
                            out=vI[j][0:wdt, 0:gw], in0=bank[0:wdt, 0:gw],
                            scalar=b1col[l][0:wdt, 2 * j:2 * j + 1],
                            in1=gl[0:wdt, 0:gw], op0=OP.add, op1=OP.mult))
                        bd.wrote(vI[j], tk)
                        bd.did_read(bank, tk)
                        gl.rs.append(tk)
                    bd.VE(m_f)

                accs = [pbank[i] for i in range(6)] + [pbfF[0], pbfF[1]]

                def w2_mm(t, gw=gw, l=l):
                    for i in range(2):
                        bd.read(t, pbf[i])
                        bd.wait_r(t, pbf[i])
                    for dm in range(8):
                        bd.wait_r(t, accs[dm])
                    for j in range(N_JT):
                        wdt = W1_TILES[j]
                        ap, home = w2home(j)
                        bd.read(t, home)
                        bd.read(t, vI[j])
                        for dm in range(8):
                            i = t.matmul(
                                accs[dm][:, 0:gw] if dm < 6 else accs[dm][:, 0:gw],
                                ap[0:wdt, 128 * dm:128 * (dm + 1)],
                                vI[j][0:wdt, 0:gw],
                                start=(j == 0), stop=(j == N_JT - 1))
                        tk = bd.inc("w2mm", i)
                        vI[j].rs.append(tk)
                        home.rs.append(tk)
                    for dm in range(8):
                        bd.wrote(accs[dm], ("w2mm", bd.counts["w2mm"]))
                bd.TE(w2_mm)

                for dm in range(8):
                    def out_f(v, dm=dm, g0=g0, gw=gw, l=l,
                              tok_ffnx=tok_ffnx):
                        acc = accs[dm]
                        bd.read(v, acc)
                        bd.wait(v, tok_ffnx)
                        bd.wait_r(v, fsp[dm])
                        tk = bd.inc("outf", v.scalar_tensor_tensor(
                            out=fsp[dm][:, 1:1 + gw], in0=acc[:, 0:gw],
                            scalar=b2c[l][:, dm:dm + 1],
                            in1=xfr[dm][:, 0:gw], op0=OP.add, op1=OP.add))
                        fsp[dm].w = tk
                        bd.did_read(acc, tk)
                        if dm >= 6:
                            bd.did_read(pbf[dm - 6], tk)
                        bd.did_read(xfr[dm], tk)
                    bd.VE(out_f)

                if l == 0 and g0 <= HALO - 1 < g0 + gw:
                    def bzo(v, g0=g0):
                        col = 1 + (HALO - 1 - g0)
                        for c in range(NCHUNK):
                            bd.read(v, fsp[c])
                            last = v.tensor_scalar_mul(
                                fsp[c][:, col:col + 1], fsp[c][:, col:col + 1],
                                bcolb[:, 0:1])
                        tk = bd.inc("bzo", last)
                        for c in range(NCHUNK):
                            fsp[c].w = tk
                    bd.VE(bzo)

                def out_sp(g, g0=g0, gw=gw, l=l, ffn0=ffn0):
                    for c in range(NCHUNK):
                        bd.read(g, fsp[c])
                    for c in range(NCHUNK):
                        if l == 0:
                            tk = bd.inc("l1sp", g.dma_start(
                                out=l1o_sp[128 * c:128 * (c + 1),
                                           g0 - ffn0:g0 - ffn0 + gw],
                                in_=fsp[c][:, 1:1 + gw]), 16)
                            tk = bd.inc("l1sp", g.dma_start(
                                out=l1ob_sp[128 * c:128 * (c + 1),
                                            g0 - ffn0:g0 - ffn0 + gw],
                                in_=fsp[c][:, 1:1 + gw]), 16)
                        else:
                            tk = bd.inc("osp", g.dma_start(
                                out=out_d[128 * c:128 * (c + 1),
                                          g0 - R_FFN2[0]:g0 - R_FFN2[0] + gw],
                                in_=fsp[c][:, 1:1 + gw]), 16)
                            bd.wait(g, tk)
                            tk = bd.inc("osp", g.dma_start(
                                out=out_d[128 * c:128 * (c + 1),
                                          g0 - R_FFN2[0]:g0 - R_FFN2[0] + gw],
                                in_=x2_sp[128 * c:128 * (c + 1), g0:g0 + gw],
                                accum_op=OP.add), 16)
                    for c in range(NCHUNK):
                        fsp[c].rs.append(tk)
                bd.GP(out_sp)

            if l == 0 and dev == "l1":
                def dmp_l1(g):
                    bd.wait(g, ("l1sp", bd.counts["l1sp"]))
                    for c in range(NCHUNK):
                        tk = bd.inc("dmpd", g.dma_start(
                            out=dbg_d[128 * c:128 * (c + 1), :],
                            in_=l1o_sp[128 * c:128 * (c + 1), :]), 16)
                    bd.wait(g, tk)
                bd.GP(dmp_l1)
                return nc

        def fin_f(g):
            bd.wait(g, ("osp", bd.counts["osp"]))
        bd.GP(fin_f)
    return nc




def kernel(**inputs):
    nc = build_program(dev=None)
    in_maps = [_prep_core(inputs, c // 2, c % 2) for c in range(8)]
    res = run_bass_kernel_spmd(nc, in_maps, core_ids=list(range(8)))
    out = np.zeros((B, T, DIM), np.float32)
    for c in range(8):
        b, s = c // 2, c % 2
        out[b, s * OWN:(s + 1) * OWN, :] = res.results[c]["out"].T
    return out


# version 04
# speedup vs baseline: 1.1845x; 1.1780x over previous
import numpy as np
import ml_dtypes
import contextlib
import concourse.bass as bass
import concourse.mybir as mybir
from concourse.bass_utils import run_bass_kernel_spmd
from concourse.alu_op_type import AluOpType as OP

F32 = mybir.dt.float32
BF16 = mybir.dt.bfloat16
AF = mybir.ActivationFunctionType
AX = mybir.AxisListType.X
bfloat16 = ml_dtypes.bfloat16

B, T, DIM = 4, 4096, 1024
NCHUNK = DIM // 128
HALO, OWN = 1280, 2048
WIN = HALO + OWN
NEG = -30000.0
HEADS_O, HD = 16, 64
NSUM = 64
HEADS_I = 8
WLOC = 512
HID = 2730
H2 = 2 * HID
QK_SCALE = 8.0
R_OUT = (0, WIN)
R_KV1 = (128, WIN)
R_ATT1 = (640, WIN)
R_KV2 = (640, WIN)
R_MID2 = (1152, WIN)
R_FFN2 = (1280, WIN)
GS = 512


def groups(r):
    out, t = [], r[0]
    while t < r[1]:
        w = min(GS, r[1] - t)
        out.append((t, w))
        t += w
    return out


def qtiles(r):
    return list(range(r[0], r[1], 128))


N_QT_OUT = len(qtiles(R_OUT))
N_QT_A1 = len(qtiles(R_ATT1))
N_QT_A2 = len(qtiles(R_MID2))

W1_TILES = []
_j = 0
while _j < HID:
    W1_TILES.append(min(128, HID - _j))
    _j += 128
N_JT = len(W1_TILES)



def _pair_perm():
    perm = []
    for p in range(HEADS_I // 2):
        a, b = 2 * p, 2 * p + 1
        perm += list(range(64 * a, 64 * a + 32))
        perm += list(range(64 * b, 64 * b + 32))
        perm += list(range(64 * a + 32, 64 * a + 64))
        perm += list(range(64 * b + 32, 64 * b + 64))
    return np.array(perm)


def _qs_col(qs):
    d = np.concatenate([qs[0:32], qs[0:32], qs[32:64], qs[32:64]])
    return np.ascontiguousarray(d.reshape(128, 1).astype(np.float32))


def _rotary_tables(t_abs):
    half = np.arange(0, HD, 2, dtype=np.float64)
    inv_freq = 1.0 / 10000.0 ** (half / HD)
    tt = t_abs.astype(np.float64)
    fr = tt[:, None] * inv_freq[None, :]
    base = (half + 0.4 * HD) / (1.4 * HD)
    power = (tt - T // 2) / float(WLOC // 2)
    sc = base[None, :] ** power[:, None]
    cos, sin = np.cos(fr), np.sin(fr)

    def dup(a):
        return np.concatenate([a, a], axis=1)
    cos64, sin64, sc64 = dup(cos), dup(sin), dup(sc)
    sign = np.where(np.arange(HD) < 32, -1.0, 1.0)[None, :]
    tabs = [cos64 * sc64, sin64 * sc64 * sign, cos64 / sc64, sin64 / sc64 * sign]
    rows = (np.arange(128) % 32) + 32 * (np.arange(128) // 64)
    out = [np.ascontiguousarray(a.T[rows].astype(bfloat16)) for a in tabs]
    for i in (1, 3):
        out[i] = np.ascontiguousarray(
            np.concatenate([out[i][64:], out[i][:64]], axis=0))
    return out


def _inner_masks(first_abs):
    r = np.arange(128)[:, None]
    c = np.arange(640)[None, :]
    out = []
    for t0 in first_abs:
        bad = (c < r) | (c > r + 512) | ((t0 - 512 + c) < 0)
        out.append(np.where(bad, NEG, 0.0))
    return np.stack(out).astype(bfloat16)


def _outer_masks(first_abs):
    r = np.arange(128)[:, None]
    c = np.arange(NSUM)[None, :]
    out = []
    for t0 in first_abs:
        bad = (64 * c) > (t0 + r)
        out.append(np.where(bad, NEG, 0.0))
    return np.stack(out).astype(bfloat16)


def _prep_core(inp, b, s):
    S = s * OWN
    x = np.asarray(inp["L0_attn_x"][b])
    xT = np.ascontiguousarray(x.T)
    lo = S - HALO
    xw = np.zeros((DIM, WIN), np.float32)
    src_lo = max(lo, 0)
    xw[:, src_lo - lo:] = xT[:, src_lo:S + OWN]
    d = {
        "xf": xT,
        "xw": xw,
        "Wq": np.ascontiguousarray(np.asarray(inp["L0_attn_Wq"])),
        "Wk": np.ascontiguousarray(np.asarray(inp["L0_attn_Wk"])),
        "Wv": np.ascontiguousarray(np.asarray(inp["L0_attn_Wv"])),
        "Wo": np.ascontiguousarray(np.asarray(inp["L0_attn_Wo"])),
        "bq_c": np.ascontiguousarray(np.asarray(inp["L0_attn_bq"]).reshape(NCHUNK, 128).T),
        "bk_c": np.ascontiguousarray(np.asarray(inp["L0_attn_bk"]).reshape(NCHUNK, 128).T),
        "bo_c": np.ascontiguousarray(np.asarray(inp["L0_attn_bo"]).reshape(NCHUNK, 128).T),
        "bv_r": np.ascontiguousarray(np.asarray(inp["L0_attn_bv"]).reshape(1, DIM)),
    }
    perm = _pair_perm()
    for l in range(2):
        wqkv = np.asarray(inp["L0_ffn_wqkv"][l])
        d[f"wqkv{l}"] = np.ascontiguousarray(np.concatenate(
            [wqkv[:, 0:512][:, perm], wqkv[:, 512:1024][:, perm],
             wqkv[:, 1024:1536]], axis=1))
        d[f"qs{l}"] = _qs_col(np.asarray(inp["L0_ffn_q_scale"][l]))
        d[f"ks{l}"] = _qs_col(np.asarray(inp["L0_ffn_k_scale"][l]))
        d[f"wo{l}"] = np.ascontiguousarray(np.asarray(inp["L0_ffn_wo"][l]))
        d[f"lg_c{l}"] = np.ascontiguousarray(
            np.asarray(inp["L0_ffn_ln_g"][l]).reshape(NCHUNK, 128).T)
        d[f"lg_r{l}"] = np.ascontiguousarray(
            np.asarray(inp["L0_ffn_ln_g"][l]).reshape(1, DIM))
        d[f"lb_r{l}"] = np.ascontiguousarray(
            np.asarray(inp["L0_ffn_ln_b"][l]).reshape(1, DIM))
        w1 = np.asarray(inp["L0_ffn_w1"][l])
        b1 = np.asarray(inp["L0_ffn_b1"][l])
        pieces, bcols = [], []
        j = 0
        while j < HID:
            wdt = min(128, HID - j)
            pieces += [w1[:, j:j + wdt], w1[:, HID + j:HID + j + wdt]]
            bc = np.zeros((128, 2), np.float32)
            bc[:wdt, 0] = b1[j:j + wdt]
            bc[:wdt, 1] = b1[HID + j:HID + j + wdt]
            bcols.append(bc)
            j += wdt
        d[f"w1i{l}"] = np.ascontiguousarray(np.concatenate(pieces, axis=1))
        d[f"b1i{l}"] = np.ascontiguousarray(np.concatenate(bcols, axis=1))
        d[f"w2_{l}"] = np.ascontiguousarray(np.asarray(inp["L0_ffn_w2"][l]))
        d[f"b2_c{l}"] = np.ascontiguousarray(
            np.asarray(inp["L0_ffn_b2"][l]).reshape(NCHUNK, 128).T)
    d["ident"] = np.eye(128, dtype=bfloat16)
    ev = np.zeros((128,), np.float32)
    ev[0:32] = 1.0
    ev[64:96] = 1.0
    d["hsel2"] = np.ascontiguousarray(
        np.stack([ev, 1.0 - ev], axis=1).astype(bfloat16))
    d["psel2"] = np.ascontiguousarray(
        np.stack([ev, 1.0 - ev], axis=0).astype(bfloat16))
    d["selperm"] = np.ascontiguousarray(np.stack([ev, 1.0 - ev], axis=1))
    d["bcol"] = np.full((128, 1), 0.0 if s == 0 else 1.0, np.float32)
    d["mask_o"] = _outer_masks([lo + 128 * i for i in range(N_QT_OUT)])
    d["mask_i1"] = _inner_masks([lo + R_ATT1[0] + 128 * i for i in range(N_QT_A1)])
    d["mask_i2"] = _inner_masks([lo + R_MID2[0] + 128 * i for i in range(N_QT_A2)])
    tabs = _rotary_tables(np.arange(lo, S + OWN))
    for nm, a in zip(("csq", "sq", "csk", "sk"), tabs):
        d[f"rot_{nm}"] = a
    return d



class Buf:

    def __init__(self, ap):
        self.ap = ap
        self.w = None
        self.rs = []

    def __getitem__(self, idx):
        return self.ap[idx]


class Bld:

    def __init__(self, nc, es, block):
        self.nc = nc
        self.es = es
        self.block = block
        self.sems = {}
        self.counts = {}

    def _sem(self, name):
        if name not in self.sems:
            self.sems[name] = self.es.enter_context(self.nc.semaphore(name))
            self.counts[name] = 0
        return self.sems[name]

    def inc(self, name, instr, k=1):
        s = self._sem(name)
        instr.then_inc(s, k)
        self.counts[name] += k
        return (name, self.counts[name])

    def wait(self, eng, tok):
        if tok is None:
            return
        name, v = tok
        if v > 0:
            eng.wait_ge(self._sem(name), v)

    def wait_r(self, eng, buf):
        for t in buf.rs:
            self.wait(eng, t)

    def wrote(self, buf, tok):
        buf.w = tok
        buf.rs = []

    def read(self, eng, buf):
        self.wait(eng, buf.w)

    def did_read(self, buf, tok):
        buf.rs.append(tok)

    def SY(self, f):
        self.block.sync(f)

    def TE(self, f):
        self.block.tensor(f)

    def VE(self, f):
        self.block.vector(f)

    def SC(self, f):
        self.block.scalar(f)

    def GP(self, f):
        self.block.gpsimd(f)



def build_program(dev=None):
    nc = bass.Bass("TRN2", target_bir_lowering=False)
    es = contextlib.ExitStack()

    def din(name, shape, dt=F32):
        return nc.dram_tensor(name, list(shape), dt, kind="ExternalInput").ap()

    xf_d = din("xf", [DIM, T])
    xw_d = din("xw", [DIM, WIN])
    Wq_d = din("Wq", [DIM, DIM]); Wk_d = din("Wk", [DIM, DIM])
    Wv_d = din("Wv", [DIM, DIM]); Wo_d = din("Wo", [DIM, DIM])
    bq_d = din("bq_c", [128, NCHUNK]); bk_d = din("bk_c", [128, NCHUNK])
    bo_d = din("bo_c", [128, NCHUNK]); bv_d = din("bv_r", [1, DIM])
    ident_d = din("ident", [128, 128], BF16)
    mask_o_d = din("mask_o", [N_QT_OUT, 128, NSUM], BF16)
    mask_i_d = [din("mask_i1", [N_QT_A1, 128, 640], BF16),
                din("mask_i2", [N_QT_A2, 128, 640], BF16)]
    rot_d = {nm: din(f"rot_{nm}", [128, WIN], BF16)
             for nm in ("csq", "sq", "csk", "sk")}
    wqkv_d = [din(f"wqkv{l}", [DIM, 1536]) for l in range(2)]
    qs_d = [din(f"qs{l}", [128, 1]) for l in range(2)]
    ks_d = [din(f"ks{l}", [128, 1]) for l in range(2)]
    wo2_d = [din(f"wo{l}", [512, DIM]) for l in range(2)]
    lgc_d = [din(f"lg_c{l}", [128, NCHUNK]) for l in range(2)]
    lgr_d = [din(f"lg_r{l}", [1, DIM]) for l in range(2)]
    lbr_d = [din(f"lb_r{l}", [1, DIM]) for l in range(2)]
    w1i_d = [din(f"w1i{l}", [DIM, H2]) for l in range(2)]
    b1i_d = [din(f"b1i{l}", [128, 44]) for l in range(2)]
    w2_d = [din(f"w2_{l}", [HID, DIM]) for l in range(2)]
    b2_d = [din(f"b2_c{l}", [128, NCHUNK]) for l in range(2)]

    hsel2_d = din("hsel2", [128, 2], BF16)
    psel2_d = din("psel2", [2, 128], BF16)
    selperm_d = din("selperm", [128, 2])
    bcol_d = din("bcol", [128, 1])

    out_d = nc.dram_tensor("out", [DIM, OWN], F32, kind="ExternalOutput").ap()

    def dsp(name, tokens, dt=F32):
        return nc.dram_tensor(name, [DIM, tokens], dt, kind="Internal").ap()

    x2_sp = dsp("x2_sp", WIN)
    xffn_sp = dsp("xffn_sp", WIN)
    mid_sp = dsp("mid_sp", WIN - R_ATT1[0])
    l1o_sp = dsp("l1o_sp", WIN - R_ATT1[0])
    l1ob_sp = dsp("l1ob_sp", WIN - R_ATT1[0], BF16)
    xnmid_sp = dsp("xnmid_sp", WIN - R_ATT1[0], BF16)
    w1b_sp = [nc.dram_tensor(f"w1b{l}", [DIM, H2], BF16, kind="Internal").ap()
              for l in range(2)]
    w2b_sp = [nc.dram_tensor(f"w2b{l}", [HID, DIM], BF16, kind="Internal").ap()
              for l in range(2)]
    dbg_d = None
    if dev in ("p2q", "p2s", "p2t", "p2v", "p2sa", "p2sb"):
        dbg_d = nc.dram_tensor("dbg", [DIM, 4096], F32, kind="ExternalOutput").ap()
    elif dev is not None and dev.startswith("x2g"):
        dbg_d = nc.dram_tensor("dbg", [DIM, WIN], F32, kind="ExternalOutput").ap()
    elif dev == "x2":
        dbg_d = nc.dram_tensor("dbg", [DIM, WIN], F32, kind="ExternalOutput").ap()
    elif dev == "sum":
        dbg_d = nc.dram_tensor("dbg", [DIM, NSUM], F32, kind="ExternalOutput").ap()
    elif dev == "kv":
        dbg_d = nc.dram_tensor("dbg", [3 * 128, DIM], F32, kind="ExternalOutput").ap()
    elif dev == "xffn":
        dbg_d = nc.dram_tensor("dbg", [DIM, WIN], F32, kind="ExternalOutput").ap()
    elif dev == "l1":
        dbg_d = nc.dram_tensor("dbg", [DIM, WIN - R_ATT1[0]], F32,
                               kind="ExternalOutput").ap()

    def sbt(name, shape, dt):
        return es.enter_context(nc.sbuf_tensor(name, list(shape), dt))[:]

    def sbuf(name, shape, dt):
        return Buf(sbt(name, shape, dt))

    with es:
        ident = sbt("identb", [128, 128], BF16)
        bqc = sbt("bqc", [128, NCHUNK], F32)
        bkc = sbt("bkc", [128, NCHUNK], F32)
        boc = sbt("boc", [128, NCHUNK], F32)
        b2c = [sbt(f"b2c{l}", [128, NCHUNK], F32) for l in range(2)]
        lgcol = [sbt(f"lgcol{l}", [128, NCHUNK], F32) for l in range(2)]
        b1col = [sbt(f"b1col{l}", [128, 44], F32) for l in range(2)]
        qscol = [sbt(f"qscol{l}", [128, 1], F32) for l in range(2)]
        kscol = [sbt(f"kscol{l}", [128, 1], F32) for l in range(2)]
        rows = sbt("rows", [128, 3072], F32)
        RO_BV, RO_MU, RO_M2, RO_LG = 0, 1024, 1536, 2048
        rows_bf = sbt("rows_bf", [128, 1024], BF16)
        crow = sbt("crow", [128, 5696], BF16)
        CR_BV, CR_ONE, CR_M1 = 0, 5120, 5184
        CR_LG = [1024, 3072]
        CR_LB = [2048, 4096]
        e0ones = sbt("e0ones", [128, 128], BF16)
        onescol = sbt("onescol", [128, 1], BF16)
        epscol = sbt("epscol", [128, 1], F32)

        selcol = sbt("selcol", [128, 2], F32)
        wA = [sbuf(f"wA{c}", [128, DIM], BF16) for c in range(NCHUNK)]
        wB = [sbuf(f"wB{c}", [128, DIM], BF16) for c in range(NCHUNK)]

        xfr = [sbuf(f"xfr{c}", [128, GS], F32) for c in range(NCHUNK)]
        xbr = [sbuf("xbr0", [128, GS], BF16)]
        xsq = [sbuf(f"xsq{i}", [128, GS], BF16) for i in range(2)]
        t1r = [sbuf("t1r0", [128, GS], F32)]

        so_s = [sbuf(f"so{i}", [128, 640], F32) for i in range(2)]
        stat = sbuf("stat_s", [128, 64], F32)
        pcm = sbuf("pcm", [128, 1024], BF16)
        maskt = [sbuf(f"maskt{i}", [128, 640], BF16) for i in range(2)]

        khat = [sbuf(f"khat{p}", [128, 3200], BF16) for p in range(4)]
        qhat = [sbuf(f"qhat{p}", [128, 2688], BF16) for p in range(4)]
        vI = [sbuf(f"vI{t}", [128, 512], BF16) for t in range(25)]
        kband = [sbuf(f"kband{i}", [128, 640], BF16) for i in range(2)]
        pn_s = kband
        xgb = [sbuf(f"xgb{c}", [128, 2 * 513], BF16) for c in range(NCHUNK)]
        fsp = [sbuf(f"fsp{c}", [128, 513], F32) for c in range(NCHUNK)]
        qn_s = [sbuf(f"qn{i}", [128, 512], BF16) for i in range(2)]
        t2_s = [sbuf(f"t2{i}", [128, 512], BF16) for i in range(2)]
        t3_s = [sbuf(f"t3{i}", [128, 512], BF16) for i in range(2)]
        hsel2 = sbt("hsel2b", [128, 2], BF16)
        psel2 = sbt("psel2b", [2, 128], BF16)
        selperm = sbt("selpermb", [128, 2], F32)
        bcolb = sbt("bcolb", [128, 1], F32)
        zcol = sbuf("zcol", [128, 8], F32)
        sum_fm = [vI[2 + c // 4].ap[:, 128 * (c % 4):128 * (c % 4) + NSUM]
                  for c in range(NCHUNK)]
        ksumh = [vI[4 + h // 8].ap[:, 64 * (h % 8):64 * (h % 8) + NSUM]
                 for h in range(16)]
        oI = [Buf(qn_s[0].ap), Buf(qn_s[1].ap), Buf(t2_s[0].ap), Buf(t2_s[1].ap)]
        rsb = sbt("rsb", [2, 512], BF16)
        eps18 = sbt("eps18", [128, 1], F32)
        qo = [Buf(khat[p % 4].ap[:, 512 * (p // 4):512 * (p // 4) + 512])
              for p in range(8)]
        ofm = [Buf(khat[p % 4].ap[:, 1024 + 512 * (p // 4):
                                  1024 + 512 * (p // 4) + 512])
               for p in range(8)]
        vlo_b = Buf(khat[0].ap[:, 2048:3072])
        vhi_b = Buf(khat[1].ap[:, 2048:3072])
        vlo, vhi = vlo_b.ap, vhi_b.ap
        se_s = [Buf(qhat[i].ap.bitcast(F32)[:, 0:512]) for i in range(2)]
        x2r = [Buf(fsp[c].ap[:, 1:513]) for c in range(NCHUNK)]
        pbank = [Buf(es.enter_context(
            nc.psum_tensor(f"pb{i}", [128, 512], F32))[:]) for i in range(6)]
        pbf = [Buf(es.enter_context(
            nc.psum_tensor(f"pQ{i}", [128, 1024], BF16))[:]) for i in range(2)]
        pbfF = [Buf(pbf[i].ap.bitcast(F32)) for i in range(2)]

        block = es.enter_context(nc.Block())
        bd = Bld(nc, es, block)

        def p0_sync(sy):
            for dst, src in [(ident, ident_d), (bqc, bq_d), (bkc, bk_d),
                             (boc, bo_d)]:
                bd.inc("cld", sy.dma_start(out=dst, in_=src), 16)
            for l in range(2):
                for dst, src in [(b2c[l], b2_d[l]), (lgcol[l], lgc_d[l]),
                                 (b1col[l], b1i_d[l]), (qscol[l], qs_d[l]),
                                 (kscol[l], ks_d[l])]:
                    bd.inc("cld", sy.dma_start(out=dst, in_=src), 16)
            bd.inc("cld", sy.dma_start(out=rows[0:1, RO_BV:RO_BV + DIM],
                                       in_=bv_d), 16)
            bd.inc("cld", sy.dma_start(out=hsel2, in_=hsel2_d), 16)
            bd.inc("cld", sy.dma_start(out=psel2, in_=psel2_d), 16)
            bd.inc("cld", sy.dma_start(out=selperm, in_=selperm_d), 16)
            bd.inc("cld", sy.dma_start(out=bcolb, in_=bcol_d), 16)
        bd.SY(p0_sync)
        tok_cld = ("cld", bd.counts["cld"])

        def p0_wcast(g):
            for l in range(2):
                for r in range(0, DIM, 256):
                    tk = bd.inc("wpc", g.dma_start(
                        out=w1b_sp[l][r:r + 256, :],
                        in_=w1i_d[l][r:r + 256, :]), 16)
                for r in range(0, HID, 256):
                    rw = min(256, HID - r)
                    tk = bd.inc("wpc", g.dma_start(
                        out=w2b_sp[l][r:r + rw, :],
                        in_=w2_d[l][r:r + rw, :]), 16)
        bd.GP(p0_wcast)
        tok_wpc = ("wpc", bd.counts["wpc"])

        def p0_init(v):
            bd.wait(v, tok_cld)
            v.memset(e0ones, 0.0)
            v.drain()
            v.memset(e0ones[0:1, 0:128], 1.0)
            v.memset(onescol, 1.0)
            v.memset(crow[0:1, CR_ONE:CR_ONE + 64], 1.0)
            v.memset(crow[0:1, CR_M1:CR_M1 + 512], -1.0)
            v.memset(epscol, 1e-5)
            v.memset(eps18, 1e-18)
            v.memset(vlo, 0.0)
            v.memset(vhi, 0.0)
            v.memset(rows_bf, 0.0)
            v.memset(selcol[0:64, 0:1], 1.0)
            v.memset(selcol[64:128, 0:1], 0.0)
            v.memset(selcol[0:64, 1:2], 0.0)
            v.memset(selcol[64:128, 1:2], 1.0)
            bd.inc("cinit", v.tensor_copy(out=crow[0:1, CR_BV:CR_BV + DIM],
                                          in_=rows[0:1, RO_BV:RO_BV + DIM]))
        bd.VE(p0_init)
        tok_cinit = ("cinit", bd.counts["cinit"])

        for l in range(2):
            for which, src in (("lg", lgr_d[l]), ("lb", lbr_d[l])):
                def r1(sy, src=src):
                    bd.wait(sy, ("rowcast", bd.counts.get("rowcast", 0)))
                    if bd.counts.get("rowld", 0) == 0:
                        bd.wait(sy, tok_cinit)
                    bd.inc("rowld", sy.dma_start(
                        out=rows[0:1, RO_LG:RO_LG + DIM], in_=src), 16)
                bd.SY(r1)

                def r2(v, l=l, which=which):
                    bd.wait(v, ("rowld", bd.counts["rowld"]))
                    off = CR_LG[l] if which == "lg" else CR_LB[l]
                    bd.inc("rowcast", v.tensor_copy(
                        out=crow[0:1, off:off + DIM],
                        in_=rows[0:1, RO_LG:RO_LG + DIM]))
                bd.VE(r2)

        def emit_ln(w, xf32_bufs, xn_slices, affine_l=None, extra_bf=None,
                    out_f32=None):
            bsx, bsq, brb, bmb = pbank[0], pbank[1], pbank[2], pbank[3]
            for c in range(NCHUNK):
                def cs_f(g, c=c):
                    bd.read(g, xf32_bufs[c][0])
                    bd.wait_r(g, xbr[0])
                    tk = bd.inc("ln_cast", g.tensor_copy(
                        out=xbr[0][:, 0:w], in_=xf32_bufs[c][1]))
                    bd.wrote(xbr[0], tk)
                    bd.did_read(xf32_bufs[c][0], tk)
                bd.GP(cs_f)

                def sq_f(sc, c=c):
                    bd.wait(sc, xf32_bufs[c][0].w)
                    bd.wait_r(sc, xsq[c % 2])
                    tk = bd.inc("ln_sq", sc.activation(
                        out=xsq[c % 2][:, 0:w], in_=xf32_bufs[c][1],
                        func=AF.Square))
                    bd.wrote(xsq[c % 2], tk)
                    bd.did_read(xf32_bufs[c][0], tk)
                bd.SC(sq_f)

                def sums_f(t, c=c):
                    if c == 0:
                        bd.wait_r(t, bsx)
                        bd.wait_r(t, bsq)
                    bd.read(t, xbr[0])
                    i1 = t.matmul(bsx[0:1, 0:w], onescol, xbr[0][:, 0:w],
                                  start=(c == 0), stop=(c == NCHUNK - 1))
                    tk = bd.inc("ln_sx", i1)
                    bd.did_read(xbr[0], tk)
                    bd.read(t, xsq[c % 2])
                    i2 = t.matmul(bsq[0:1, 0:w], onescol, xsq[c % 2][:, 0:w],
                                  start=(c == 0), stop=(c == NCHUNK - 1))
                    tk = bd.inc("ln_sq2", i2)
                    bd.did_read(xsq[c % 2], tk)
                    if c == NCHUNK - 1:
                        bd.wrote(bsx, ("ln_sx", bd.counts["ln_sx"]))
                        bd.wrote(bsq, ("ln_sq2", bd.counts["ln_sq2"]))
                bd.TE(sums_f)

            def stats_f(v):
                bd.read(v, bsx)
                bd.wait(v, ("ln_apply", bd.counts.get("ln_apply", 0)))
                t1 = bd.inc("vch", v.tensor_scalar_mul(
                    rows[0:1, RO_MU:RO_MU + w], bsx[0:1, 0:w], 1.0 / DIM))
                bd.wait(v, t1)
                t2 = bd.inc("vch", v.tensor_tensor(
                    out=rows[0:1, RO_M2:RO_M2 + w],
                    in0=rows[0:1, RO_MU:RO_MU + w],
                    in1=rows[0:1, RO_MU:RO_MU + w], op=OP.mult))
                bd.read(v, bsq)
                bd.wait(v, t2)
                tk = bd.inc("ln_ve", v.scalar_tensor_tensor(
                    out=rows[0:1, RO_M2:RO_M2 + w], in0=bsq[0:1, 0:w],
                    scalar=1.0 / DIM, in1=rows[0:1, RO_M2:RO_M2 + w],
                    op0=OP.mult, op1=OP.subtract))
                bd.did_read(bsx, tk)
                bd.did_read(bsq, tk)
            bd.VE(stats_f)

            def rstd_f(sc):
                bd.wait(sc, ("ln_ve", bd.counts["ln_ve"]))
                bd.wait(sc, ("ln_rbc", bd.counts.get("ln_rbc", 0)))
                bd.wait(sc, tok_cinit)
                tl = bd.inc("sch", sc.activation(
                    out=rows[0:1, RO_M2:RO_M2 + w],
                    in_=rows[0:1, RO_M2:RO_M2 + w], func=AF.Ln,
                    bias=epscol[0:1, :]))
                bd.wait(sc, tl)
                bd.inc("ln_rstd", sc.activation(
                    out=rows_bf[0:1, 0:w], in_=rows[0:1, RO_M2:RO_M2 + w],
                    func=AF.Exp, scale=-0.5))
            bd.SC(rstd_f)

            def mr_f(v):
                bd.wait(v, ("ln_rstd", bd.counts["ln_rstd"]))
                bd.wait(v, ("ln_mbc", bd.counts.get("ln_mbc", 0)))
                bd.wait(v, ("ln_dp", bd.counts.get("ln_dp", 0)))
                bd.inc("ln_mr", v.tensor_tensor(
                    out=rows_bf[0:1, 512:512 + w],
                    in0=rows[0:1, RO_MU:RO_MU + w],
                    in1=rows_bf[0:1, 0:w], op=OP.mult))
            bd.VE(mr_f)

            def bc_f(t):
                bd.wait(t, tok_cinit)
                bd.wait_r(t, brb)
                bd.wait(t, ("ln_rstd", bd.counts["ln_rstd"]))
                tk = bd.inc("ln_rbc", t.matmul(
                    brb[:, 0:w], e0ones, rows_bf[:, 0:w], start=True, stop=True))
                bd.wrote(brb, tk)
                if affine_l is None:
                    bd.wait_r(t, bmb)
                    bd.wait(t, ("ln_mr", bd.counts["ln_mr"]))
                    tk = bd.inc("ln_mbc", t.matmul(
                        bmb[:, 0:w], e0ones, rows_bf[:, 512:512 + w],
                        start=True, stop=True))
                    bd.wrote(bmb, tk)
            bd.TE(bc_f)

            if affine_l is None:
                def apply_plain(v):
                    bd.read(v, pbank[2])
                    bd.read(v, pbank[3])
                    for c in range(NCHUNK):
                        bd.wait_r(v, t1r[0])
                        bd.wait_r(v, xn_slices[c][0])
                        v.drain()
                        v.scalar_tensor_tensor(
                            out=t1r[0][:, 0:w], in0=xf32_bufs[c][1],
                            scalar=1.0,
                            in1=pbank[2][:, 0:w], op0=OP.mult, op1=OP.mult)
                        v.drain()
                        last = v.tensor_tensor(
                            out=xn_slices[c][1], in0=t1r[0][:, 0:w],
                            in1=pbank[3][:, 0:w], op=OP.subtract)
                        if out_f32 is not None:
                            bd.wait_r(v, out_f32[c][0])
                            v.drain()
                            last = v.tensor_tensor(
                                out=out_f32[c][1], in0=t1r[0][:, 0:w],
                                in1=pbank[3][:, 0:w], op=OP.subtract)
                        if c == NCHUNK - 1:
                            tk = bd.inc("ln_apply", last)
                        else:
                            tk = bd.inc("ln_app_c", last)
                        bd.wrote(t1r[0], tk)
                        bd.wrote(xn_slices[c][0], tk)
                        if out_f32 is not None:
                            bd.wrote(out_f32[c][0], tk)
                        bd.did_read(xf32_bufs[c][0], tk)
                    bd.did_read(pbank[2], tk)
                    bd.did_read(pbank[3], tk)
                bd.VE(apply_plain)
            else:
                for c in range(NCHUNK):
                    def dp_f(t, c=c):
                        bd.wait_r(t, bmb)
                        if c == 0:
                            bd.wait(t, ("ln_mr", bd.counts["ln_mr"]))
                        lgs = crow[0:1, CR_LG[affine_l] + 128 * c:
                                    CR_LG[affine_l] + 128 * (c + 1)]
                        lbs = crow[0:1, CR_LB[affine_l] + 128 * c:
                                    CR_LB[affine_l] + 128 * (c + 1)]
                        t.matmul(bmb[:, 0:w], lgs, rows_bf[0:1, 512:512 + w],
                                 start=True, stop=False)
                        tk = bd.inc("ln_dp", t.matmul(
                            bmb[:, 0:w], lbs, crow[0:1, CR_M1:CR_M1 + w],
                            start=False, stop=True))
                        bd.wrote(bmb, tk)
                    bd.TE(dp_f)

                    def ap_f(v, c=c):
                        if c == 0:
                            bd.read(v, pbank[2])
                        bd.read(v, pbank[3])
                        bd.wait_r(v, t1r[0])
                        bd.wait_r(v, xn_slices[c][0])
                        v.drain()
                        v.scalar_tensor_tensor(
                            out=t1r[0][:, 0:w], in0=xf32_bufs[c][1],
                            scalar=lgcol[affine_l][:, c:c + 1],
                            in1=pbank[2][:, 0:w], op0=OP.mult, op1=OP.mult)
                        v.drain()
                        last = v.tensor_tensor(
                            out=xn_slices[c][1], in0=t1r[0][:, 0:w],
                            in1=pbank[3][:, 0:w], op=OP.subtract)
                        if c == NCHUNK - 1:
                            tk = bd.inc("ln_apply", last)
                        else:
                            tk = bd.inc("ln_dused", last)
                        bd.wrote(t1r[0], tk)
                        bd.wrote(xn_slices[c][0], tk)
                        bd.did_read(xf32_bufs[c][0], tk)
                        bd.did_read(bmb, tk)
                        if c == NCHUNK - 1:
                            bd.did_read(pbank[2], tk)
                    bd.VE(ap_f)
            return ("ln_apply", bd.counts["ln_apply"])

        for g in range(8):
            slot = g % 2

            def p1_ld(sy, g=g, slot=slot):
                for c in range(NCHUNK):
                    bd.wait_r(sy, xfr[c])
                for c in range(NCHUNK):
                    tk = bd.inc(f"xld{slot}", sy.dma_start(
                        out=xfr[c][:, 0:GS],
                        in_=xf_d[128 * c:128 * (c + 1), g * GS:(g + 1) * GS]), 16)
                for c in range(NCHUNK):
                    bd.wrote(xfr[c], tk)
            bd.SY(p1_ld)
            xf32 = [(xfr[c], xfr[c][:, 0:GS])
                    for c in range(NCHUNK)]
            xnout = [(xgb[c], xgb[c][:, slot * 513 + 1:slot * 513 + 1 + GS])
                     for c in range(NCHUNK)]
            ln_tok = emit_ln(GS, xf32, xnout)

            def p1_sum(v, g=g, slot=slot, ln_tok=ln_tok):
                bd.wait(v, ln_tok)
                bd.wait_r(v, stat)
                pslot = (g - 1) % 2
                for c in range(NCHUNK):
                    if c > 0:
                        v.drain()
                    src = xgb[c][:, slot * 513 + 2: slot * 513 + 2 + 448]
                    src3 = src.rearrange("p (j w) -> p j w", w=64)
                    v.tensor_reduce(out=stat[:, 1:8], in_=src3,
                                    axis=AX, op=OP.add)
                    if g >= 1:
                        v.tensor_reduce(
                            out=stat[:, 0:1],
                            in_=xgb[c][:, pslot * 513 + 450: pslot * 513 + 513],
                            axis=AX, op=OP.add)
                        v.drain()
                        v.tensor_tensor(out=stat[:, 0:1], in0=stat[:, 0:1],
                                        in1=xgb[c][:, slot * 513 + 1:slot * 513 + 2],
                                        op=OP.add)
                        v.drain()
                        last = v.tensor_scalar_mul(
                            sum_fm[c][:, 8 * g: 8 * g + 8],
                            stat[:, 0:8], 1.0 / 64)
                    else:
                        v.drain()
                        last = v.tensor_scalar_mul(
                            sum_fm[c][:, 1:8], stat[:, 1:8], 1.0 / 64)
                        last = v.tensor_copy(out=sum_fm[c][:, 0:1],
                                             in_=xgb[c][:, 1:2])
                tk = bd.inc("p1_sum", last)
                bd.wrote(stat, tk)
                for c in range(NCHUNK):
                    bd.did_read(xgb[c], tk)
            bd.VE(p1_sum)
        tok_sum = ("p1_sum", bd.counts["p1_sum"])

        def stream_w(dram, dest, tag, ncols=DIM, nchunk=NCHUNK):
            def ld(g):
                for c in range(nchunk):
                    bd.wait_r(g, dest[c])
                for c in range(nchunk):
                    tk = bd.inc(f"wc_{tag}", g.dma_start(
                        out=dest[c][:, 0:ncols],
                        in_=dram[128 * c:128 * (c + 1), 0:ncols]), 16)
                for c in range(nchunk):
                    bd.wrote(dest[c], tk)
            bd.GP(ld)
            return [( f"wc_{tag}", bd.counts[f"wc_{tag}"])]

        if dev == "sum":
            def dump_sum(v):
                bd.wait(v, tok_sum)
                for c in range(NCHUNK):
                    tk = bd.inc("dmp", v.tensor_copy(out=t1r[c][:, 0:NSUM],
                                                     in_=sum_fm[c]))
                    bd.wrote(t1r[c], tk)
            bd.VE(dump_sum)

            def dump_dma(g):
                for c in range(NCHUNK):
                    bd.read(g, t1r[c])
                for c in range(NCHUNK):
                    tk = bd.inc("dmpd", g.dma_start(
                        out=dbg_d[128 * c:128 * (c + 1), :],
                        in_=t1r[c][:, 0:NSUM]), 16)
                bd.wait(g, tk)
            bd.GP(dump_dma)
            return nc

        tok_wk = stream_w(Wk_d, wA, "wk")
        tok_wv = stream_w(Wv_d, wB, "wv")

        def p1b_ks(t):
            bd.wait(t, tok_sum)
            bd.wait(t, tok_wk[-1])
            for p in range(8):
                bank = pbank[4 + p // 4]
                if p % 4 == 0:
                    bd.wait_r(t, bank)
                off = 64 * (p % 4)
                for c in range(NCHUNK):
                    i = t.matmul(bank[:, off:off + NSUM],
                                 wA[c][:, 128 * p:128 * (p + 1)], sum_fm[c],
                                 start=(c == 0), stop=(c == NCHUNK - 1))
                tk = bd.inc("ks_mm", i)
                if p % 4 == 3:
                    bd.wrote(bank, tk)
            for c in range(NCHUNK):
                bd.did_read(wA[c], ("ks_mm", bd.counts["ks_mm"]))
        bd.TE(p1b_ks)

        def p1b_ksev(v):
            bd.wait(v, ("ks_mm", bd.counts["ks_mm"]))
            bd.wait(v, tok_cinit)
            for h in range(16):
                p = h // 2
                bank = pbank[4 + p // 4]
                off = 64 * (p % 4)
                last = v.tensor_scalar(
                    out=ksumh[h], in0=bank[:, off:off + NSUM],
                    scalar1=bkc[:, p:p + 1], scalar2=selcol[:, h % 2:h % 2 + 1],
                    op0=OP.add, op1=OP.mult)
            tk = bd.inc("ks_ev", last)
            bd.did_read(pbank[4], tk)
            bd.did_read(pbank[5], tk)
        bd.VE(p1b_ksev)
        tok_ks = ("ks_ev", bd.counts["ks_ev"])

        def p1b_vs(t):
            bd.wait(t, tok_sum)
            bd.wait(t, tok_wv[-1])
            bd.wait(t, tok_cinit)
            for h in range(2):
                bank = pbank[4 + h]
                bd.wait_r(t, bank)
                for c in range(NCHUNK):
                    t.matmul(bank[0:64, 0:512], sum_fm[c],
                             wB[c][:, 512 * h:512 * (h + 1)],
                             start=(c == 0), stop=False)
                tk = bd.inc("vs_mm", t.matmul(
                    bank[0:64, 0:512], crow[0:1, CR_ONE:CR_ONE + 64],
                    crow[0:1, CR_BV + 512 * h:CR_BV + 512 * (h + 1)],
                    start=False, stop=True))
                bd.wrote(bank, tk)
            for c in range(NCHUNK):
                bd.did_read(wB[c], ("vs_mm", bd.counts["vs_mm"]))
        bd.TE(p1b_vs)

        def p1b_vsev(v):
            bd.wait(v, ("vs_mm", bd.counts["vs_mm"]))
            bd.wait(v, tok_cinit)
            for h in range(2):
                bank = pbank[4 + h]
                v.tensor_copy(out=vlo[0:64, 512 * h:512 * (h + 1)],
                              in_=bank[0:64, 0:512])
                last = v.tensor_copy(out=vhi[64:128, 512 * h:512 * (h + 1)],
                                     in_=bank[0:64, 0:512])
                bd.did_read(bank, bd.inc("vs_ev", last))
        bd.VE(p1b_vsev)
        tok_vs = ("vs_ev", bd.counts["vs_ev"])

        if dev == "kv":
            def dump_kv(v):
                bd.wait(v, tok_ks)
                bd.wait(v, tok_vs)
                for p in range(8):
                    tk = bd.inc("dmp", v.tensor_copy(
                        out=t1r[0][:, 0:NSUM] if False else xfr[0][:, 0:NSUM],
                        in_=ksum[p]))
            bd.VE(dump_kv)

            def dump_kv2(v):
                for p in range(8):
                    tk = bd.inc("dmp2", v.tensor_copy(
                        out=xfr[p][:, 0:NSUM], in_=ksum[p]))
                    bd.wrote(xfr[p], tk)
                tk = bd.inc("dmp2", v.tensor_copy(out=x2r[0][:, 0:DIM // 2],
                                                  in_=vlo[:, 0:DIM // 2]))
                bd.wrote(x2r[0], tk)
                tk = bd.inc("dmp2", v.tensor_copy(out=x2r[1][:, 0:DIM // 2],
                                                  in_=vlo[:, DIM // 2:]))
                bd.wrote(x2r[1], tk)
                tk = bd.inc("dmp2", v.tensor_copy(out=x2r[2][:, 0:DIM // 2],
                                                  in_=vhi[:, 0:DIM // 2]))
                bd.wrote(x2r[2], tk)
                tk = bd.inc("dmp2", v.tensor_copy(out=x2r[3][:, 0:DIM // 2],
                                                  in_=vhi[:, DIM // 2:]))
                bd.wrote(x2r[3], tk)
            bd.VE(dump_kv2)

            def dump_kvd(g):
                for p in range(8):
                    bd.read(g, xfr[p])
                    tk = bd.inc("dmpd", g.dma_start(
                        out=dbg_d[0:128, NSUM * p:NSUM * (p + 1)],
                        in_=xfr[p][:, 0:NSUM]), 16)
                for i in range(4):
                    bd.read(g, x2r[i])
                    tk = bd.inc("dmpd", g.dma_start(
                        out=dbg_d[128 * (1 + i // 2):128 * (2 + i // 2),
                                  (DIM // 2) * (i % 2):(DIM // 2) * (i % 2 + 1)],
                        in_=x2r[i][:, 0:DIM // 2]), 16)
                bd.wait(g, tk)
            bd.GP(dump_kvd)
            return nc

        tok_wq = stream_w(Wq_d, wA, "wq")
        tok_wo = stream_w(Wo_d, wB, "wo")

        p2_groups = groups(R_OUT)
        if dev is not None and dev.startswith("x2g"):
            p2_groups = p2_groups[:int(dev[3:])]
        for gi, (g0, gw) in enumerate(p2_groups):
            slot = gi % 2

            def p2_ld(sy, g0=g0, gw=gw, slot=slot):
                for c in range(NCHUNK):
                    bd.wait_r(sy, xfr[c])
                for c in range(NCHUNK):
                    tk = bd.inc(f"xld{slot}", sy.dma_start(
                        out=xfr[c][:, 0:gw],
                        in_=xw_d[128 * c:128 * (c + 1), g0:g0 + gw]), 16)
                for c in range(NCHUNK):
                    bd.wrote(xfr[c], tk)
            bd.SY(p2_ld)
            xf32 = [(xfr[c], xfr[c][:, 0:gw])
                    for c in range(NCHUNK)]
            xnout = [(xgb[c], xgb[c][:, slot * 513 + 1:slot * 513 + 1 + gw])
                     for c in range(NCHUNK)]
            ln_tok = emit_ln(gw, xf32, xnout)

            for p in range(8):
                def p2_q(t, p=p, gi=gi, gw=gw, slot=slot, ln_tok=ln_tok):
                    if p == 0:
                        bd.wait(t, ln_tok)
                        if gi == 0:
                            bd.wait(t, tok_wq[-1])
                            bd.wait(t, tok_ks)
                    bank = pbank[4 + p % 2]
                    bd.wait_r(t, bank)
                    for c in range(NCHUNK):
                        i = t.matmul(bank[:, 0:gw],
                                     wA[c][:, 128 * p:128 * (p + 1)],
                                     xgb[c][:, slot * 513 + 1:slot * 513 + 1 + gw],
                                     start=(c == 0), stop=(c == NCHUNK - 1))
                    tk = bd.inc("p2_qmm", i)
                    bd.wrote(pbank[4 + p % 2], tk)
                    if p == 7:
                        for c in range(NCHUNK):
                            bd.did_read(xgb[c], tk)
                bd.TE(p2_q)

                def p2_qev(v, p=p, gw=gw):
                    bank = pbank[4 + p % 2]
                    bd.read(v, bank)
                    bd.wait_r(v, qo[p])
                    tk = bd.inc("p2_qev", v.tensor_scalar_add(
                        qo[p][:, 0:gw], bank[:, 0:gw], bqc[:, p:p + 1]))
                    bd.wrote(qo[p], tk)
                    bd.did_read(bank, tk)
                bd.VE(p2_qev)

            if dev == "p2q":
                def dump_q(v):
                    bd.wait(v, ("p2_qev", bd.counts["p2_qev"]))
                    for p in range(8):
                        tk = bd.inc("dmp", v.tensor_copy(out=x2r[p][:, 0:GS],
                                                         in_=qo[p][:, 0:GS]))
                        bd.wrote(x2r[p], tk)
                bd.VE(dump_q)

                def dump_qd(g):
                    for p in range(8):
                        bd.read(g, x2r[p])
                        tk = bd.inc("dmpd", g.dma_start(
                            out=dbg_d[128 * p:128 * (p + 1), 0:GS],
                            in_=x2r[p][:, 0:GS]), 16)
                    bd.wait(g, tk)
                bd.GP(dump_qd)
                return nc

            for qt in range(gw // 128):
                qti = g0 // 128 + qt
                qc = qt * 128
                mslot = maskt[qti % 2]

                def p2_mask(sy, qti=qti, mslot=mslot):
                    bd.wait_r(sy, mslot)
                    tk = bd.inc(f"p2_mld{qti % 2}", sy.dma_start(
                        out=mslot[:, 0:NSUM], in_=mask_o_d[qti, :, :]), 16)
                    bd.wrote(mslot, tk)
                bd.SY(p2_mask)

                def p2_s(t, qc=qc):
                    for pr in range(8):
                        bd.read(t, qo[pr])
                    for i in range(2):
                        bd.wait_r(t, pbank[4 + i])
                    for h in range(HEADS_O):
                        bank = pbank[4 + h // 8]
                        off = 64 * (h % 8)
                        pr = h // 2
                        i = t.matmul(bank[:, off:off + NSUM],
                                     qo[pr][:, qc:qc + 128],
                                     ksumh[h],
                                     start=True, stop=True)
                    tk = bd.inc("p2_smm", i)
                    for i in range(2):
                        bd.wrote(pbank[4 + i], tk)
                    for pr in range(8):
                        bd.did_read(qo[pr], tk)
                bd.TE(p2_s)

                if dev == "p2sa" and qt == 0:
                    def dump_sa(v):
                        bd.wait(v, ("p2_smm", bd.counts["p2_smm"]))
                        for i in range(2):
                            tk = bd.inc("dmp", v.tensor_copy(
                                out=x2r[i][:, 0:GS], in_=pbank[4 + i][:, 0:512]))
                            bd.wrote(x2r[i], tk)
                    bd.VE(dump_sa)

                    def dump_sad(g):
                        for i in range(2):
                            bd.read(g, x2r[i])
                            tk = bd.inc("dmpd", g.dma_start(
                                out=dbg_d[128 * i:128 * (i + 1), 0:GS],
                                in_=x2r[i][:, 0:GS]), 16)
                        bd.wait(g, tk)
                    bd.GP(dump_sad)
                    return nc

                def p2_soev(v, mslot=mslot):
                    bd.wait(v, ("p2_smm", bd.counts["p2_smm"]))
                    bd.read(v, mslot)
                    for i in range(2):
                        bd.wait_r(v, so_s[i])
                        m3 = mslot[:, 0:NSUM].unsqueeze(1) \
                            .broadcast_to([128, 8, NSUM])
                        tk = bd.inc("p2_soev", v.scalar_tensor_tensor(
                            out=so_s[i][:, 0:512].rearrange(
                                "p (h c) -> p h c", c=NSUM),
                            in0=pbank[4 + i][:, 0:512].rearrange(
                                "p (h c) -> p h c", c=NSUM),
                            scalar=0.125, in1=m3, op0=OP.mult, op1=OP.add))
                        bd.wrote(so_s[i], tk)
                        bd.did_read(pbank[4 + i], tk)
                        bd.did_read(mslot, tk)
                bd.VE(p2_soev)

                if dev == "p2sb" and qt == 0:
                    def dump_sb(v):
                        bd.wait(v, ("p2_soev", bd.counts["p2_soev"]))
                        for i in range(2):
                            tk = bd.inc("dmp", v.tensor_copy(
                                out=x2r[i][:, 0:GS], in_=so_s[i][:, 0:GS]))
                            bd.wrote(x2r[i], tk)
                    bd.VE(dump_sb)

                    def dump_sbd(g):
                        for i in range(2):
                            bd.read(g, x2r[i])
                            tk = bd.inc("dmpd", g.dma_start(
                                out=dbg_d[128 * i:128 * (i + 1), 0:GS],
                                in_=x2r[i][:, 0:GS]), 16)
                        bd.wait(g, tk)
                    bd.GP(dump_sbd)
                    return nc

                def p2_soft(v):
                    bd.wait_r(v, stat)
                    v.drain()
                    for i in range(2):
                        v.tensor_reduce(
                            out=stat[:, 16 * i:16 * i + 8],
                            in_=so_s[i][:, 0:512].rearrange(
                                "p (h c) -> p h c", c=NSUM),
                            axis=AX, op=OP.max, negate=True)
                        nb = stat[:, 16 * i:16 * i + 8].unsqueeze(2) \
                            .broadcast_to([128, 8, NSUM])
                        bd.wait_r(v, se_s[i])
                        v.drain()
                        tk = bd.inc("p2_se", v.tensor_tensor(
                            out=se_s[i][:, 0:512].rearrange(
                                "p (h c) -> p h c", c=NSUM),
                            in0=so_s[i][:, 0:512].rearrange(
                                "p (h c) -> p h c", c=NSUM),
                            in1=nb, op=OP.add))
                        bd.wrote(se_s[i], tk)
                        bd.did_read(so_s[i], tk)
                    stat.w = ("p2_se", bd.counts["p2_se"])
                bd.VE(p2_soft)

                def p2_exp(sc):
                    bd.wait(sc, ("p2_se", bd.counts["p2_se"]))
                    for i in range(2):
                        bd.wait_r(sc, pn_s[i])
                        tk = bd.inc("p2_exp", sc.activation(
                            out=pn_s[i][:, 0:512], in_=se_s[i][:, 0:512],
                            func=AF.Exp))
                        bd.wrote(pn_s[i], tk)
                        bd.did_read(se_s[i], tk)
                bd.SC(p2_exp)

                def p2_norm(v):
                    bd.wait(v, ("p2_exp", bd.counts["p2_exp"]))
                    for i in range(2):
                        v.tensor_reduce(
                            out=stat[:, 32 + 8 * i:40 + 8 * i],
                            in_=pn_s[i][:, 0:512].rearrange(
                                "p (h c) -> p h c", c=NSUM),
                            axis=AX, op=OP.add)
                        rz = stat[:, 48 + 8 * i:56 + 8 * i]
                        v.drain()
                        v.reciprocal(out=rz, in_=stat[:, 32 + 8 * i:40 + 8 * i])
                        rzb = rz.unsqueeze(2).broadcast_to([128, 8, NSUM])
                        v.drain()
                        tk = bd.inc("p2_pn", v.tensor_tensor(
                            out=pn_s[i][:, 0:512].rearrange(
                                "p (h c) -> p h c", c=NSUM),
                            in0=pn_s[i][:, 0:512].rearrange(
                                "p (h c) -> p h c", c=NSUM),
                            in1=rzb, op=OP.mult))
                        pn_s[i].w = tk
                bd.VE(p2_norm)

                if dev == "p2s" and qt == 0:
                    def dump_s(v):
                        bd.wait(v, ("p2_pn", bd.counts["p2_pn"]))
                        for i in range(2):
                            tk = bd.inc("dmp", v.tensor_copy(
                                out=x2r[i][:, 0:GS], in_=pn_s[i][:, 0:GS]))
                            bd.wrote(x2r[i], tk)
                        for i in range(2):
                            tk = bd.inc("dmp", v.tensor_copy(
                                out=x2r[2 + i][:, 0:GS], in_=so_s[i][:, 0:GS]))
                            bd.wrote(x2r[2 + i], tk)
                    bd.VE(dump_s)

                    def dump_sd(g):
                        for i in range(4):
                            bd.read(g, x2r[i])
                            tk = bd.inc("dmpd", g.dma_start(
                                out=dbg_d[128 * i:128 * (i + 1), 0:GS],
                                in_=x2r[i][:, 0:GS]), 16)
                        bd.wait(g, tk)
                    bd.GP(dump_sd)
                    return nc

                def p2_tr(t):
                    for i in range(2):
                        bd.read(t, pn_s[i])
                    bd.wait_r(t, pbf[0])
                    for p in range(8):
                        i = t.transpose(
                            pbf[0][:, 128 * p:128 * (p + 1)],
                            pn_s[p // 4][:, 128 * (p % 4):128 * (p % 4) + 128],
                            ident)
                    tk = bd.inc("p2_tr", i)
                    bd.wrote(pbf[0], tk)
                    for i in range(2):
                        bd.did_read(pn_s[i], tk)
                bd.TE(p2_tr)

                def p2_pcm(sc):
                    bd.read(sc, pbf[0])
                    bd.wait_r(sc, pcm)
                    tk = bd.inc("p2_pcm", sc.copy(out=pcm.ap, in_=pbf[0].ap))
                    bd.wrote(pcm, tk)
                    bd.did_read(pbf[0], tk)
                bd.SC(p2_pcm)

                if dev == "p2t" and qt == 0:
                    def dump_t(v):
                        bd.wait(v, ("p2_pcm", bd.counts["p2_pcm"]))
                        tk = bd.inc("dmp", v.tensor_copy(
                            out=x2r[0][:, 0:GS], in_=pcm[:, 0:512]))
                        bd.wrote(x2r[0], tk)
                        tk = bd.inc("dmp", v.tensor_copy(
                            out=x2r[1][:, 0:GS], in_=pcm[:, 512:1024]))
                        bd.wrote(x2r[1], tk)
                    bd.VE(dump_t)

                    def dump_td(g):
                        for i in range(2):
                            bd.read(g, x2r[i])
                            tk = bd.inc("dmpd", g.dma_start(
                                out=dbg_d[128 * i:128 * (i + 1), 0:GS],
                                in_=x2r[i][:, 0:GS]), 16)
                        bd.wait(g, tk)
                    bd.GP(dump_td)
                    return nc

                def p2_av(t, qti=qti):
                    bd.read(t, pcm)
                    if qti == 0:
                        bd.wait(t, tok_vs)
                    for bk in range(2, 6):
                        bd.wait_r(t, pbank[bk])
                    for h in range(HEADS_O):
                        vt = vlo if h % 2 == 0 else vhi
                        pr = h // 2
                        bank = pbank[2 + pr // 2]
                        base = 64 * (h % 2)
                        i = t.matmul(
                            bank[base:base + 64,
                                 128 * (pr % 2):128 * (pr % 2) + 128],
                            vt[:, 64 * h:64 * h + 64],
                            pcm[:, 128 * pr:128 * pr + 128],
                            start=True, stop=True,
                            tile_position=(0, base))
                    tk = bd.inc("p2_avmm", i)
                    for bk in range(2, 6):
                        bd.wrote(pbank[bk], tk)
                    bd.did_read(pcm, tk)
                bd.TE(p2_av)

                def p2_avev(v, qc=qc):
                    bd.wait(v, ("p2_avmm", bd.counts["p2_avmm"]))
                    for pr in range(8):
                        bank = pbank[2 + pr // 2]
                        tk = bd.inc("p2_avev", v.tensor_copy(
                            out=ofm[pr][:, qc:qc + 128],
                            in_=bank[:, 128 * (pr % 2):128 * (pr % 2) + 128]))
                        bd.did_read(bank, tk)
                        ofm[pr].w = tk
                bd.VE(p2_avev)


            if dev == "p2v":
                def dump_v(v):
                    bd.wait(v, ("p2_avev", bd.counts["p2_avev"]))
                    for p in range(8):
                        tk = bd.inc("dmp", v.tensor_copy(
                            out=x2r[p][:, 0:GS], in_=ofm[p][:, 0:GS]))
                        bd.wrote(x2r[p], tk)
                bd.VE(dump_v)

                def dump_vd(g):
                    for p in range(8):
                        bd.read(g, x2r[p])
                        tk = bd.inc("dmpd", g.dma_start(
                            out=dbg_d[128 * p:128 * (p + 1), 0:GS],
                            in_=x2r[p][:, 0:GS]), 16)
                    bd.wait(g, tk)
                bd.GP(dump_vd)
                return nc

            for dm in range(8):
                def p2_wo(t, dm=dm, gi=gi, gw=gw):
                    if dm == 0:
                        for pr in range(8):
                            bd.read(t, ofm[pr])
                        if gi == 0:
                            bd.wait(t, tok_wo[-1])
                    bank = pbank[4 + dm % 2]
                    bd.wait_r(t, bank)
                    for c in range(NCHUNK):
                        i = t.matmul(bank[:, 0:gw],
                                     wB[c][:, 128 * dm:128 * (dm + 1)],
                                     ofm[c][:, 0:gw],
                                     start=(c == 0), stop=(c == NCHUNK - 1))
                    tk = bd.inc("p2_womm", i)
                    bd.wrote(bank, tk)
                    if dm == 7:
                        for pr in range(8):
                            bd.did_read(ofm[pr], tk)
                bd.TE(p2_wo)

                def p2_x2(v, dm=dm, gw=gw, slot=slot):
                    bank = pbank[4 + dm % 2]
                    bd.read(v, bank)
                    bd.wait_r(v, x2r[dm])
                    tk = bd.inc("p2_x2", v.scalar_tensor_tensor(
                        out=x2r[dm][:, 0:gw], in0=bank[:, 0:gw],
                        scalar=boc[:, dm:dm + 1],
                        in1=xfr[dm][:, 0:gw],
                        op0=OP.add, op1=OP.add))
                    bd.wrote(x2r[dm], tk)
                    bd.did_read(bank, tk)
                    bd.did_read(xfr[dm], tk)
                bd.VE(p2_x2)

            def p2_sp(g, g0=g0, gw=gw):
                for c in range(NCHUNK):
                    bd.read(g, x2r[c])
                for c in range(NCHUNK):
                    tk = bd.inc("p2_sp", g.dma_start(
                        out=x2_sp[128 * c:128 * (c + 1), g0:g0 + gw],
                        in_=x2r[c][:, 0:gw]), 16)
                    if dev == "x2" or (dev is not None and dev.startswith("x2g")):
                        tk = bd.inc("p2_sp", g.dma_start(
                            out=dbg_d[128 * c:128 * (c + 1), g0:g0 + gw],
                            in_=x2r[c][:, 0:gw]), 16)
                for c in range(NCHUNK):
                    bd.did_read(x2r[c], tk)
            bd.GP(p2_sp)

        if dev == "x2" or (dev is not None and dev.startswith("x2g")):
            def fin(g):
                bd.wait(g, ("p2_sp", bd.counts["p2_sp"]))
            bd.GP(fin)
            return nc

        LAYERS = [
            dict(l=0, kv0=R_KV1[0], att0=R_ATT1[0], ffn0=R_ATT1[0],
                 mask_d=mask_i_d[0]),
            dict(l=1, kv0=R_KV2[0], att0=R_MID2[0], ffn0=R_FFN2[0],
                 mask_d=mask_i_d[1]),
        ]

        def init_tails(v):
            for c in range(NCHUNK):
                v.memset(xgb[c][:, 0:1], 0.0)
                v.memset(xgb[c][:, 513:514], 0.0)
                tk = bd.inc("tails", v.memset(fsp[c][:, 0:1], 0.0))
                xgb[c].rs.append(tk)
                fsp[c].rs.append(tk)
        bd.VE(init_tails)
        tok_tails = ("tails", bd.counts["tails"])


        for L in LAYERS:
            l = L["l"]
            kv0, att0, ffn0 = L["kv0"], L["att0"], L["ffn0"]
            tag = f"L{l}"
            kvg = groups((kv0 // GS * GS, WIN))
            attg = groups((att0, WIN))
            ffng = groups((ffn0, WIN))

            def wqkv_ld(g, l=l, tag=tag):
                for c in range(NCHUNK):
                    bd.wait_r(g, wA[c])
                    tk = bd.inc(f"wqk{tag}", g.dma_start(
                        out=wA[c][:, 0:1024],
                        in_=wqkv_d[l][128 * c:128 * (c + 1), 0:1024]), 16)
                    bd.wrote(wA[c], tk)
                    bd.wait_r(g, wB[c])
                    tk = bd.inc(f"wqk{tag}", g.dma_start(
                        out=wB[c][:, 0:512],
                        in_=wqkv_d[l][128 * c:128 * (c + 1), 1024:1536]), 16)
                    bd.wrote(wB[c], tk)
            bd.GP(wqkv_ld)
            tok_wqkv = (f"wqk{tag}", bd.counts[f"wqk{tag}"])

            for gi, (g0, gw) in enumerate(kvg):
                slot = gi % 2
                r0 = max(g0, kv0)
                a0, aw = r0 - g0, g0 + gw - r0
                qr0 = max(g0, att0)
                qa0, qaw = qr0 - g0, g0 + gw - qr0
                base = slot * 513 + 1

                if l == 0:
                    def p3_ld(sy, g0=g0, gw=gw, slot=slot, gi=gi):
                        if gi == 0:
                            bd.wait(sy, ("p2_sp", bd.counts.get("p2_sp", 0)))
                        for c in range(NCHUNK):
                            bd.wait_r(sy, xfr[c])
                        for c in range(NCHUNK):
                            tk = bd.inc(f"xld{slot}", sy.dma_start(
                                out=xfr[c][:, 0:gw],
                                in_=x2_sp[128 * c:128 * (c + 1), g0:g0 + gw]), 16)
                        for c in range(NCHUNK):
                            bd.wrote(xfr[c], tk)
                    bd.SY(p3_ld)
                    xf32 = [(xfr[c], xfr[c][:, 0:gw])
                            for c in range(NCHUNK)]
                    xnout = [(xgb[c], xgb[c][:, base:base + gw])
                             for c in range(NCHUNK)]
                    xf32out = [(xfr[c], xfr[c][:, 0:gw])
                               for c in range(NCHUNK)]
                    src_tok = emit_ln(gw, xf32, xnout, out_f32=xf32out)

                    def p3_sp(g, g0=g0, gw=gw, slot=slot, src_tok=src_tok):
                        bd.wait(g, src_tok)
                        for c in range(NCHUNK):
                            tk = bd.inc(f"xfsp{slot}", g.dma_start(
                                out=xffn_sp[128 * c:128 * (c + 1), g0:g0 + gw],
                                in_=xfr[c][:, 0:gw]), 16)
                        for c in range(NCHUNK):
                            xfr[c].rs.append(tk)
                    bd.GP(p3_sp)

                    def tails_c(v, slot=slot, gw=gw, src_tok=src_tok):
                        bd.wait(v, src_tok)
                        if bd.counts.get("tails", 0):
                            bd.wait(v, tok_tails)
                        nslot = 1 - slot
                        for c in range(NCHUNK):
                            bd.wait_r(v, xgb[c])
                            last = v.tensor_copy(
                                out=xgb[c][:, nslot * 513:nslot * 513 + 1],
                                in_=xgb[c][:, slot * 513 + gw:slot * 513 + gw + 1])
                        tk = bd.inc(f"tl{slot}", last)
                        for c in range(NCHUNK):
                            xgb[c].w = tk
                    bd.VE(tails_c)

                    if g0 <= HALO - 1 < g0 + gw:
                        def bz(v, slot=slot, g0=g0, src_tok=src_tok):
                            bd.wait(v, src_tok)
                            col = slot * 513 + 1 + (HALO - 1 - g0)
                            for c in range(NCHUNK):
                                last = v.tensor_scalar_mul(
                                    xgb[c][:, col:col + 1],
                                    xgb[c][:, col:col + 1], bcolb[:, 0:1])
                            tk = bd.inc(f"bz{slot}", last)
                            for c in range(NCHUNK):
                                xgb[c].w = tk
                        bd.VE(bz)
                    src_tok = (f"tl{slot}", bd.counts[f"tl{slot}"])
                else:
                    def p5_ld(sy, gi=gi, g0=g0, gw=gw, slot=slot):
                        if gi == 0:
                            bd.wait(sy, ("l1sp", bd.counts.get("l1sp", 0)))
                        lo1 = max(g0 - 1, R_ATT1[0])
                        ofs = slot * 513 + 1 - (g0 - lo1)
                        for c in range(NCHUNK):
                            bd.wait_r(sy, xgb[c])
                        for c in range(NCHUNK):
                            tk = bd.inc(f"gld{slot}", sy.dma_start(
                                out=xgb[c][:, ofs:slot * 513 + 1 + gw],
                                in_=l1ob_sp[128 * c:128 * (c + 1),
                                            lo1 - R_ATT1[0]:g0 + gw - R_ATT1[0]]), 16)
                        for c in range(NCHUNK):
                            bd.wrote(xgb[c], tk)
                    bd.SY(p5_ld)
                    if gi == 0:
                        def p5_ms(v, slot=slot):
                            bd.wait(v, (f"gld{slot}",
                                        bd.counts.get(f"gld{slot}", 0)))
                            for c in range(NCHUNK):
                                last = v.memset(
                                    xgb[c][:, slot * 513:slot * 513 + 129], 0.0)
                            tk = bd.inc(f"gld{slot}ms", last)
                            for c in range(NCHUNK):
                                xgb[c].w = tk
                        bd.VE(p5_ms)
                        src_tok = (f"gld{slot}ms", bd.counts[f"gld{slot}ms"])
                    else:
                        src_tok = (f"gld{slot}", bd.counts[f"gld{slot}"])

                def rhs_ap(c, a, b, slot=slot):
                    bb = slot * 513 + 1
                    if c < 4:
                        return xgb[c][:, bb + a:bb + b]
                    return xgb[c][:, bb + a - 1:bb + b - 1]

                def rot_ld(sy, g0=g0, gw=gw, slot=slot, l=l):
                    tabs = ["csq", "sq", "csk", "sk"]
                    for ti, nm in enumerate(tabs):
                        hb = wB[4 * slot + ti]
                        bd.wait_r(sy, hb)
                        tk = bd.inc(f"rt{slot}", sy.dma_start(
                            out=hb[:, 512:512 + gw],
                            in_=rot_d[nm][:, g0:g0 + gw]), 16)
                        hb.w = tk
                bd.SY(rot_ld)
                tok_rot = (f"rt{slot}", bd.counts[f"rt{slot}"])

                jobs = [("k", p) for p in range(4)]
                if qaw > 0:
                    jobs += [("q", p) for p in range(4)]
                nt = (aw + 127) // 128
                jobs += [("v", tt) for tt in range(nt)]
                for ji, (kind, p) in enumerate(jobs):
                    bank = pbank[[4, 5, 0, 1][ji % 4]]

                    def mm_f(t, kind=kind, p=p, bank=bank, gi=gi,
                             src_tok=src_tok, a0=a0, aw=aw, qa0=qa0, qaw=qaw,
                             rhs_ap=rhs_ap, ji=ji):
                        if ji == 0:
                            if gi == 0:
                                bd.wait(t, tok_wqkv)
                            bd.wait(t, src_tok)
                            if l == 0:
                                bd.wait(t, (f"bz{slot}",
                                            bd.counts.get(f"bz{slot}", 0)))
                        bd.wait_r(t, bank)
                        if kind == "k":
                            for c in range(NCHUNK):
                                i = t.matmul(bank[:, 0:aw],
                                             wA[c][:, 512 + 128 * p:640 + 128 * p],
                                             rhs_ap(c, a0, a0 + aw),
                                             start=(c == 0), stop=(c == NCHUNK - 1))
                        elif kind == "q":
                            for c in range(NCHUNK):
                                i = t.matmul(bank[:, 0:qaw],
                                             wA[c][:, 128 * p:128 * (p + 1)],
                                             rhs_ap(c, qa0, qa0 + qaw),
                                             start=(c == 0), stop=(c == NCHUNK - 1))
                        else:
                            tt = p
                            tw = min(128, aw - tt * 128)
                            for c in range(NCHUNK):
                                i = t.matmul(bank[0:tw, 0:512],
                                             rhs_ap(c, a0 + tt * 128,
                                                    a0 + tt * 128 + tw),
                                             wB[c][:, 0:512],
                                             start=(c == 0), stop=(c == NCHUNK - 1))
                        tk = bd.inc(f"pj{tag}", i)
                        bd.wrote(bank, tk)
                        for c in range(NCHUNK):
                            bd.did_read(xgb[c], tk)
                        if kind == "v":
                            for c in range(NCHUNK):
                                bd.did_read(wB[c], tk)
                        else:
                            for c in range(NCHUNK):
                                bd.did_read(wA[c], tk)
                    bd.TE(mm_f)

                    if kind == "v":
                        def vev_f(v, p=p, bank=bank, a0=a0, aw=aw, r0=r0,
                                  kv0=kv0):
                            tt = p
                            tw = min(128, aw - tt * 128)
                            vi = (r0 + tt * 128 - kv0) // 128
                            bd.read(v, bank)
                            bd.wait_r(v, vI[vi])
                            tk = bd.inc("vev", v.tensor_copy(
                                out=vI[vi][0:tw if tw < 128 else 128, :],
                                in_=bank[0:tw, 0:512]))
                            bd.wrote(vI[vi], tk)
                            bd.did_read(bank, tk)
                        bd.VE(vev_f)
                        continue

                    dst_is_q = (kind == "q")
                    w_ = qaw if dst_is_q else aw
                    col0 = (qr0 - att0) if dst_is_q else (r0 - kv0)
                    dest = qhat[p] if dst_is_q else khat[p]
                    tb_cs = wB[4 * slot + (0 if dst_is_q else 2)]
                    tb_s = wB[4 * slot + (1 if dst_is_q else 3)]
                    t_of = 512 + (qa0 if dst_is_q else a0)
                    tabs_cs = tb_cs[:, t_of:t_of + w_]
                    tabs_s = tb_s[:, t_of:t_of + w_]
                    scol = qscol[l] if dst_is_q else kscol[l]

                    def ev_f(sc, bank=bank, w_=w_, ji=ji):
                        bd.read(sc, bank)
                        rb = qn_s[ji % 2]
                        bd.wait_r(sc, rb)
                        tk = bd.inc("rawev", sc.copy(out=rb[:, 0:w_],
                                                     in_=bank[:, 0:w_]))
                        bd.wrote(rb, tk)
                        bd.did_read(bank, tk)
                    bd.SC(ev_f)

                    def sq_f(sc, w_=w_, ji=ji):
                        rb = qn_s[ji % 2]
                        sq = t2_s[ji % 2]
                        bd.read(sc, rb)
                        bd.wait_r(sc, sq)
                        tk = bd.inc("rsq", sc.activation(
                            out=sq[:, 0:w_], in_=rb[:, 0:w_], func=AF.Square))
                        bd.wrote(sq, tk)
                        rb.rs.append(tk)
                    bd.SC(sq_f)

                    def n2_f(t, w_=w_, ji=ji):
                        sq = t2_s[ji % 2]
                        bd.read(t, sq)
                        bd.wait_r(t, pbank[2])
                        tk = bd.inc("n2", t.matmul(
                            pbank[2][0:2, 0:w_], hsel2, sq[:, 0:w_],
                            start=True, stop=True))
                        bd.wrote(pbank[2], tk)
                        bd.did_read(sq, tk)
                    bd.TE(n2_f)

                    def rstd_f2(sc, w_=w_):
                        bd.read(sc, pbank[2])
                        bd.wait(sc, ("rbc2", bd.counts.get("rbc2", 0)))
                        tk = bd.inc("rs1", sc.activation(
                            out=rows[0:2, RO_LG:RO_LG + w_],
                            in_=pbank[2][0:2, 0:w_],
                            func=AF.Ln, bias=eps18[0:2, :]))
                        bd.wait(sc, tk)
                        tk2 = bd.inc("rs2", sc.activation(
                            out=rsb[0:2, 0:w_],
                            in_=rows[0:2, RO_LG:RO_LG + w_],
                            func=AF.Exp, scale=-0.5))
                        bd.did_read(pbank[2], tk2)
                    bd.SC(rstd_f2)

                    def rbc_f(t, w_=w_):
                        bd.wait(t, ("rs2", bd.counts["rs2"]))
                        bd.wait_r(t, pbank[3])
                        tk = bd.inc("rbc2", t.matmul(
                            pbank[3][:, 0:w_], psel2, rsb[0:2, 0:w_],
                            start=True, stop=True))
                        bd.wrote(pbank[3], tk)
                    bd.TE(rbc_f)

                    def rot_f(v, w_=w_, ji=ji, scol=scol, tabs_cs=tabs_cs,
                              tabs_s=tabs_s, dest=dest, col0=col0,
                              tok_rot=tok_rot, tb_cs=tb_cs, tb_s=tb_s):
                        rb = qn_s[ji % 2]
                        qn = t2_s[ji % 2]
                        t2 = t3_s[ji % 2]
                        bd.read(v, rb)
                        bd.read(v, pbank[3])
                        bd.wait_r(v, qn)
                        bd.wait(v, tok_rot)
                        v.scalar_tensor_tensor(
                            out=qn[:, 0:w_], in0=rb[:, 0:w_], scalar=scol[:, 0:1],
                            in1=pbank[3][:, 0:w_], op0=OP.mult, op1=OP.mult)
                        v.drain()
                        bd.wait_r(v, t2)
                        v.tensor_tensor(out=t2[0:64, 0:w_], in0=qn[64:128, 0:w_],
                                        in1=tabs_s[64:128, :], op=OP.mult)
                        v.tensor_tensor(out=t2[64:128, 0:w_], in0=qn[0:64, 0:w_],
                                        in1=tabs_s[0:64, :], op=OP.mult)
                        v.drain()
                        v.tensor_tensor(out=qn[:, 0:w_], in0=qn[:, 0:w_],
                                        in1=tabs_cs, op=OP.mult)
                        v.drain()
                        bd.wait_r(v, dest)
                        tk = bd.inc("rotd", v.tensor_tensor(
                            out=dest[:, col0:col0 + w_], in0=qn[:, 0:w_],
                            in1=t2[0:128, 0:w_], op=OP.add))
                        dest.w = tk
                        bd.wrote(qn, tk)
                        bd.wrote(t2, tk)
                        rb.rs.append(tk)
                        bd.did_read(pbank[3], tk)
                        bd.did_read(tb_cs, tk)
                        bd.did_read(tb_s, tk)
                    bd.VE(rot_f)

            if l == 0 and dev == "xffn":
                def dmp_xf(g):
                    bd.wait(g, ("xfsp0", bd.counts.get("xfsp0", 0)))
                    bd.wait(g, ("xfsp1", bd.counts.get("xfsp1", 0)))
                    for c in range(NCHUNK):
                        tk = bd.inc("dmpd", g.dma_start(
                            out=dbg_d[128 * c:128 * (c + 1), :],
                            in_=xffn_sp[128 * c:128 * (c + 1), :]), 16)
                    bd.wait(g, tk)
                bd.GP(dmp_xf)
                return nc

            def wo_ld(g, l=l, tag=tag):
                for c in range(4):
                    bd.wait_r(g, wB[c])
                    tk = bd.inc(f"wol{tag}", g.dma_start(
                        out=wB[c][:, 0:1024],
                        in_=wo2_d[l][128 * c:128 * (c + 1), 0:1024]), 16)
                    bd.wrote(wB[c], tk)
            bd.GP(wo_ld)
            tok_wo2 = (f"wol{tag}", bd.counts[f"wol{tag}"])

            for gi, (g0, gw) in enumerate(attg):
                slot = gi % 2
                def res_ld(sy, g0=g0, gw=gw, slot=slot, l=l, gi=gi):
                    if gi == 0:
                        if l == 0:
                            bd.wait(sy, ("xfsp0", bd.counts.get("xfsp0", 0)))
                            bd.wait(sy, ("xfsp1", bd.counts.get("xfsp1", 0)))
                        else:
                            bd.wait(sy, ("l1sp", bd.counts.get("l1sp", 0)))
                    for c in range(NCHUNK):
                        bd.wait_r(sy, xfr[c])
                    src = xffn_sp if l == 0 else l1o_sp
                    o = 0 if l == 0 else R_ATT1[0]
                    for c in range(NCHUNK):
                        tk = bd.inc(f"xld{slot}", sy.dma_start(
                            out=xfr[c][:, 0:gw],
                            in_=src[128 * c:128 * (c + 1), g0 - o:g0 - o + gw]), 16)
                    for c in range(NCHUNK):
                        bd.wrote(xfr[c], tk)
                bd.SY(res_ld)

                for qt in range(gw // 128):
                    w0 = g0 + 128 * qt
                    qti = (w0 - att0) // 128
                    qc = w0 - att0
                    mslot = maskt[qti % 2]

                    def i_mask(sy, qti=qti, mslot=mslot, L=L):
                        bd.wait_r(sy, mslot)
                        tk = bd.inc(f"p2_mld{qti % 2}", sy.dma_start(
                            out=mslot[:, 0:640], in_=L["mask_d"][qti, :, :]), 16)
                        bd.wrote(mslot, tk)
                    bd.SY(i_mask)

                    for h in range(HEADS_I):
                        pr = h // 2
                        par = h % 2
                        b0 = w0 - 512 - kv0
                        kb = kband[h % 2]
                        sb0, sb1 = (pbank[0], pbank[1]) if h % 2 == 0 else \
                                   (pbank[2], pbank[3])

                        def kb_f(v, pr=pr, par=par, b0=b0, kb=kb):
                            bd.read(v, khat[pr])
                            bd.wait_r(v, kb)
                            tk = bd.inc("kbs", v.tensor_scalar_mul(
                                kb[:, 0:640], khat[pr][:, b0:b0 + 640],
                                selperm[:, par:par + 1]))
                            bd.wrote(kb, tk)
                            khat[pr].rs.append(tk)
                        bd.VE(kb_f)

                        def s_f(t, pr=pr, qc=qc, kb=kb, sb0=sb0, sb1=sb1,
                                gi=gi, h=h):
                            if gi == 0 and qt == 0 and h == 0:
                                pass
                            bd.read(t, qhat[pr])
                            bd.read(t, kb)
                            bd.wait_r(t, sb0)
                            bd.wait_r(t, sb1)
                            t.matmul(sb0[:, 0:512], qhat[pr][:, qc:qc + 128],
                                     kb[:, 0:512], start=True, stop=True)
                            i = t.matmul(sb1[:, 0:128], qhat[pr][:, qc:qc + 128],
                                         kb[:, 512:640], start=True, stop=True)
                            tk = bd.inc("smm", i)
                            bd.wrote(sb0, tk)
                            bd.wrote(sb1, tk)
                            kb.rs.append(tk)
                            qhat[pr].rs.append(tk)
                        bd.TE(s_f)

                        def soev_f(v, mslot=mslot, h=h, sb0=sb0, sb1=sb1):
                            so = so_s[h % 2]
                            bd.read(v, sb0)
                            bd.read(v, sb1)
                            bd.read(v, mslot)
                            bd.wait_r(v, so)
                            v.scalar_tensor_tensor(
                                out=so[:, 0:512], in0=sb0[:, 0:512],
                                scalar=QK_SCALE, in1=mslot[:, 0:512],
                                op0=OP.mult, op1=OP.add)
                            tk = bd.inc("soe", v.scalar_tensor_tensor(
                                out=so[:, 512:640], in0=sb1[:, 0:128],
                                scalar=QK_SCALE, in1=mslot[:, 512:640],
                                op0=OP.mult, op1=OP.add))
                            bd.wrote(so, tk)
                            bd.did_read(sb0, tk)
                            bd.did_read(sb1, tk)
                            bd.did_read(mslot, tk)
                            v.drain()
                            tk = bd.inc("soe", v.tensor_reduce(
                                out=stat[:, h:h + 1], in_=so[:, 0:640],
                                axis=AX, op=OP.max, negate=True))
                            stat.w = tk
                        bd.VE(soev_f)

                        def exp_f(sc, h=h):
                            so = so_s[h % 2]
                            pn = pn_s[h % 2]
                            bd.read(sc, so)
                            bd.wait(sc, ("soe", bd.counts["soe"]))
                            bd.wait_r(sc, pn)
                            tk = bd.inc("expf", sc.activation(
                                out=pn[:, 0:640], in_=so[:, 0:640], func=AF.Exp,
                                bias=stat[:, h:h + 1],
                                accum_out=zcol[:, h:h + 1]))
                            bd.wrote(pn, tk)
                            zcol.w = tk
                            so.rs.append(tk)
                        bd.SC(exp_f)

                        def norm_f(v, h=h):
                            pn = pn_s[h % 2]
                            bd.read(v, pn)
                            bd.read(v, zcol)
                            v.reciprocal(out=stat[:, 32 + h:33 + h],
                                         in_=zcol[:, h:h + 1])
                            v.drain()
                            tk = bd.inc("pnn", v.tensor_scalar_mul(
                                pn[:, 0:640], pn[:, 0:640],
                                stat[:, 32 + h:33 + h]))
                            pn.w = tk
                        bd.VE(norm_f)

                        def tr_f(t, h=h):
                            pn = pn_s[h % 2]
                            bd.read(t, pn)
                            bd.wait_r(t, pbf[0])
                            for j in range(5):
                                i = t.transpose(
                                    pbf[0][:, 128 * j:128 * (j + 1)],
                                    pn[:, 128 * j:128 * (j + 1)], ident)
                            tk = bd.inc("trf", i)
                            bd.wrote(pbf[0], tk)
                            pn.rs.append(tk)
                        bd.TE(tr_f)

                        def pcm_f(sc):
                            bd.read(sc, pbf[0])
                            bd.wait_r(sc, pcm)
                            tk = bd.inc("pcmf", sc.copy(out=pcm[:, 0:640],
                                                        in_=pbf[0][:, 0:640]))
                            bd.wrote(pcm, tk)
                            bd.did_read(pbf[0], tk)
                        bd.SC(pcm_f)

                        def av_f(t, h=h, w0=w0, kv0=kv0):
                            bd.read(t, pcm)
                            bank = pbank[4 + h // 4]
                            if h % 4 == 0:
                                bd.wait_r(t, bank)
                            for j in range(5):
                                vi = (w0 - 512 + 128 * j - kv0) // 128
                                bd.read(t, vI[vi])
                                i = t.matmul(
                                    bank[0:64, 128 * (h % 4):128 * (h % 4) + 128],
                                    vI[vi][:, 64 * h:64 * (h + 1)],
                                    pcm[:, 128 * j:128 * (j + 1)],
                                    start=(j == 0), stop=(j == 4))
                            tk = bd.inc("avf", i)
                            bd.wrote(bank, tk) if h % 4 == 3 else bank.rs.append(tk)
                            bd.did_read(pcm, tk)
                            for j in range(5):
                                vi = (w0 - 512 + 128 * j - kv0) // 128
                                vI[vi].rs.append(tk)
                        bd.TE(av_f)

                        if h % 4 == 3:
                            def avev_f(v, h=h, qt=qt):
                                bank = pbank[4 + h // 4]
                                bd.read(v, bank)
                                for hh in range(h - 3, h + 1):
                                    prr = hh // 2
                                    ro = 64 * (hh % 2)
                                    bd.wait_r(v, oI[prr])
                                    tk = bd.inc("avev", v.tensor_copy(
                                        out=oI[prr][ro:ro + 64,
                                                    128 * qt:128 * qt + 128],
                                        in_=bank[0:64,
                                                 128 * (hh % 4):128 * (hh % 4) + 128]))
                                    oI[prr].w = tk
                                bd.did_read(bank, tk)
                            bd.VE(avev_f)

                for dm in range(8):
                    def wo_f(t, dm=dm, gi=gi, gw=gw, tok_wo2=tok_wo2):
                        if dm == 0:
                            if gi == 0:
                                bd.wait(t, tok_wo2)
                            for pr in range(4):
                                bd.read(t, oI[pr])
                        bank = pbank[4 + dm % 2]
                        bd.wait_r(t, bank)
                        for c in range(4):
                            i = t.matmul(bank[:, 0:gw],
                                         wB[c][:, 128 * dm:128 * (dm + 1)],
                                         oI[c][:, 0:gw],
                                         start=(c == 0), stop=(c == 3))
                        tk = bd.inc("womm2", i)
                        bd.wrote(bank, tk)
                        if dm == 7:
                            for pr in range(4):
                                oI[pr].rs.append(tk)
                    bd.TE(wo_f)

                    def mid_f(v, dm=dm, gw=gw, slot=slot):
                        bank = pbank[4 + dm % 2]
                        bd.read(v, bank)
                        bd.read(v, xfr[dm])
                        bd.wait_r(v, fsp[dm])
                        tk = bd.inc("midf", v.scalar_tensor_tensor(
                            out=fsp[dm][:, 1:1 + gw], in0=bank[:, 0:gw],
                            scalar=1.0,
                            in1=xfr[dm][:, 0:gw],
                            op0=OP.mult, op1=OP.add))
                        fsp[dm].w = tk
                        bd.did_read(bank, tk)
                        bd.did_read(xfr[dm], tk)
                    bd.VE(mid_f)

                if g0 <= HALO - 1 < g0 + gw:
                    def bzm(v, g0=g0):
                        col = 1 + (HALO - 1 - g0)
                        for c in range(NCHUNK):
                            bd.read(v, fsp[c])
                            last = v.tensor_scalar_mul(
                                fsp[c][:, col:col + 1], fsp[c][:, col:col + 1],
                                bcolb[:, 0:1])
                        tk = bd.inc("bzm", last)
                        for c in range(NCHUNK):
                            fsp[c].w = tk
                    bd.VE(bzm)

                def mid_sp_f(g, g0=g0, gw=gw, att0=att0):
                    for c in range(NCHUNK):
                        bd.read(g, fsp[c])
                    for c in range(NCHUNK):
                        tk = bd.inc("midsp", g.dma_start(
                            out=mid_sp[128 * c:128 * (c + 1),
                                       g0 - att0:g0 - att0 + gw],
                            in_=fsp[c][:, 1:1 + gw]), 16)
                    for c in range(NCHUNK):
                        fsp[c].rs.append(tk)
                bd.GP(mid_sp_f)

                xf32 = [(fsp[c], fsp[c][:, 1:1 + gw] if c < 4
                         else fsp[c][:, 0:gw]) for c in range(NCHUNK)]
                xnout = [(xgb[c], xgb[c][:, slot * 513 + 1:slot * 513 + 1 + gw])
                         for c in range(NCHUNK)]
                mid_tok = emit_ln(gw, xf32, xnout, affine_l=l)

                def xn_sp_f(g, g0=g0, gw=gw, att0=att0, slot=slot,
                            mid_tok=mid_tok):
                    bd.wait(g, mid_tok)
                    for c in range(NCHUNK):
                        tk = bd.inc("xnsp", g.dma_start(
                            out=xnmid_sp[128 * c:128 * (c + 1),
                                         g0 - att0:g0 - att0 + gw],
                            in_=xgb[c][:, slot * 513 + 1:slot * 513 + 1 + gw]), 16)
                    for c in range(NCHUNK):
                        xgb[c].rs.append(tk)
                bd.GP(xn_sp_f)

                def tail_f(v, gw=gw, mid_tok=mid_tok):
                    bd.wait(v, mid_tok)
                    for c in range(NCHUNK):
                        last = v.tensor_copy(out=fsp[c][:, 0:1],
                                             in_=fsp[c][:, gw:gw + 1])
                    tk = bd.inc("ftl", last)
                    for c in range(NCHUNK):
                        fsp[c].w = tk
                bd.VE(tail_f)

            def w2home(j):
                if j < 8:
                    return qhat[j // 2][:, 1024 * (j % 2):1024 * (j % 2) + 1024], \
                        qhat[j // 2]
                if j < 20:
                    jj = j - 8
                    return khat[jj // 3][:, 1024 * (jj % 3):1024 * (jj % 3) + 1024], \
                        khat[jj // 3]
                jj = j - 20
                ap = so_s[jj].ap.bitcast(BF16)
                return ap[:, 0:1024], so_s[jj]

            def w2_ld(sy, l=l, tag=tag):
                bd.wait(sy, tok_wpc)
                homes = []
                for j in range(N_JT):
                    ap, home = w2home(j)
                    if home not in homes:
                        bd.wait_r(sy, home)
                        homes.append(home)
                for j in range(N_JT):
                    wdt = W1_TILES[j]
                    ap, home = w2home(j)
                    tk = bd.inc(f"w2l{tag}", sy.dma_start(
                        out=ap[0:wdt, :],
                        in_=w2b_sp[l][128 * j:128 * j + wdt, :]), 16)
                for home in homes:
                    bd.wrote(home, tk)
            bd.SY(w2_ld)
            tok_w2 = (f"w2l{tag}", bd.counts[f"w2l{tag}"])

            for gi, (g0, gw) in enumerate(ffng):
                slot = gi % 2

                def ffn_ld(sy, g0=g0, gw=gw, slot=slot, ffn0=ffn0, att0=att0,
                           l=l, gi=gi):
                    if gi == 0:
                        bd.wait(sy, ("xnsp", bd.counts.get("xnsp", 0)))
                        bd.wait(sy, ("midsp", bd.counts.get("midsp", 0)))
                    for c in range(NCHUNK):
                        bd.wait_r(sy, xgb[c])
                    for c in range(NCHUNK):
                        tk = bd.inc(f"gld{slot}", sy.dma_start(
                            out=xgb[c][:, slot * 513:slot * 513 + gw],
                            in_=xnmid_sp[128 * c:128 * (c + 1),
                                         g0 - att0:g0 - att0 + gw]), 16)
                    for c in range(NCHUNK):
                        bd.wrote(xgb[c], tk)
                    for c in range(NCHUNK):
                        bd.wait_r(sy, xfr[c])
                    for c in range(NCHUNK):
                        tk = bd.inc(f"xld0", sy.dma_start(
                            out=xfr[c][:, 0:gw],
                            in_=mid_sp[128 * c:128 * (c + 1),
                                       g0 - att0:g0 - att0 + gw]), 16)
                    for c in range(NCHUNK):
                        bd.wrote(xfr[c], tk)
                bd.SY(ffn_ld)
                tok_ffnld = (f"gld{slot}", bd.counts[f"gld{slot}"])
                tok_ffnx = ("xld0", bd.counts["xld0"])

                for j in range(N_JT):
                    wdt = W1_TILES[j]
                    sa = wA[2 * (j % 2)]
                    sg = wA[2 * (j % 2) + 1]

                    def w1_ld2(sy, j=j, wdt=wdt, sa=sa, sg=sg, l=l, gi=gi):
                        if j == 0 and gi == 0:
                            bd.wait(sy, tok_wpc)
                        off = sum(2 * W1_TILES[jj] for jj in range(j))
                        srcr = w1b_sp[l].rearrange("(c p) f -> p c f", p=128)
                        for ag, ws in ((0, sa), (1, sg)):
                            bd.wait_r(sy, ws)
                            tk = bd.inc(f"w1l{j % 2}{ag}", sy.dma_start(
                                out=ws[:, 0:8 * wdt].rearrange(
                                    "p (c f) -> p c f", c=NCHUNK),
                                in_=srcr[:, :, off + wdt * ag:
                                         off + wdt * (ag + 1)]), 16)
                            bd.wrote(ws, tk)
                    bd.SY(w1_ld2)

                    def w1_mm(t, j=j, wdt=wdt, sa=sa, sg=sg, gw=gw, slot=slot,
                              gi=gi, tok_ffnld=tok_ffnld, tok_w2=tok_w2):
                        if j == 0 and gi == 0:
                            bd.wait(t, tok_w2)
                        if j == 0:
                            bd.wait(t, tok_ffnld)
                        for ag, ws in ((0, sa), (1, sg)):
                            bd.read(t, ws)
                            bank = pbank[(4 + ag) if j % 2 == 0 else (2 + ag)]
                            bd.wait_r(t, bank)
                            for c in range(NCHUNK):
                                i = t.matmul(
                                    bank[0:wdt, 0:gw],
                                    ws[:, wdt * c:wdt * (c + 1)],
                                    xgb[c][:, slot * 513:slot * 513 + gw],
                                    start=(c == 0), stop=(c == NCHUNK - 1))
                            tk = bd.inc("w1mm", i)
                            bd.wrote(bank, tk)
                            ws.rs.append(tk)
                        if j == N_JT - 1:
                            for c in range(NCHUNK):
                                xgb[c].rs.append(tk)
                    bd.TE(w1_mm)

                    def gelu_f(sc, j=j, wdt=wdt, gw=gw):
                        bank = pbank[5 if j % 2 == 0 else 3]
                        bd.read(sc, bank)
                        gl = xsq[j % 2]
                        bd.wait_r(sc, gl)
                        tk = bd.inc("gel", sc.activation(
                            out=gl[0:wdt, 0:gw], in_=bank[0:wdt, 0:gw],
                            func=AF.Gelu, bias=b1col[l][0:wdt, 2 * j + 1:2 * j + 2]))
                        bd.wrote(gl, tk)
                        bd.did_read(bank, tk)
                    bd.SC(gelu_f)

                    def m_f(v, j=j, wdt=wdt, gw=gw, l=l):
                        bank = pbank[4 if j % 2 == 0 else 2]
                        gl = xsq[j % 2]
                        bd.read(v, bank)
                        bd.read(v, gl)
                        bd.wait_r(v, vI[j])
                        tk = bd.inc("mf", v.scalar_tensor_tensor(
                            out=vI[j][0:wdt, 0:gw], in0=bank[0:wdt, 0:gw],
                            scalar=b1col[l][0:wdt, 2 * j:2 * j + 1],
                            in1=gl[0:wdt, 0:gw], op0=OP.add, op1=OP.mult))
                        bd.wrote(vI[j], tk)
                        bd.did_read(bank, tk)
                        gl.rs.append(tk)
                    bd.VE(m_f)

                accs = [pbank[i] for i in range(6)] + [pbfF[0], pbfF[1]]

                def w2_mm(t, gw=gw, l=l):
                    for i in range(2):
                        bd.read(t, pbf[i])
                        bd.wait_r(t, pbf[i])
                    for dm in range(8):
                        bd.wait_r(t, accs[dm])
                    for j in range(N_JT):
                        wdt = W1_TILES[j]
                        ap, home = w2home(j)
                        bd.read(t, home)
                        bd.read(t, vI[j])
                        for dm in range(8):
                            i = t.matmul(
                                accs[dm][:, 0:gw] if dm < 6 else accs[dm][:, 0:gw],
                                ap[0:wdt, 128 * dm:128 * (dm + 1)],
                                vI[j][0:wdt, 0:gw],
                                start=(j == 0), stop=(j == N_JT - 1))
                        tk = bd.inc("w2mm", i)
                        vI[j].rs.append(tk)
                        home.rs.append(tk)
                    for dm in range(8):
                        bd.wrote(accs[dm], ("w2mm", bd.counts["w2mm"]))
                bd.TE(w2_mm)

                for dm in range(8):
                    def out_f(v, dm=dm, g0=g0, gw=gw, l=l,
                              tok_ffnx=tok_ffnx):
                        acc = accs[dm]
                        bd.read(v, acc)
                        bd.wait(v, tok_ffnx)
                        bd.wait_r(v, fsp[dm])
                        tk = bd.inc("outf", v.scalar_tensor_tensor(
                            out=fsp[dm][:, 1:1 + gw], in0=acc[:, 0:gw],
                            scalar=b2c[l][:, dm:dm + 1],
                            in1=xfr[dm][:, 0:gw], op0=OP.add, op1=OP.add))
                        fsp[dm].w = tk
                        bd.did_read(acc, tk)
                        if dm >= 6:
                            bd.did_read(pbf[dm - 6], tk)
                        bd.did_read(xfr[dm], tk)
                    bd.VE(out_f)

                if l == 0 and g0 <= HALO - 1 < g0 + gw:
                    def bzo(v, g0=g0):
                        col = 1 + (HALO - 1 - g0)
                        for c in range(NCHUNK):
                            bd.read(v, fsp[c])
                            last = v.tensor_scalar_mul(
                                fsp[c][:, col:col + 1], fsp[c][:, col:col + 1],
                                bcolb[:, 0:1])
                        tk = bd.inc("bzo", last)
                        for c in range(NCHUNK):
                            fsp[c].w = tk
                    bd.VE(bzo)

                def out_sp(g, g0=g0, gw=gw, l=l, ffn0=ffn0):
                    for c in range(NCHUNK):
                        bd.read(g, fsp[c])
                    for c in range(NCHUNK):
                        if l == 0:
                            tk = bd.inc("l1sp", g.dma_start(
                                out=l1o_sp[128 * c:128 * (c + 1),
                                           g0 - ffn0:g0 - ffn0 + gw],
                                in_=fsp[c][:, 1:1 + gw]), 16)
                            tk = bd.inc("l1sp", g.dma_start(
                                out=l1ob_sp[128 * c:128 * (c + 1),
                                            g0 - ffn0:g0 - ffn0 + gw],
                                in_=fsp[c][:, 1:1 + gw]), 16)
                        else:
                            tk = bd.inc("osp", g.dma_start(
                                out=out_d[128 * c:128 * (c + 1),
                                          g0 - R_FFN2[0]:g0 - R_FFN2[0] + gw],
                                in_=fsp[c][:, 1:1 + gw]), 16)
                            bd.wait(g, tk)
                            tk = bd.inc("osp", g.dma_start(
                                out=out_d[128 * c:128 * (c + 1),
                                          g0 - R_FFN2[0]:g0 - R_FFN2[0] + gw],
                                in_=x2_sp[128 * c:128 * (c + 1), g0:g0 + gw],
                                accum_op=OP.add), 16)
                    for c in range(NCHUNK):
                        fsp[c].rs.append(tk)
                bd.GP(out_sp)

            if l == 0 and dev == "l1":
                def dmp_l1(g):
                    bd.wait(g, ("l1sp", bd.counts["l1sp"]))
                    for c in range(NCHUNK):
                        tk = bd.inc("dmpd", g.dma_start(
                            out=dbg_d[128 * c:128 * (c + 1), :],
                            in_=l1o_sp[128 * c:128 * (c + 1), :]), 16)
                    bd.wait(g, tk)
                bd.GP(dmp_l1)
                return nc

        def fin_f(g):
            bd.wait(g, ("osp", bd.counts["osp"]))
        bd.GP(fin_f)
    return nc




def kernel(**inputs):
    nc = build_program(dev=None)
    in_maps = [_prep_core(inputs, c // 2, c % 2) for c in range(8)]
    res = run_bass_kernel_spmd(nc, in_maps, core_ids=list(range(8)))
    out = np.zeros((B, T, DIM), np.float32)
    for c in range(8):
        b, s = c // 2, c % 2
        out[b, s * OWN:(s + 1) * OWN, :] = res.results[c]["out"].T
    return out
